# version 16
# baseline (speedup 1.0000x reference)
import contextlib
import numpy as np
import concourse.bass as bass
import concourse.mybir as mybir
from concourse.bass_utils import run_bass_kernel_spmd

F32 = mybir.dt.float32
BF16 = mybir.dt.bfloat16
I32 = mybir.dt.int32
ALU = mybir.AluOpType
AF = mybir.ActivationFunctionType
AX = mybir.AxisListType

B = 16
S = 2048
H = 2048
HN = 128
HC = 16
INTER = 5504
OCI = 43
NBLK = 64
SCALE = float(1.0 / np.sqrt(HN))
EPS = 1e-5
INVH = 1.0 / 2048.0
NJL = 17
SLOT_G = 0
SLOT_U = OCI
SLOT_D = 2 * OCI
DW = (2048, 2048, INTER - 4096)


def build_nc(debug_dump=False):
    nc = bass.Bass("TRN2", target_bir_lowering=False)

    xT_in = nc.dram_tensor("xT_in", [H, B, S], F32, kind="ExternalInput").ap()
    x_res = nc.dram_tensor("x_res", [2, S, H], F32, kind="ExternalInput").ap()
    wqkvT = nc.dram_tensor("wqkvT", [H, 768], F32, kind="ExternalInput").ap()
    wprojT = nc.dram_tensor("wprojT", [H, H], F32, kind="ExternalInput").ap()
    bias2dT = nc.dram_tensor("bias2dT", [128, 16], F32, kind="ExternalInput").ap()
    lut2d = nc.dram_tensor("lut2d", [1, 16], F32, kind="ExternalInput").ap()
    idx_sh = nc.dram_tensor("idx_sh", [NJL, 128, H], I32, kind="ExternalInput").ap()
    ident_in = nc.dram_tensor("ident_in", [128, 128], F32, kind="ExternalInput").ap()
    onesc_in = nc.dram_tensor("onesc_in", [128, 1], F32, kind="ExternalInput").ap()
    onesr_in = nc.dram_tensor("onesr_in", [1, 128], F32, kind="ExternalInput").ap()
    y = nc.dram_tensor("y", [2, S, H], F32, kind="ExternalOutput").ap()

    KD = "ExternalOutput" if debug_dump else "Internal"
    qk_dram = nc.dram_tensor("qk_dram", [4, B, 128, S], BF16, kind=KD).ap()
    v_dram = nc.dram_tensor("v_dram", [2, B, S, 128], BF16, kind=KD).ap()
    ctx_dram = nc.dram_tensor("ctx_dram", [2, B, 128, S], BF16, kind=KD).ap()
    x1T_dram = nc.dram_tensor("x1T_dram", [HC, 128, 4096], BF16, kind=KD).ap()
    x1R_dram = nc.dram_tensor("x1R_dram", [4096, H], F32, kind=KD).ap()
    h_dram = nc.dram_tensor("h_dram", [INTER, 4096], BF16, kind=KD).ap()
    ag_in = nc.dram_tensor("ag_in", [NJL, 128, H], BF16, kind="Internal")
    ag_out = nc.dram_tensor("ag_out", [8 * NJL, 128, H], BF16, kind="Internal",
                            addr_space="Shared")
    ago = ag_out.ap()

    with contextlib.ExitStack() as st0:
        E = st0.enter_context

        identB = E(nc.sbuf_tensor("identB", [128, 128], BF16))
        identF = E(nc.sbuf_tensor("identF", [128, 128], F32))
        onesc = E(nc.sbuf_tensor("onesc", [128, 1], BF16))
        onesr = E(nc.sbuf_tensor("onesr", [1, 128], F32))
        lut_sb = E(nc.sbuf_tensor("lut_sb", [1, 16], F32))
        lutB = E(nc.sbuf_tensor("lutB", [128, 16], F32))
        bias_sb = E(nc.sbuf_tensor("bias_sb", [128, 16], F32))

        names = ["ph", "c0", "c1", "xld", "wql", "qm", "qe", "qs0", "qs1", "vm",
                 "ve", "vs0", "vs1", "gl", "gc", "gd", "gs", "cc",
                 "pa", "pb", "rsm", "gg", "uu",
                 "qk0", "qk1", "ss", "exp", "pv", "rc", "bcm", "bce", "ctx", "cs0", "cs1",
                 "ctxl", "prm", "prc", "pre", "t1", "xrl",
                 "l1", "l2", "l3", "l4", "l5", "x1b", "x1fr", "xf0", "xf1",
                 "xt1", "xte", "xts",
                 "wl0", "wl1", "gum", "sil", "hh", "hs0", "hs1",
                 "h5", "wd0", "wd1", "dnm", "dne", "trm5", "t15",
                 "m1", "m2", "m3", "m4", "m5", "yrd", "ou0", "xr0", "misc"]
        sem = {n: E(nc.semaphore("s_" + n)) for n in names}
        block = E(nc.Block())

        @block.gpsimd
        def _(g):
            g.dma_start(out=identB[:], in_=ident_in).then_inc(sem["c0"], 16)
            g.dma_start(out=onesc[:], in_=onesc_in).then_inc(sem["c0"], 16)
            g.wait_ge(sem["c0"], 32)
            g.nop().then_inc(sem["ph"], 1)

        @block.sync
        def _(sy):
            sy.dma_start(out=identF[:], in_=ident_in).then_inc(sem["c1"], 16)
            sy.dma_start(out=onesr[:], in_=onesr_in).then_inc(sem["c1"], 16)
            sy.dma_start(out=lut_sb[:], in_=lut2d).then_inc(sem["c1"], 16)
            sy.dma_start(out=bias_sb[:], in_=bias2dT).then_inc(sem["c1"], 16)
            sy.wait_ge(sem["c1"], 64)
            sy.nop().then_inc(sem["ph"], 1)

        with nc.psum_tensor("ps_lut", [128, 16], F32) as ps_lut:
            @block.tensor
            def _(t):
                t.wait_ge(sem["c1"], 64)
                t.matmul(ps_lut[:], onesr[:], lut_sb[:], start=True, stop=True
                         ).then_inc(sem["misc"], 1)
                t.nop().then_inc(sem["ph"], 1)

            @block.scalar
            def _(s):
                s.wait_ge(sem["misc"], 1)
                s.activation(lutB[:], ps_lut[:], AF.Copy).then_inc(sem["ph"], 1)

        @block.vector
        def _(v):
            v.nop().then_inc(sem["ph"], 1)

        st12 = contextlib.ExitStack()
        E12 = st12.enter_context
        QT = [E12(nc.sbuf_tensor(f"QT{i}", [128, S], BF16)) for i in range(2)]
        KT = [E12(nc.sbuf_tensor(f"KT{i}", [128, S], BF16)) for i in range(2)]
        V16 = [E12(nc.sbuf_tensor(f"V16{i}", [128, S], BF16)) for i in range(2)]
        PT = [E12(nc.sbuf_tensor(f"PT{i}", [128, 512], BF16)) for i in range(20)]
        recip = [E12(nc.sbuf_tensor(f"recip{i}", [1, 512], F32)) for i in range(2)]
        bc_sb = [E12(nc.sbuf_tensor(f"bc{i}", [128, 512], F32)) for i in range(2)]
        ctxo = [E12(nc.sbuf_tensor(f"ctxo{i}", [128, 512], BF16)) for i in range(2)]

        with contextlib.ExitStack() as st1:
            E1 = st1.enter_context
            wqkvT_bf = E1(nc.sbuf_tensor("wqkvT_bf", [128, HC * 768], BF16))
            XT = [E1(nc.sbuf_tensor(f"XT{i}", [128, HC * 512], BF16)) for i in range(3)]
            qk_buf = [E1(nc.sbuf_tensor(f"qkb{i}", [128, 512], BF16)) for i in range(2)]
            v_buf = [E1(nc.sbuf_tensor(f"vb{i}", [128, 256], BF16)) for i in range(2)]
            idx_i = [E1(nc.sbuf_tensor(f"idx_i{i}", [128, H], I32)) for i in range(2)]
            idxb = [E1(nc.sbuf_tensor(f"idxb{i}", [128, H], BF16)) for i in range(2)]
            wacc = [E1(nc.sbuf_tensor(f"wacc{i}", [128, H], BF16)) for i in range(2)]
            wtmp = E1(nc.sbuf_tensor("wtmp", [128, H], BF16))
            ps_qk = [E1(nc.psum_tensor(f"ps_qk{i}", [128, 512], F32)) for i in range(2)]
            ps_v = [E1(nc.psum_tensor(f"ps_v{i}", [128, 256], F32)) for i in range(2)]

            @block.sync
            def _(sy):
                sy.wait_ge(sem["ph"], 5)
                for j in range(NJL):
                    if j >= 2:
                        sy.wait_ge(sem["gc"], j - 1)
                    sy.dma_start(out=idx_i[j % 2][:], in_=idx_sh[j]
                                 ).then_inc(sem["gl"], 16)
                sy.nop().then_inc(sem["ph"], 1)

            @block.gpsimd
            def _(g):
                g.wait_ge(sem["ph"], 5)
                wq3 = wqkvT.rearrange("(hc p) o -> p hc o", p=128)
                wb3 = wqkvT_bf[:].rearrange("p (hc o) -> p hc o", o=768)
                for q in range(4):
                    g.dma_start(out=wb3[:, q * 4:(q + 1) * 4, :],
                                in_=wq3[:, q * 4:(q + 1) * 4, :]).then_inc(sem["wql"], 16)
                for blk in range(NBLK):
                    b, sb = blk // 4, blk % 4
                    if blk >= 3:
                        g.wait_ge(sem["qm"], 4 * (blk - 2))
                        g.wait_ge(sem["vm"], 4 * (blk - 2))
                    g.dma_start(
                        out=XT[blk % 3][:].rearrange("p (hc s) -> p hc s", s=512),
                        in_=xT_in[:, b, sb * 512:(sb + 1) * 512].rearrange(
                            "(hc p) s -> p hc s", p=128),
                    ).then_inc(sem["xld"], 16)
                    if blk % 2 == 0 and blk // 2 < NJL:
                        j = blk // 2
                        g.wait_ge(sem["gl"], 16 * (j + 1))
                        if j >= 2:
                            g.wait_ge(sem["gd"], j - 1)
                        g.tensor_copy(idxb[j % 2][:], idx_i[j % 2][:]
                                      ).then_inc(sem["gc"], 1)
                        if j >= 1:
                            g.wait_ge(sem["gd"], j)
                            g.dma_start(out=ag_in.ap()[j - 1], in_=wacc[(j - 1) % 2][:]
                                        ).then_inc(sem["gs"], 16)
                g.wait_ge(sem["gd"], NJL)
                g.dma_start(out=ag_in.ap()[NJL - 1], in_=wacc[(NJL - 1) % 2][:]
                            ).then_inc(sem["gs"], 16)
                g.wait_ge(sem["gs"], 16 * NJL)
                g.collective_compute(
                    "AllGather", ALU.bypass, replica_groups=[list(range(8))],
                    ins=[ag_in.ap().opt()], outs=[ago.opt()],
                ).then_inc(sem["cc"])
                g.nop().then_inc(sem["ph"], 1)

            @block.scalar
            def _(s):
                s.wait_ge(sem["ph"], 5)
                for blk in range(NBLK):
                    b, sb = blk // 4, blk % 4
                    srange = slice(sb * 512, (sb + 1) * 512)
                    for oc in range(4):
                        k = blk * 4 + oc
                        s.wait_ge(sem["qm"], k + 1)
                        if k >= 2:
                            s.wait_ge(sem["qs%d" % (k % 2)], 16 * (k // 2))
                        s.activation(qk_buf[oc % 2][:], ps_qk[oc % 2][:], AF.Copy
                                     ).then_inc(sem["qe"], 1)
                        s.wait_ge(sem["qe"], k + 1)
                        s.dma_start(out=qk_dram[oc, b, :, srange], in_=qk_buf[oc % 2][:]
                                    ).then_inc(sem["qs%d" % (oc % 2)], 16)
                    for ss in range(4):
                        k = blk * 4 + ss
                        s.wait_ge(sem["vm"], k + 1)
                        if k >= 2:
                            s.wait_ge(sem["vs%d" % (k % 2)], 32 * (k // 2))
                        s.activation(v_buf[ss % 2][:], ps_v[ss % 2][:], AF.Copy
                                     ).then_inc(sem["ve"], 1)
                        s.wait_ge(sem["ve"], k + 1)
                        sr2 = slice(sb * 512 + ss * 128, sb * 512 + (ss + 1) * 128)
                        s.dma_start(out=v_dram[0, b, sr2, :], in_=v_buf[ss % 2][:, 0:128]
                                    ).then_inc(sem["vs%d" % (ss % 2)], 16)
                        s.dma_start(out=v_dram[1, b, sr2, :], in_=v_buf[ss % 2][:, 128:256]
                                    ).then_inc(sem["vs%d" % (ss % 2)], 16)
                s.nop().then_inc(sem["ph"], 1)

            @block.tensor
            def _(t):
                t.wait_ge(sem["ph"], 5)
                t.wait_ge(sem["wql"], 64)
                for blk in range(NBLK):
                    t.wait_ge(sem["xld"], 16 * (blk + 1))
                    for oc in range(4):
                        k = blk * 4 + oc
                        if k >= 2:
                            t.wait_ge(sem["qe"], k - 1)
                        for hc in range(HC):
                            ins = t.matmul(ps_qk[oc % 2][:],
                                           wqkvT_bf[:, hc * 768 + oc * 128: hc * 768 + (oc + 1) * 128],
                                           XT[blk % 3][:, hc * 512:(hc + 1) * 512],
                                           start=(hc == 0), stop=(hc == 15))
                        ins.then_inc(sem["qm"], 1)
                    for ss in range(4):
                        k = blk * 4 + ss
                        if k >= 2:
                            t.wait_ge(sem["ve"], k - 1)
                        for hc in range(HC):
                            ins = t.matmul(ps_v[ss % 2][:],
                                           XT[blk % 3][:, hc * 512 + ss * 128: hc * 512 + (ss + 1) * 128],
                                           wqkvT_bf[:, hc * 768 + 512: hc * 768 + 768],
                                           start=(hc == 0), stop=(hc == 15))
                        ins.then_inc(sem["vm"], 1)
                t.nop().then_inc(sem["ph"], 1)

            @block.vector
            def _(v):
                v.wait_ge(sem["ph"], 5)
                for j in range(NJL):
                    w = H
                    v.wait_ge(sem["gc"], j + 1)
                    if j >= 2:
                        v.wait_ge(sem["gs"], 16 * (j - 1))
                    v.tensor_scalar(wacc[j % 2][:, :w], idxb[j % 2][:, :w], 0.0,
                                    lutB[:, 0:1], ALU.is_equal, ALU.mult)
                    for tt in range(1, 16):
                        v.tensor_scalar(wtmp[:, :w], idxb[j % 2][:, :w], float(tt),
                                        lutB[:, tt:tt + 1], ALU.is_equal, ALU.mult)
                        ins = v.tensor_tensor(wacc[j % 2][:, :w], wacc[j % 2][:, :w],
                                              wtmp[:, :w], ALU.add)
                    ins.then_inc(sem["gd"], 1)
                v.nop().then_inc(sem["ph"], 1)

        with contextlib.ExitStack() as st2:
            E2 = st2.enter_context
            ps_s = [E2(nc.psum_tensor(f"ps_s{i}", [128, 512], F32)) for i in range(4)]
            ps_c = [E2(nc.psum_tensor(f"ps_c{i}", [128, 512], F32)) for i in range(2)]
            ps_sum = [E2(nc.psum_tensor(f"ps_sm{i}", [1, 512], F32)) for i in range(1)]
            ps_bc = [E2(nc.psum_tensor(f"ps_bc{i}", [128, 512], F32)) for i in range(1)]

            @block.sync
            def _(sy):
                for g in range(32):
                    i, b = g // 16, g % 16
                    if g >= 2:
                        sy.wait_ge(sem["pv"], 64 * (g - 1))
                    w0 = 32 * (4 * b + 4)
                    sy.wait_ge(sem["qs0"], w0)
                    sy.wait_ge(sem["qs1"], w0)
                    sy.wait_ge(sem["vs0"], 2 * w0)
                    sy.wait_ge(sem["vs1"], 2 * w0)
                    sy.dma_start(out=QT[g % 2][:], in_=qk_dram[2 * i, b]
                                 ).then_inc(sem["qk%d" % (g % 2)], 16)
                    sy.dma_start(out=KT[g % 2][:], in_=qk_dram[2 * i + 1, b]
                                 ).then_inc(sem["qk%d" % (g % 2)], 16)
                    v3 = v_dram[i, b].rearrange("(t p) d -> p t d", p=128)
                    o3 = V16[g % 2][:].rearrange("p (t d) -> p t d", d=128)
                    for q in range(4):
                        sy.dma_start(out=o3[:, q * 4:(q + 1) * 4, :],
                                     in_=v3[:, q * 4:(q + 1) * 4, :]
                                     ).then_inc(sem["qk%d" % (g % 2)], 16)
                sy.nop().then_inc(sem["ph"], 1)

            @block.gpsimd
            def _(g):
                g.wait_ge(sem["ph"], 10)

                def spill(k2):
                    i, b, qc = k2 // 64, (k2 // 4) % 16, k2 % 4
                    g.wait_ge(sem["ctx"], k2 + 1)
                    g.dma_start(out=ctx_dram[i, b, :, qc * 512:(qc + 1) * 512],
                                in_=ctxo[qc % 2][:]).then_inc(sem["cs%d" % (k2 % 2)], 16)

                for k in range(128):
                    spill(k)
                g.wait_ge(sem["cs0"], 16 * 64)
                g.wait_ge(sem["cs1"], 16 * 64)
                g.nop().then_inc(sem["ph"], 1)

            @block.tensor
            def _(t):
                t.wait_ge(sem["ph"], 10)

                def rowsum(k2):
                    t.wait_ge(sem["exp"], 16 * (k2 + 1))
                    t.wait_ge(sem["rc"], k2)
                    e0b = k2 * 16
                    for i in range(16):
                        ins = t.matmul(ps_sum[0][:], onesc[:], PT[(e0b + i) % 20][:],
                                       start=(i == 0), stop=(i == 15))
                    ins.then_inc(sem["rsm"], 1)

                def bcast(k2):
                    t.wait_ge(sem["rc"], k2 + 1)
                    if k2 >= 1:
                        t.wait_ge(sem["bce"], k2)
                    t.matmul(ps_bc[0][:], onesr[:], recip[k2 % 2][:],
                             start=True, stop=True).then_inc(sem["bcm"], 1)

                for g in range(32):
                    t.wait_ge(sem["qk%d" % (g % 2)], 96 * (g // 2 + 1))
                    for qc in range(4):
                        k = g * 4 + qc
                        e0 = k * 16
                        qsl = slice(qc * 512, (qc + 1) * 512)

                        def mmS(kt):
                            if e0 + kt >= 4:
                                t.wait_ge(sem["exp"], e0 + kt - 3)
                            t.matmul(ps_s[(e0 + kt) % 4][:],
                                     KT[g % 2][:, kt * 128:(kt + 1) * 128],
                                     QT[g % 2][:, qsl], start=True, stop=True
                                     ).then_inc(sem["ss"], 1)

                        def mmPV(kt):
                            t.wait_ge(sem["exp"], e0 + kt + 1)
                            if kt == 0 and k >= 2:
                                t.wait_ge(sem["ctx"], k - 1)
                            t.matmul(ps_c[qc % 2][:],
                                     V16[g % 2][:, kt * 128:(kt + 1) * 128],
                                     PT[(e0 + kt) % 20][:], start=(kt == 0), stop=(kt == 15)
                                     ).then_inc(sem["pv"], 1)

                        mmS(0)
                        mmS(1)
                        mmS(2)
                        for kt in range(13):
                            mmPV(kt)
                            if kt == 1 and k >= 1:
                                rowsum(k - 1)
                            if kt == 4 and k >= 1:
                                bcast(k - 1)
                            mmS(kt + 3)
                        mmPV(13)
                        mmPV(14)
                        mmPV(15)
                rowsum(127)
                bcast(127)
                t.nop().then_inc(sem["ph"], 1)

            @block.scalar
            def _(s):
                s.wait_ge(sem["ph"], 10)

                def bce_evac(k2):
                    s.wait_ge(sem["bcm"], k2 + 1)
                    if k2 >= 2:
                        s.wait_ge(sem["ctx"], k2 - 1)
                    s.activation(bc_sb[k2 % 2][:], ps_bc[0][:], AF.Copy
                                 ).then_inc(sem["bce"], 1)

                for k in range(128):
                    e0 = k * 16
                    for kt in range(16):
                        T = e0 + kt
                        s.wait_ge(sem["ss"], T + 1)
                        if T >= 20:
                            s.wait_ge(sem["pv"], T - 19)
                        s.activation(PT[T % 20][:], ps_s[T % 4][:],
                                     AF.Exp, scale=SCALE).then_inc(sem["exp"], 1)
                        if kt == 8 and k >= 1:
                            bce_evac(k - 1)
                bce_evac(127)
                s.nop().then_inc(sem["ph"], 1)

            @block.vector
            def _(v):
                v.wait_ge(sem["ph"], 10)

                def vrecip(k2):
                    v.wait_ge(sem["rsm"], k2 + 1)
                    if k2 >= 2:
                        v.wait_ge(sem["bcm"], k2 - 1)
                    v.reciprocal(recip[k2 % 2][:], ps_sum[0][:]).then_inc(sem["rc"], 1)

                def vctx(k2):
                    v.wait_ge(sem["bce"], k2 + 1)
                    v.wait_ge(sem["pv"], 16 * (k2 + 1))
                    if k2 >= 2:
                        v.wait_ge(sem["cs%d" % (k2 % 2)], 16 * (k2 // 2))
                    v.scalar_tensor_tensor(ctxo[k2 % 2][:], ps_c[k2 % 2][:], 1.0,
                                           bc_sb[k2 % 2][:], ALU.mult, ALU.mult
                                           ).then_inc(sem["ctx"], 1)

                for k in range(1, 128):
                    vrecip(k - 1)
                    vctx(k - 1)
                vrecip(127)
                vctx(127)
                v.nop().then_inc(sem["ph"], 1)
        st12.close()

        with contextlib.ExitStack() as st3:
            E3 = st3.enter_context
            wprojT_bf = E3(nc.sbuf_tensor("wprojT_bf", [128, HC * H], BF16))
            ctx_rhs = E3(nc.sbuf_tensor("ctx_rhs", [128, HC * 512], BF16))
            xres_sb = E3(nc.sbuf_tensor("xres_sb", [128, 4 * 2048], F32))
            t1_sb = E3(nc.sbuf_tensor("t1_sb", [128, 4 * 2048], F32))
            pr_sb = [E3(nc.sbuf_tensor(f"pr{i}", [128, 512], BF16)) for i in range(2)]
            x1b = [E3(nc.sbuf_tensor(f"x1b{i}", [128, 2048], BF16)) for i in range(2)]
            x1f = [E3(nc.sbuf_tensor(f"x1f{i}", [128, 2048], F32)) for i in range(2)]
            x1t_blk = E3(nc.sbuf_tensor("x1t_blk", [128, HC * 512], BF16))
            rowsum = [E3(nc.sbuf_tensor(f"rs{i}", [128, 1], F32)) for i in range(2)]
            sumsq = [E3(nc.sbuf_tensor(f"sq{i}", [128, 1], F32)) for i in range(2)]
            meanb = [E3(nc.sbuf_tensor(f"mn{i}", [128, 1], F32)) for i in range(2)]
            ex2 = [E3(nc.sbuf_tensor(f"e2{i}", [128, 1], F32)) for i in range(2)]
            msq = [E3(nc.sbuf_tensor(f"mq{i}", [128, 1], F32)) for i in range(2)]
            varb = [E3(nc.sbuf_tensor(f"vr{i}", [128, 1], F32)) for i in range(2)]
            lnv = [E3(nc.sbuf_tensor(f"lv{i}", [128, 1], F32)) for i in range(2)]
            rstd = [E3(nc.sbuf_tensor(f"rd{i}", [128, 1], F32)) for i in range(2)]
            ps_pr = [E3(nc.psum_tensor(f"ps_pr{i}", [128, 512], F32)) for i in range(2)]
            ps_rm = [E3(nc.psum_tensor(f"ps_rm{i}", [128, 512], F32)) for i in range(4)]
            ps_x1 = [E3(nc.psum_tensor(f"ps_x1{i}", [128, 512], F32)) for i in range(2)]

            x1t3 = x1t_blk[:].rearrange("p (hc jj) -> p hc jj", jj=512)

            @block.sync
            def _(sy):
                sy.wait_ge(sem["ph"], 15)
                sy.wait_ge(sem["cs0"], 16 * 64)
                sy.wait_ge(sem["cs1"], 16 * 64)
                for t in range(8):
                    i, sb = t // 4, t % 4
                    if t >= 1:
                        sy.wait_ge(sem["prm"], 16 * t)
                        sy.wait_ge(sem["t1"], 16 * t)
                    for b in range(B):
                        sy.dma_start(out=ctx_rhs[:, b * 512:(b + 1) * 512],
                                     in_=ctx_dram[i, b, :, sb * 512:(sb + 1) * 512]
                                     ).then_inc(sem["ctxl"], 16)
                    for j in range(4):
                        s0 = sb * 512 + j * 128
                        sy.dma_start(out=xres_sb[:, j * 2048:(j + 1) * 2048],
                                     in_=x_res[i, s0:s0 + 128, :]).then_inc(sem["xrl"], 16)
                sy.nop().then_inc(sem["ph"], 1)

            @block.gpsimd
            def _(g):
                g.wait_ge(sem["ph"], 15)
                wp3 = wprojT.rearrange("(fc p) o -> p fc o", p=128)
                wpb3 = wprojT_bf[:].rearrange("p (fc o) -> p fc o", o=H)
                for q in range(4):
                    g.dma_start(out=wpb3[:, q * 4:(q + 1) * 4, :],
                                in_=wp3[:, q * 4:(q + 1) * 4, :]).then_inc(sem["wql"], 16)
                for t in range(8):
                    i, sb = t // 4, t % 4
                    for j in range(4):
                        gj = t * 4 + j
                        r0 = i * 2048 + sb * 512 + j * 128
                        g.wait_ge(sem["x1fr"], gj + 1)
                        g.dma_start(out=x1R_dram[r0:r0 + 128, :], in_=x1f[j % 2][:]
                                    ).then_inc(sem["xf%d" % (gj % 2)], 16)
                    g.wait_ge(sem["xte"], 16 * (t + 1))
                    rblk = i * 2048 + sb * 512
                    for hc in range(HC):
                        g.dma_start(out=x1T_dram[hc, :, rblk:rblk + 512],
                                    in_=x1t_blk[:, hc * 512:(hc + 1) * 512]
                                    ).then_inc(sem["xts"], 16)
                g.nop().then_inc(sem["ph"], 1)

            @block.tensor
            def _(t_):
                t_.wait_ge(sem["ph"], 15)
                t_.wait_ge(sem["wql"], 128)

                def emit_tr(t, oc):
                    k = t * 16 + oc
                    t_.wait_ge(sem["pre"], k + 1)
                    for j in range(4):
                        gid = t * 16 + (oc // 4) * 4 + j
                        if oc % 4 == 0 and gid >= 4:
                            t_.wait_ge(sem["t1"], gid - 3)
                        ins = t_.matmul(ps_rm[j][:, (oc % 4) * 128:(oc % 4 + 1) * 128],
                                        pr_sb[oc % 2][:, j * 128:(j + 1) * 128], identB[:],
                                        start=(oc % 4 == 0), stop=(oc % 4 == 3))
                    ins.then_inc(sem["prc"], 1)

                for t in range(8):
                    t_.wait_ge(sem["ctxl"], 256 * (t + 1))
                    for oc in range(16):
                        k = t * 16 + oc
                        if k >= 2:
                            t_.wait_ge(sem["pre"], k - 1)
                        for fc in range(HC):
                            ins = t_.matmul(ps_pr[oc % 2][:],
                                            wprojT_bf[:, fc * 2048 + oc * 128: fc * 2048 + (oc + 1) * 128],
                                            ctx_rhs[:, fc * 512:(fc + 1) * 512],
                                            start=(fc == 0), stop=(fc == 15))
                        ins.then_inc(sem["prm"], 1)
                        if oc >= 1:
                            emit_tr(t, oc - 1)
                    emit_tr(t, 15)
                    for j in range(4):
                        gj = t * 4 + j
                        t_.wait_ge(sem["x1b"], gj + 1)
                        for hcg in range(4):
                            xg = 4 * gj + hcg
                            if xg >= 2:
                                t_.wait_ge(sem["xte"], xg - 1)
                            for u in range(4):
                                hc = hcg * 4 + u
                                ins = t_.matmul(ps_x1[hcg % 2][:, u * 128:(u + 1) * 128],
                                                x1b[j % 2][:, hc * 128:(hc + 1) * 128], identB[:],
                                                start=(u == 0), stop=(u == 3))
                            ins.then_inc(sem["xt1"], 1)
                t_.nop().then_inc(sem["ph"], 1)

            @block.scalar
            def _(s):
                s.wait_ge(sem["ph"], 15)
                for t in range(8):
                    for oc in range(16):
                        k = t * 16 + oc
                        s.wait_ge(sem["prm"], k + 1)
                        if k >= 2:
                            s.wait_ge(sem["prc"], k - 1)
                        s.activation(pr_sb[oc % 2][:], ps_pr[oc % 2][:], AF.Identity,
                                     bias=bias_sb[:, oc:oc + 1]).then_inc(sem["pre"], 1)
                    for j in range(4):
                        gj = t * 4 + j
                        s.wait_ge(sem["t1"], t * 16 + 12 + j + 1)
                        if gj >= 2:
                            s.wait_ge(sem["xt1"], 4 * (gj - 1))
                        s.activation(x1b[j % 2][:], t1_sb[:, j * 2048:(j + 1) * 2048],
                                     AF.Square, accum_out=sumsq[j % 2][:]
                                     ).then_inc(sem["l2"], 1)
                        s.wait_ge(sem["l3"], gj + 1)
                        s.activation(lnv[j % 2][:], varb[j % 2][:], AF.Ln
                                     ).then_inc(sem["l4"], 1)
                        s.wait_ge(sem["l4"], gj + 1)
                        s.activation(rstd[j % 2][:], lnv[j % 2][:], AF.Exp, scale=-0.5
                                     ).then_inc(sem["l5"], 1)
                        for hcg in range(4):
                            xg = 4 * gj + hcg
                            s.wait_ge(sem["xt1"], xg + 1)
                            if t >= 1 and hcg == 0 and j == 0:
                                s.wait_ge(sem["xts"], 256 * t)
                            s.activation(x1t3[:, hcg * 4:(hcg + 1) * 4, j * 128:(j + 1) * 128],
                                         ps_x1[hcg % 2][:].rearrange("p (u c) -> p u c", c=128),
                                         AF.Copy).then_inc(sem["xte"], 1)
                s.nop().then_inc(sem["ph"], 1)

            @block.vector
            def _(v):
                v.wait_ge(sem["ph"], 15)
                for t in range(8):
                    for ocg in range(4):
                        for j in range(4):
                            gid = t * 16 + ocg * 4 + j
                            v.wait_ge(sem["prc"], t * 16 + 4 * ocg + 4)
                            v.wait_ge(sem["xrl"], 64 * (t + 1))
                            v.scalar_tensor_tensor(
                                t1_sb[:, j * 2048 + ocg * 512: j * 2048 + (ocg + 1) * 512],
                                ps_rm[j][:], 1.0,
                                xres_sb[:, j * 2048 + ocg * 512: j * 2048 + (ocg + 1) * 512],
                                ALU.mult, ALU.add).then_inc(sem["t1"], 1)
                    for j in range(4):
                        gj = t * 4 + j
                        tsl = t1_sb[:, j * 2048:(j + 1) * 2048]
                        v.wait_ge(sem["t1"], t * 16 + 12 + j + 1)
                        n0 = 5 * gj
                        v.reduce_sum(rowsum[j % 2][:], tsl, axis=AX.X).then_inc(sem["l1"], 1)
                        v.wait_ge(sem["l2"], gj + 1)
                        v.wait_ge(sem["l1"], n0 + 1)
                        v.tensor_scalar(meanb[j % 2][:], rowsum[j % 2][:], INVH, None,
                                        ALU.mult).then_inc(sem["l1"], 1)
                        v.tensor_scalar(ex2[j % 2][:], sumsq[j % 2][:], INVH, None,
                                        ALU.mult).then_inc(sem["l1"], 1)
                        v.wait_ge(sem["l1"], n0 + 3)
                        v.tensor_tensor(msq[j % 2][:], meanb[j % 2][:], meanb[j % 2][:],
                                        ALU.mult).then_inc(sem["l1"], 1)
                        v.wait_ge(sem["l1"], n0 + 4)
                        v.tensor_tensor(varb[j % 2][:], ex2[j % 2][:], msq[j % 2][:],
                                        ALU.subtract).then_inc(sem["l1"], 1)
                        v.wait_ge(sem["l1"], n0 + 5)
                        v.tensor_scalar(varb[j % 2][:], varb[j % 2][:], EPS, None,
                                        ALU.add).then_inc(sem["l3"], 1)
                        v.wait_ge(sem["l5"], gj + 1)
                        v.tensor_scalar(x1b[j % 2][:], tsl, meanb[j % 2][:], rstd[j % 2][:],
                                        ALU.subtract, ALU.mult).then_inc(sem["x1b"], 1)
                        if gj >= 2:
                            v.wait_ge(sem["xf%d" % (gj % 2)], 16 * (gj // 2))
                        v.tensor_scalar(x1f[j % 2][:], tsl, meanb[j % 2][:], rstd[j % 2][:],
                                        ALU.subtract, ALU.mult).then_inc(sem["x1fr"], 1)
                v.nop().then_inc(sem["ph"], 1)

        with contextlib.ExitStack() as st4:
            E4 = st4.enter_context
            x1T_sb = E4(nc.sbuf_tensor("x1T_sb", [128, HC * 4096], BF16))
            wg_t = [E4(nc.sbuf_tensor(f"wg{i}", [128, H], BF16)) for i in range(2)]
            wu_t = [E4(nc.sbuf_tensor(f"wu{i}", [128, H], BF16)) for i in range(2)]
            sg = [E4(nc.sbuf_tensor(f"sg{i}", [128, 512], F32)) for i in range(2)]
            h_sb = [E4(nc.sbuf_tensor(f"hs{i}", [128, 4096], BF16)) for i in range(2)]
            ps_g = [E4(nc.psum_tensor(f"ps_g{i}", [128, 512], F32)) for i in range(2)]
            ps_u = [E4(nc.psum_tensor(f"ps_u{i}", [128, 512], F32)) for i in range(2)]

            @block.sync
            def _(sy):
                sy.wait_ge(sem["ph"], 20)
                sy.wait_ge(sem["xts"], 256 * 8)
                for hc in range(HC):
                    sy.dma_start(out=x1T_sb[:, hc * 4096:(hc + 1) * 4096],
                                 in_=x1T_dram[hc]).then_inc(sem["h5"], 16)
                sy.wait_ge(sem["cc"], 1)
                for oc in range(OCI):
                    if oc >= 2:
                        sy.wait_ge(sem["gg"], 8 * (oc - 1))
                    sy.dma_start(out=wg_t[oc % 2][:], in_=ago[SLOT_G + oc]
                                 ).then_inc(sem["wl%d" % (oc % 2)], 16)
                    sy.dma_start(out=wu_t[oc % 2][:], in_=ago[SLOT_U + oc]
                                 ).then_inc(sem["wl%d" % (oc % 2)], 16)
                sy.nop().then_inc(sem["ph"], 1)

            @block.gpsimd
            def _(g):
                g.wait_ge(sem["ph"], 20)
                for oc in range(OCI):
                    g.wait_ge(sem["hh"], 8 * (oc + 1))
                    g.dma_start(out=h_dram[oc * 128:(oc + 1) * 128, :], in_=h_sb[oc % 2][:]
                                ).then_inc(sem["hs%d" % (oc % 2)], 16)
                g.nop().then_inc(sem["ph"], 1)

            @block.tensor
            def _(t):
                t.wait_ge(sem["ph"], 20)
                t.wait_ge(sem["h5"], 256)
                for oc in range(OCI):
                    t.wait_ge(sem["wl%d" % (oc % 2)], 32 * (oc // 2 + 1))
                    for rc in range(8):
                        m = oc * 8 + rc
                        if m >= 2:
                            t.wait_ge(sem["sil"], m - 1)
                            t.wait_ge(sem["hh"], m - 1)
                        for hc in range(HC):
                            t.matmul(ps_g[rc % 2][:], wg_t[oc % 2][:, hc * 128:(hc + 1) * 128],
                                     x1T_sb[:, hc * 4096 + rc * 512: hc * 4096 + (rc + 1) * 512],
                                     start=(hc == 0), stop=(hc == 15))
                        for hc in range(HC):
                            ins = t.matmul(ps_u[rc % 2][:], wu_t[oc % 2][:, hc * 128:(hc + 1) * 128],
                                           x1T_sb[:, hc * 4096 + rc * 512: hc * 4096 + (rc + 1) * 512],
                                           start=(hc == 0), stop=(hc == 15))
                        ins.then_inc(sem["gg"], 1)
                t.nop().then_inc(sem["ph"], 1)

            @block.scalar
            def _(s):
                s.wait_ge(sem["ph"], 20)
                for oc in range(OCI):
                    for rc in range(8):
                        m = oc * 8 + rc
                        s.wait_ge(sem["gg"], m + 1)
                        if m >= 2:
                            s.wait_ge(sem["hh"], m - 1)
                        s.activation(sg[rc % 2][:], ps_g[rc % 2][:], AF.Silu
                                     ).then_inc(sem["sil"], 1)
                s.nop().then_inc(sem["ph"], 1)

            @block.vector
            def _(v):
                v.wait_ge(sem["ph"], 20)
                for oc in range(OCI):
                    for rc in range(8):
                        m = oc * 8 + rc
                        v.wait_ge(sem["sil"], m + 1)
                        if rc == 0 and oc >= 2:
                            v.wait_ge(sem["hs%d" % (oc % 2)], 16 * (oc // 2))
                        v.scalar_tensor_tensor(h_sb[oc % 2][:, rc * 512:(rc + 1) * 512],
                                               ps_u[rc % 2][:], 1.0, sg[rc % 2][:],
                                               ALU.mult, ALU.mult).then_inc(sem["hh"], 1)
                v.nop().then_inc(sem["ph"], 1)

        with contextlib.ExitStack() as st5:
            E5 = st5.enter_context
            h5_sb = E5(nc.sbuf_tensor("h5_sb", [128, OCI * 512], BF16))
            wd_t = [E5(nc.sbuf_tensor(f"wd{i}", [128, INTER], BF16)) for i in range(2)]
            dn_sb = E5(nc.sbuf_tensor("dn_sb", [128, 16 * 512], BF16))
            x1r_sb = [E5(nc.sbuf_tensor(f"x1r{i}", [128, 2048], F32)) for i in range(4)]
            t2_sb = E5(nc.sbuf_tensor("t2_sb", [128, 4 * 2048], F32))
            rw5 = [E5(nc.sbuf_tensor(f"rw5{i}", [128, 1], F32)) for i in range(2)]
            sq5 = [E5(nc.sbuf_tensor(f"sq5{i}", [128, 1], F32)) for i in range(2)]
            mn5 = [E5(nc.sbuf_tensor(f"mn5{i}", [128, 1], F32)) for i in range(2)]
            e25 = [E5(nc.sbuf_tensor(f"e25{i}", [128, 1], F32)) for i in range(2)]
            mq5 = [E5(nc.sbuf_tensor(f"mq5{i}", [128, 1], F32)) for i in range(2)]
            vr5 = [E5(nc.sbuf_tensor(f"vr5{i}", [128, 1], F32)) for i in range(2)]
            lv5 = [E5(nc.sbuf_tensor(f"lv5{i}", [128, 1], F32)) for i in range(2)]
            rd5 = [E5(nc.sbuf_tensor(f"rd5{i}", [128, 1], F32)) for i in range(2)]
            ps_d = [E5(nc.psum_tensor(f"ps_d{i}", [128, 512], F32)) for i in range(2)]
            ps_r5 = [E5(nc.psum_tensor(f"ps_r5{i}", [128, 512], F32)) for i in range(4)]

            h_3 = h_dram.rearrange("(t p) r -> p t r", p=128)
            h5_3 = h5_sb[:].rearrange("p (t c) -> p t c", c=512)
            tchunks = [(0, 4), (4, 8), (8, 12), (12, 16), (16, 20), (20, 24),
                       (24, 28), (28, 32), (32, 36), (36, 40), (40, 43)]

            @block.sync
            def _(sy):
                sy.wait_ge(sem["ph"], 25)
                sy.wait_ge(sem["hs0"], 16 * 22)
                sy.wait_ge(sem["hs1"], 16 * 21)
                sy.wait_ge(sem["xf0"], 16 * 16)
                sy.wait_ge(sem["xf1"], 16 * 16)
                for rc in range(8):
                    r0 = rc * 512
                    i_n, s0 = rc // 4, (rc % 4) * 512
                    if rc >= 1:
                        sy.wait_ge(sem["dnm"], 16 * rc)
                    for (t0, t1) in tchunks:
                        sy.dma_start(out=h5_3[:, t0:t1, :], in_=h_3[:, t0:t1, r0:r0 + 512]
                                     ).then_inc(sem["h5"], 16)
                    for d2c in range(16):
                        dm = rc * 16 + d2c
                        if dm >= 2:
                            sy.wait_ge(sem["dnm"], dm - 1)
                        for kk in range(3):
                            sy.dma_start(
                                out=wd_t[d2c % 2][:, kk * 2048:kk * 2048 + DW[kk]],
                                in_=ago[SLOT_D + 3 * d2c + kk][:, :DW[kk]],
                            ).then_inc(sem["wd%d" % (d2c % 2)], 16)
                    for j in range(4):
                        if rc >= 1:
                            sy.wait_ge(sem["t15"], (rc - 1) * 16 + 13 + j)
                        sy.dma_start(out=x1r_sb[j][:],
                                     in_=x1R_dram[r0 + j * 128: r0 + (j + 1) * 128, :]
                                     ).then_inc(sem["xr0"], 16)
                sy.nop().then_inc(sem["ph"], 1)

            @block.gpsimd
            def _(g):
                g.wait_ge(sem["ph"], 25)
                for rc in range(8):
                    i_n, s0 = rc // 4, (rc % 4) * 512
                    for j in range(4):
                        gj5 = rc * 4 + j
                        g.wait_ge(sem["yrd"], gj5 + 1)
                        g.dma_start(out=y[i_n, s0 + j * 128: s0 + (j + 1) * 128, :],
                                    in_=t2_sb[:, j * 2048:(j + 1) * 2048]
                                    ).then_inc(sem["ou0"], 16)
                g.wait_ge(sem["ou0"], 64 * 8)
                g.nop().then_inc(sem["ph"], 1)

            @block.tensor
            def _(t):
                t.wait_ge(sem["ph"], 25)
                for rc in range(8):
                    t.wait_ge(sem["h5"], 256 + 176 * (rc + 1))
                    for d2c in range(16):
                        dm = rc * 16 + d2c
                        t.wait_ge(sem["wd%d" % (d2c % 2)], 48 * (dm // 2 + 1))
                        if dm >= 2:
                            t.wait_ge(sem["dne"], dm - 1)
                        for oc in range(OCI):
                            ins = t.matmul(ps_d[d2c % 2][:], wd_t[d2c % 2][:, oc * 128:(oc + 1) * 128],
                                           h5_sb[:, oc * 512:(oc + 1) * 512],
                                           start=(oc == 0), stop=(oc == OCI - 1))
                        ins.then_inc(sem["dnm"], 1)
                    for d2c in range(16):
                        dm = rc * 16 + d2c
                        t.wait_ge(sem["dne"], dm + 1)
                        for j in range(4):
                            gid = rc * 16 + (d2c // 4) * 4 + j
                            if d2c % 4 == 0 and gid >= 4:
                                t.wait_ge(sem["t15"], gid - 3)
                            ins = t.matmul(ps_r5[j][:, (d2c % 4) * 128:(d2c % 4 + 1) * 128],
                                           dn_sb[:, d2c * 512 + j * 128: d2c * 512 + (j + 1) * 128],
                                           identB[:], start=(d2c % 4 == 0), stop=(d2c % 4 == 3))
                        ins.then_inc(sem["trm5"], 1)
                t.nop().then_inc(sem["ph"], 1)

            @block.scalar
            def _(s):
                s.wait_ge(sem["ph"], 25)
                for rc in range(8):
                    for d2c in range(16):
                        dm = rc * 16 + d2c
                        s.wait_ge(sem["dnm"], dm + 1)
                        if rc >= 1 and d2c == 0:
                            s.wait_ge(sem["trm5"], 16 * rc)
                        s.activation(dn_sb[:, d2c * 512:(d2c + 1) * 512], ps_d[d2c % 2][:],
                                     AF.Copy).then_inc(sem["dne"], 1)
                    for j in range(4):
                        gj5 = rc * 4 + j
                        s.wait_ge(sem["t15"], rc * 16 + 12 + j + 1)
                        s.activation(dn_sb[:, j * 2048:(j + 1) * 2048],
                                     t2_sb[:, j * 2048:(j + 1) * 2048], AF.Square,
                                     accum_out=sq5[j % 2][:]).then_inc(sem["m2"], 1)
                        s.wait_ge(sem["m3"], gj5 + 1)
                        s.activation(lv5[j % 2][:], vr5[j % 2][:], AF.Ln
                                     ).then_inc(sem["m4"], 1)
                        s.wait_ge(sem["m4"], gj5 + 1)
                        s.activation(rd5[j % 2][:], lv5[j % 2][:], AF.Exp, scale=-0.5
                                     ).then_inc(sem["m5"], 1)
                s.nop().then_inc(sem["ph"], 1)

            @block.vector
            def _(v):
                v.wait_ge(sem["ph"], 25)
                for rc in range(8):
                    for d2g in range(4):
                        for j in range(4):
                            gid = rc * 16 + d2g * 4 + j
                            v.wait_ge(sem["trm5"], rc * 16 + 4 * d2g + 4)
                            v.wait_ge(sem["xr0"], 64 * (rc + 1))
                            if d2g == 0 and rc >= 1:
                                v.wait_ge(sem["ou0"], 64 * rc)
                            v.scalar_tensor_tensor(
                                t2_sb[:, j * 2048 + d2g * 512: j * 2048 + (d2g + 1) * 512],
                                ps_r5[j][:], 1.0,
                                x1r_sb[j][:, d2g * 512:(d2g + 1) * 512],
                                ALU.mult, ALU.add).then_inc(sem["t15"], 1)
                    for j in range(4):
                        gj5 = rc * 4 + j
                        tsl = t2_sb[:, j * 2048:(j + 1) * 2048]
                        v.wait_ge(sem["t15"], rc * 16 + 12 + j + 1)
                        n0 = 5 * gj5
                        v.reduce_sum(rw5[j % 2][:], tsl, axis=AX.X).then_inc(sem["m1"], 1)
                        v.wait_ge(sem["m2"], gj5 + 1)
                        v.wait_ge(sem["m1"], n0 + 1)
                        v.tensor_scalar(mn5[j % 2][:], rw5[j % 2][:], INVH, None,
                                        ALU.mult).then_inc(sem["m1"], 1)
                        v.tensor_scalar(e25[j % 2][:], sq5[j % 2][:], INVH, None,
                                        ALU.mult).then_inc(sem["m1"], 1)
                        v.wait_ge(sem["m1"], n0 + 3)
                        v.tensor_tensor(mq5[j % 2][:], mn5[j % 2][:], mn5[j % 2][:],
                                        ALU.mult).then_inc(sem["m1"], 1)
                        v.wait_ge(sem["m1"], n0 + 4)
                        v.tensor_tensor(vr5[j % 2][:], e25[j % 2][:], mq5[j % 2][:],
                                        ALU.subtract).then_inc(sem["m1"], 1)
                        v.wait_ge(sem["m1"], n0 + 5)
                        v.tensor_scalar(vr5[j % 2][:], vr5[j % 2][:], EPS, None,
                                        ALU.add).then_inc(sem["m3"], 1)
                        v.wait_ge(sem["m5"], gj5 + 1)
                        v.tensor_scalar(tsl, tsl, mn5[j % 2][:], rd5[j % 2][:],
                                        ALU.subtract, ALU.mult).then_inc(sem["yrd"], 1)
                v.nop().then_inc(sem["ph"], 1)
    return nc


def _gather_blocks(gi, ui, di):
    blocks = np.zeros((8 * NJL, 128, H), dtype=np.int32)
    for oc in range(OCI):
        blk = gi[oc * 128:(oc + 1) * 128, :].T.reshape(16, 128, 128)
        blocks[SLOT_G + oc] = blk.transpose(1, 0, 2).reshape(128, H)
        blk = ui[oc * 128:(oc + 1) * 128, :].T.reshape(16, 128, 128)
        blocks[SLOT_U + oc] = blk.transpose(1, 0, 2).reshape(128, H)
    for dc in range(16):
        wd = di[dc * 128:(dc + 1) * 128, :].T.reshape(OCI, 128, 128)
        wd = wd.transpose(1, 0, 2).reshape(128, INTER)
        for kk in range(3):
            blocks[SLOT_D + 3 * dc + kk, :, :DW[kk]] = \
                wd[:, kk * 2048:kk * 2048 + DW[kk]]
    return np.ascontiguousarray(blocks)


def kernel(**inputs):
    _ = inputs["L0_attn_key"], inputs["L0_attn_value"]
    x = np.ascontiguousarray(np.asarray(inputs["L0_attn_query"], dtype=np.float32))
    W = np.asarray(inputs["L0_attn_Wqkv"], dtype=np.float32)
    Wp = np.asarray(inputs["L0_attn_Wproj"], dtype=np.float32)
    bp = np.asarray(inputs["L0_attn_bproj"], dtype=np.float32)
    lut = np.asarray(inputs["L0_ffn_lut"], dtype=np.float32)
    gi = np.ascontiguousarray(np.asarray(inputs["L0_ffn_gate_idx"], dtype=np.int32))
    ui = np.ascontiguousarray(np.asarray(inputs["L0_ffn_up_idx"], dtype=np.int32))
    di = np.ascontiguousarray(np.asarray(inputs["L0_ffn_down_idx"], dtype=np.int32))

    xT = np.ascontiguousarray(x.transpose(2, 0, 1))
    wprojT = np.ascontiguousarray(Wp.T)
    bias2dT = np.ascontiguousarray(bp.reshape(16, 128).T)
    lut2d = np.ascontiguousarray(lut.reshape(1, 16))
    ident = np.eye(128, dtype=np.float32)
    onesc = np.ones((128, 1), dtype=np.float32)
    onesr = np.ones((1, 128), dtype=np.float32)
    gblocks = _gather_blocks(gi, ui, di)

    nc = build_nc()
    in_maps = []
    for c in range(8):
        n0, n1 = 2 * c, 2 * c + 1
        wq = np.concatenate([
            W[n0 * 384: n0 * 384 + 128],
            W[n0 * 384 + 128: n0 * 384 + 256],
            W[n1 * 384: n1 * 384 + 128],
            W[n1 * 384 + 128: n1 * 384 + 256],
            W[n0 * 384 + 256: n0 * 384 + 384],
            W[n1 * 384 + 256: n1 * 384 + 384],
        ], axis=0)
        in_maps.append({
            "xT_in": xT, "x_res": np.ascontiguousarray(x[n0:n1 + 1]),
            "wqkvT": np.ascontiguousarray(wq.T), "wprojT": wprojT,
            "bias2dT": bias2dT, "lut2d": lut2d,
            "idx_sh": gblocks[c * NJL:(c + 1) * NJL],
            "ident_in": ident, "onesc_in": onesc, "onesr_in": onesr,
        })
    res = run_bass_kernel_spmd(nc, in_maps, core_ids=list(range(8)))
    return np.concatenate([np.asarray(r["y"]) for r in res.results], axis=0)


# version 18
# speedup vs baseline: 1.0447x; 1.0447x over previous
import contextlib
import numpy as np
import concourse.bass as bass
import concourse.mybir as mybir
from concourse.bass_utils import run_bass_kernel_spmd

F32 = mybir.dt.float32
BF16 = mybir.dt.bfloat16
I32 = mybir.dt.int32
ALU = mybir.AluOpType
AF = mybir.ActivationFunctionType
AX = mybir.AxisListType

B = 16
S = 2048
H = 2048
HN = 128
HC = 16
INTER = 5504
OCI = 43
NBLK = 64
SCALE = float(1.0 / np.sqrt(HN))
EPS = 1e-5
INVH = 1.0 / 2048.0
NJL = 17
SLOT_G = 0
SLOT_U = OCI
SLOT_D = 2 * OCI
DW = (2048, 2048, INTER - 4096)


def build_nc(debug_dump=False):
    nc = bass.Bass("TRN2", target_bir_lowering=False)

    xT_in = nc.dram_tensor("xT_in", [H, B, S], F32, kind="ExternalInput").ap()
    x_res = nc.dram_tensor("x_res", [2, S, H], F32, kind="ExternalInput").ap()
    wqkvT = nc.dram_tensor("wqkvT", [H, 768], F32, kind="ExternalInput").ap()
    wprojT = nc.dram_tensor("wprojT", [H, H], F32, kind="ExternalInput").ap()
    bias2dT = nc.dram_tensor("bias2dT", [128, 16], F32, kind="ExternalInput").ap()
    lut2d = nc.dram_tensor("lut2d", [1, 16], F32, kind="ExternalInput").ap()
    idx_sh = nc.dram_tensor("idx_sh", [NJL, 128, H], I32, kind="ExternalInput").ap()
    ident_in = nc.dram_tensor("ident_in", [128, 128], F32, kind="ExternalInput").ap()
    onesc_in = nc.dram_tensor("onesc_in", [128, 1], F32, kind="ExternalInput").ap()
    onesr_in = nc.dram_tensor("onesr_in", [1, 128], F32, kind="ExternalInput").ap()
    y = nc.dram_tensor("y", [2, S, H], F32, kind="ExternalOutput").ap()

    KD = "ExternalOutput" if debug_dump else "Internal"
    qk_dram = nc.dram_tensor("qk_dram", [4, B, 128, S], BF16, kind=KD).ap()
    v_dram = nc.dram_tensor("v_dram", [2, B, S, 128], BF16, kind=KD).ap()
    ctx_dram = nc.dram_tensor("ctx_dram", [2, B, 128, S], BF16, kind=KD).ap()
    x1T_dram = nc.dram_tensor("x1T_dram", [HC, 128, 4096], BF16, kind=KD).ap()
    x1R_dram = nc.dram_tensor("x1R_dram", [4096, H], F32, kind=KD).ap()
    h_dram = nc.dram_tensor("h_dram", [INTER, 4096], BF16, kind=KD).ap()
    ag_in = nc.dram_tensor("ag_in", [NJL, 128, H], BF16, kind="Internal")
    ag_out = nc.dram_tensor("ag_out", [8 * NJL, 128, H], BF16, kind="Internal",
                            addr_space="Shared")
    ago = ag_out.ap()

    with contextlib.ExitStack() as st0:
        E = st0.enter_context

        identB = E(nc.sbuf_tensor("identB", [128, 128], BF16))
        identF = E(nc.sbuf_tensor("identF", [128, 128], F32))
        onesc = E(nc.sbuf_tensor("onesc", [128, 1], BF16))
        onesr = E(nc.sbuf_tensor("onesr", [1, 128], F32))
        lut_sb = E(nc.sbuf_tensor("lut_sb", [1, 16], F32))
        lutB = E(nc.sbuf_tensor("lutB", [128, 16], F32))
        bias_sb = E(nc.sbuf_tensor("bias_sb", [128, 16], F32))

        names = ["ph", "c0", "c1", "xld", "wql", "qm", "qe", "qs0", "qs1", "vm",
                 "ve", "vs0", "vs1", "gl", "gc", "gd", "gs", "cc",
                 "pa", "pb", "rsm", "gg", "uu",
                 "qk0", "qk1", "ss", "exp", "pv", "rc", "bcm", "bce", "ctx", "cs0", "cs1",
                 "ctxl", "prm", "prc", "pre", "t1", "xrl",
                 "l1", "l2", "l3", "l4", "l5", "x1b", "x1fr", "xf0", "xf1",
                 "xt1", "xte", "xts",
                 "wl0", "wl1", "gum", "sil", "hh", "hs0", "hs1",
                 "h5", "wd0", "wd1", "dnm", "dne", "trm5", "t15",
                 "m1", "m2", "m3", "m4", "m5", "yrd", "ou0", "xr0", "misc"]
        sem = {n: E(nc.semaphore("s_" + n)) for n in names}
        block = E(nc.Block())

        @block.gpsimd
        def _(g):
            g.dma_start(out=identB[:], in_=ident_in).then_inc(sem["c0"], 16)
            g.dma_start(out=onesc[:], in_=onesc_in).then_inc(sem["c0"], 16)
            g.wait_ge(sem["c0"], 32)
            g.nop().then_inc(sem["ph"], 1)

        @block.sync
        def _(sy):
            sy.dma_start(out=identF[:], in_=ident_in).then_inc(sem["c1"], 16)
            sy.dma_start(out=onesr[:], in_=onesr_in).then_inc(sem["c1"], 16)
            sy.dma_start(out=lut_sb[:], in_=lut2d).then_inc(sem["c1"], 16)
            sy.dma_start(out=bias_sb[:], in_=bias2dT).then_inc(sem["c1"], 16)
            sy.wait_ge(sem["c1"], 64)
            sy.nop().then_inc(sem["ph"], 1)

        with nc.psum_tensor("ps_lut", [128, 16], F32) as ps_lut:
            @block.tensor
            def _(t):
                t.wait_ge(sem["c1"], 64)
                t.matmul(ps_lut[:], onesr[:], lut_sb[:], start=True, stop=True
                         ).then_inc(sem["misc"], 1)
                t.nop().then_inc(sem["ph"], 1)

            @block.scalar
            def _(s):
                s.wait_ge(sem["misc"], 1)
                s.activation(lutB[:], ps_lut[:], AF.Copy).then_inc(sem["ph"], 1)

        @block.vector
        def _(v):
            v.nop().then_inc(sem["ph"], 1)

        st12 = contextlib.ExitStack()
        E12 = st12.enter_context
        QT = [E12(nc.sbuf_tensor(f"QT{i}", [128, S], BF16)) for i in range(2)]
        KT = [E12(nc.sbuf_tensor(f"KT{i}", [128, S], BF16)) for i in range(2)]
        V16 = [E12(nc.sbuf_tensor(f"V16{i}", [128, S], BF16)) for i in range(2)]
        PT = [E12(nc.sbuf_tensor(f"PT{i}", [128, 512], BF16)) for i in range(20)]
        recip = [E12(nc.sbuf_tensor(f"recip{i}", [1, 512], F32)) for i in range(2)]
        bc_sb = [E12(nc.sbuf_tensor(f"bc{i}", [128, 512], F32)) for i in range(2)]
        ctxo = [E12(nc.sbuf_tensor(f"ctxo{i}", [128, 512], BF16)) for i in range(2)]

        with contextlib.ExitStack() as st1:
            E1 = st1.enter_context
            wqkvT_bf = E1(nc.sbuf_tensor("wqkvT_bf", [128, HC * 768], BF16))
            XT = [E1(nc.sbuf_tensor(f"XT{i}", [128, HC * 512], BF16)) for i in range(3)]
            qk_buf = [E1(nc.sbuf_tensor(f"qkb{i}", [128, 512], BF16)) for i in range(2)]
            v_buf = [E1(nc.sbuf_tensor(f"vb{i}", [128, 256], BF16)) for i in range(2)]
            idx_i = [E1(nc.sbuf_tensor(f"idx_i{i}", [128, H], I32)) for i in range(2)]
            idxb = [E1(nc.sbuf_tensor(f"idxb{i}", [128, H], BF16)) for i in range(2)]
            wacc = [E1(nc.sbuf_tensor(f"wacc{i}", [128, H], BF16)) for i in range(2)]
            wtmp = E1(nc.sbuf_tensor("wtmp", [128, H], BF16))
            ps_qk = [E1(nc.psum_tensor(f"ps_qk{i}", [128, 512], F32)) for i in range(3)]
            ps_v = [E1(nc.psum_tensor(f"ps_v{i}", [128, 256], F32)) for i in range(3)]

            @block.sync
            def _(sy):
                sy.wait_ge(sem["ph"], 5)
                for j in range(NJL):
                    if j >= 2:
                        sy.wait_ge(sem["gc"], j - 1)
                    sy.dma_start(out=idx_i[j % 2][:], in_=idx_sh[j]
                                 ).then_inc(sem["gl"], 16)
                sy.nop().then_inc(sem["ph"], 1)

            @block.gpsimd
            def _(g):
                g.wait_ge(sem["ph"], 5)
                wq3 = wqkvT.rearrange("(hc p) o -> p hc o", p=128)
                wb3 = wqkvT_bf[:].rearrange("p (hc o) -> p hc o", o=768)
                for q in range(4):
                    g.dma_start(out=wb3[:, q * 4:(q + 1) * 4, :],
                                in_=wq3[:, q * 4:(q + 1) * 4, :]).then_inc(sem["wql"], 16)
                for blk in range(NBLK):
                    b, sb = blk // 4, blk % 4
                    if blk >= 3:
                        g.wait_ge(sem["qm"], 4 * (blk - 2))
                        g.wait_ge(sem["vm"], 4 * (blk - 2))
                    g.dma_start(
                        out=XT[blk % 3][:].rearrange("p (hc s) -> p hc s", s=512),
                        in_=xT_in[:, b, sb * 512:(sb + 1) * 512].rearrange(
                            "(hc p) s -> p hc s", p=128),
                    ).then_inc(sem["xld"], 16)
                    if blk % 2 == 0 and blk // 2 < NJL:
                        j = blk // 2
                        g.wait_ge(sem["gl"], 16 * (j + 1))
                        if j >= 2:
                            g.wait_ge(sem["gd"], j - 1)
                        g.tensor_copy(idxb[j % 2][:], idx_i[j % 2][:]
                                      ).then_inc(sem["gc"], 1)
                        if j >= 1:
                            g.wait_ge(sem["gd"], j)
                            g.dma_start(out=ag_in.ap()[j - 1], in_=wacc[(j - 1) % 2][:]
                                        ).then_inc(sem["gs"], 16)
                g.wait_ge(sem["gd"], NJL)
                g.dma_start(out=ag_in.ap()[NJL - 1], in_=wacc[(NJL - 1) % 2][:]
                            ).then_inc(sem["gs"], 16)
                g.wait_ge(sem["gs"], 16 * NJL)
                g.collective_compute(
                    "AllGather", ALU.bypass, replica_groups=[list(range(8))],
                    ins=[ag_in.ap().opt()], outs=[ago.opt()],
                ).then_inc(sem["cc"])
                g.nop().then_inc(sem["ph"], 1)

            @block.scalar
            def _(s):
                s.wait_ge(sem["ph"], 5)
                for blk in range(NBLK):
                    b, sb = blk // 4, blk % 4
                    srange = slice(sb * 512, (sb + 1) * 512)
                    for oc in range(4):
                        k = blk * 4 + oc
                        s.wait_ge(sem["qm"], k + 1)
                        if k >= 2:
                            s.wait_ge(sem["qs%d" % (k % 2)], 16 * (k // 2))
                        s.activation(qk_buf[oc % 2][:], ps_qk[k % 3][:], AF.Copy
                                     ).then_inc(sem["qe"], 1)
                        s.wait_ge(sem["qe"], k + 1)
                        s.dma_start(out=qk_dram[oc, b, :, srange], in_=qk_buf[oc % 2][:]
                                    ).then_inc(sem["qs%d" % (oc % 2)], 16)
                    for ss in range(4):
                        k = blk * 4 + ss
                        s.wait_ge(sem["vm"], k + 1)
                        if k >= 2:
                            s.wait_ge(sem["vs%d" % (k % 2)], 32 * (k // 2))
                        s.activation(v_buf[ss % 2][:], ps_v[k % 3][:], AF.Copy
                                     ).then_inc(sem["ve"], 1)
                        s.wait_ge(sem["ve"], k + 1)
                        sr2 = slice(sb * 512 + ss * 128, sb * 512 + (ss + 1) * 128)
                        s.dma_start(out=v_dram[0, b, sr2, :], in_=v_buf[ss % 2][:, 0:128]
                                    ).then_inc(sem["vs%d" % (ss % 2)], 16)
                        s.dma_start(out=v_dram[1, b, sr2, :], in_=v_buf[ss % 2][:, 128:256]
                                    ).then_inc(sem["vs%d" % (ss % 2)], 16)
                s.nop().then_inc(sem["ph"], 1)

            @block.tensor
            def _(t):
                t.wait_ge(sem["ph"], 5)
                t.wait_ge(sem["wql"], 64)
                for blk in range(NBLK):
                    t.wait_ge(sem["xld"], 16 * (blk + 1))
                    for oc in range(4):
                        k = blk * 4 + oc
                        if k >= 3:
                            t.wait_ge(sem["qe"], k - 2)
                        for hc in range(HC):
                            ins = t.matmul(ps_qk[k % 3][:],
                                           wqkvT_bf[:, hc * 768 + oc * 128: hc * 768 + (oc + 1) * 128],
                                           XT[blk % 3][:, hc * 512:(hc + 1) * 512],
                                           start=(hc == 0), stop=(hc == 15))
                        ins.then_inc(sem["qm"], 1)
                    for ss in range(4):
                        k = blk * 4 + ss
                        if k >= 3:
                            t.wait_ge(sem["ve"], k - 2)
                        for hc in range(HC):
                            ins = t.matmul(ps_v[k % 3][:],
                                           XT[blk % 3][:, hc * 512 + ss * 128: hc * 512 + (ss + 1) * 128],
                                           wqkvT_bf[:, hc * 768 + 512: hc * 768 + 768],
                                           start=(hc == 0), stop=(hc == 15))
                        ins.then_inc(sem["vm"], 1)
                t.nop().then_inc(sem["ph"], 1)

            @block.vector
            def _(v):
                v.wait_ge(sem["ph"], 5)
                for j in range(NJL):
                    w = H
                    v.wait_ge(sem["gc"], j + 1)
                    if j >= 2:
                        v.wait_ge(sem["gs"], 16 * (j - 1))
                    v.tensor_scalar(wacc[j % 2][:, :w], idxb[j % 2][:, :w], 0.0,
                                    lutB[:, 0:1], ALU.is_equal, ALU.mult)
                    for tt in range(1, 16):
                        v.tensor_scalar(wtmp[:, :w], idxb[j % 2][:, :w], float(tt),
                                        lutB[:, tt:tt + 1], ALU.is_equal, ALU.mult)
                        ins = v.tensor_tensor(wacc[j % 2][:, :w], wacc[j % 2][:, :w],
                                              wtmp[:, :w], ALU.add)
                    ins.then_inc(sem["gd"], 1)
                v.nop().then_inc(sem["ph"], 1)

        with contextlib.ExitStack() as st2:
            E2 = st2.enter_context
            ps_s = [E2(nc.psum_tensor(f"ps_s{i}", [128, 512], F32)) for i in range(4)]
            ps_c = [E2(nc.psum_tensor(f"ps_c{i}", [128, 512], F32)) for i in range(2)]
            ps_sum = [E2(nc.psum_tensor(f"ps_sm{i}", [1, 512], F32)) for i in range(1)]
            ps_bc = [E2(nc.psum_tensor(f"ps_bc{i}", [128, 512], F32)) for i in range(1)]

            @block.sync
            def _(sy):
                for g in range(32):
                    i, b = g // 16, g % 16
                    if g >= 2:
                        sy.wait_ge(sem["pv"], 64 * (g - 1))
                    w0 = 32 * (4 * b + 4)
                    sy.wait_ge(sem["qs0"], w0)
                    sy.wait_ge(sem["qs1"], w0)
                    sy.wait_ge(sem["vs0"], 2 * w0)
                    sy.wait_ge(sem["vs1"], 2 * w0)
                    sy.dma_start(out=QT[g % 2][:], in_=qk_dram[2 * i, b]
                                 ).then_inc(sem["qk%d" % (g % 2)], 16)
                    sy.dma_start(out=KT[g % 2][:], in_=qk_dram[2 * i + 1, b]
                                 ).then_inc(sem["qk%d" % (g % 2)], 16)
                    v3 = v_dram[i, b].rearrange("(t p) d -> p t d", p=128)
                    o3 = V16[g % 2][:].rearrange("p (t d) -> p t d", d=128)
                    for q in range(4):
                        sy.dma_start(out=o3[:, q * 4:(q + 1) * 4, :],
                                     in_=v3[:, q * 4:(q + 1) * 4, :]
                                     ).then_inc(sem["qk%d" % (g % 2)], 16)
                sy.nop().then_inc(sem["ph"], 1)

            @block.gpsimd
            def _(g):
                g.wait_ge(sem["ph"], 10)

                def spill(k2):
                    i, b, qc = k2 // 64, (k2 // 4) % 16, k2 % 4
                    g.wait_ge(sem["ctx"], k2 + 1)
                    g.dma_start(out=ctx_dram[i, b, :, qc * 512:(qc + 1) * 512],
                                in_=ctxo[qc % 2][:]).then_inc(sem["cs%d" % (k2 % 2)], 16)

                for k in range(128):
                    spill(k)
                g.wait_ge(sem["cs0"], 16 * 64)
                g.wait_ge(sem["cs1"], 16 * 64)
                g.nop().then_inc(sem["ph"], 1)

            @block.tensor
            def _(t):
                t.wait_ge(sem["ph"], 10)

                def rowsum(k2):
                    t.wait_ge(sem["exp"], 16 * (k2 + 1))
                    t.wait_ge(sem["rc"], k2)
                    e0b = k2 * 16
                    for i in range(16):
                        ins = t.matmul(ps_sum[0][:], onesc[:], PT[(e0b + i) % 20][:],
                                       start=(i == 0), stop=(i == 15))
                    ins.then_inc(sem["rsm"], 1)

                def bcast(k2):
                    t.wait_ge(sem["rc"], k2 + 1)
                    if k2 >= 1:
                        t.wait_ge(sem["bce"], k2)
                    t.matmul(ps_bc[0][:], onesr[:], recip[k2 % 2][:],
                             start=True, stop=True).then_inc(sem["bcm"], 1)

                for g in range(32):
                    t.wait_ge(sem["qk%d" % (g % 2)], 96 * (g // 2 + 1))
                    for qc in range(4):
                        k = g * 4 + qc
                        e0 = k * 16
                        qsl = slice(qc * 512, (qc + 1) * 512)

                        def mmS(kt):
                            if e0 + kt >= 4:
                                t.wait_ge(sem["exp"], e0 + kt - 3)
                            t.matmul(ps_s[(e0 + kt) % 4][:],
                                     KT[g % 2][:, kt * 128:(kt + 1) * 128],
                                     QT[g % 2][:, qsl], start=True, stop=True
                                     ).then_inc(sem["ss"], 1)

                        def mmPV(kt):
                            t.wait_ge(sem["exp"], e0 + kt + 1)
                            if kt == 0 and k >= 2:
                                t.wait_ge(sem["ctx"], k - 1)
                            t.matmul(ps_c[qc % 2][:],
                                     V16[g % 2][:, kt * 128:(kt + 1) * 128],
                                     PT[(e0 + kt) % 20][:], start=(kt == 0), stop=(kt == 15)
                                     ).then_inc(sem["pv"], 1)

                        mmS(0)
                        mmS(1)
                        mmS(2)
                        for kt in range(13):
                            mmPV(kt)
                            if kt == 1 and k >= 1:
                                rowsum(k - 1)
                            if kt == 4 and k >= 1:
                                bcast(k - 1)
                            mmS(kt + 3)
                        mmPV(13)
                        mmPV(14)
                        mmPV(15)
                rowsum(127)
                bcast(127)
                t.nop().then_inc(sem["ph"], 1)

            @block.scalar
            def _(s):
                s.wait_ge(sem["ph"], 10)

                def bce_evac(k2):
                    s.wait_ge(sem["bcm"], k2 + 1)
                    if k2 >= 2:
                        s.wait_ge(sem["ctx"], k2 - 1)
                    s.activation(bc_sb[k2 % 2][:], ps_bc[0][:], AF.Copy
                                 ).then_inc(sem["bce"], 1)

                for k in range(128):
                    e0 = k * 16
                    for kt in range(16):
                        T = e0 + kt
                        s.wait_ge(sem["ss"], T + 1)
                        if T >= 20:
                            s.wait_ge(sem["pv"], T - 19)
                        s.activation(PT[T % 20][:], ps_s[T % 4][:],
                                     AF.Exp, scale=SCALE).then_inc(sem["exp"], 1)
                        if kt == 8 and k >= 1:
                            bce_evac(k - 1)
                bce_evac(127)
                s.nop().then_inc(sem["ph"], 1)

            @block.vector
            def _(v):
                v.wait_ge(sem["ph"], 10)

                def vrecip(k2):
                    v.wait_ge(sem["rsm"], k2 + 1)
                    if k2 >= 2:
                        v.wait_ge(sem["bcm"], k2 - 1)
                    v.reciprocal(recip[k2 % 2][:], ps_sum[0][:]).then_inc(sem["rc"], 1)

                def vctx(k2):
                    v.wait_ge(sem["bce"], k2 + 1)
                    v.wait_ge(sem["pv"], 16 * (k2 + 1))
                    if k2 >= 2:
                        v.wait_ge(sem["cs%d" % (k2 % 2)], 16 * (k2 // 2))
                    v.scalar_tensor_tensor(ctxo[k2 % 2][:], ps_c[k2 % 2][:], 1.0,
                                           bc_sb[k2 % 2][:], ALU.mult, ALU.mult
                                           ).then_inc(sem["ctx"], 1)

                for k in range(1, 128):
                    vrecip(k - 1)
                    vctx(k - 1)
                vrecip(127)
                vctx(127)
                v.nop().then_inc(sem["ph"], 1)
        st12.close()

        with contextlib.ExitStack() as st3:
            E3 = st3.enter_context
            wprojT_bf = E3(nc.sbuf_tensor("wprojT_bf", [128, HC * H], BF16))
            ctx_rhs = E3(nc.sbuf_tensor("ctx_rhs", [128, HC * 512], BF16))
            xres_sb = E3(nc.sbuf_tensor("xres_sb", [128, 4 * 2048], F32))
            t1_sb = E3(nc.sbuf_tensor("t1_sb", [128, 4 * 2048], F32))
            pr_sb = [E3(nc.sbuf_tensor(f"pr{i}", [128, 512], BF16)) for i in range(2)]
            x1b = [E3(nc.sbuf_tensor(f"x1b{i}", [128, 2048], BF16)) for i in range(2)]
            x1f = [E3(nc.sbuf_tensor(f"x1f{i}", [128, 2048], F32)) for i in range(2)]
            x1t_blk = E3(nc.sbuf_tensor("x1t_blk", [128, HC * 512], BF16))
            rowsum = [E3(nc.sbuf_tensor(f"rs{i}", [128, 1], F32)) for i in range(2)]
            sumsq = [E3(nc.sbuf_tensor(f"sq{i}", [128, 1], F32)) for i in range(2)]
            meanb = [E3(nc.sbuf_tensor(f"mn{i}", [128, 1], F32)) for i in range(2)]
            ex2 = [E3(nc.sbuf_tensor(f"e2{i}", [128, 1], F32)) for i in range(2)]
            msq = [E3(nc.sbuf_tensor(f"mq{i}", [128, 1], F32)) for i in range(2)]
            varb = [E3(nc.sbuf_tensor(f"vr{i}", [128, 1], F32)) for i in range(2)]
            lnv = [E3(nc.sbuf_tensor(f"lv{i}", [128, 1], F32)) for i in range(2)]
            rstd = [E3(nc.sbuf_tensor(f"rd{i}", [128, 1], F32)) for i in range(2)]
            ps_pr = [E3(nc.psum_tensor(f"ps_pr{i}", [128, 512], F32)) for i in range(2)]
            ps_rm = [E3(nc.psum_tensor(f"ps_rm{i}", [128, 512], F32)) for i in range(4)]
            ps_x1 = [E3(nc.psum_tensor(f"ps_x1{i}", [128, 512], F32)) for i in range(2)]

            x1t3 = x1t_blk[:].rearrange("p (hc jj) -> p hc jj", jj=512)

            @block.sync
            def _(sy):
                sy.wait_ge(sem["ph"], 15)
                sy.wait_ge(sem["cs0"], 16 * 64)
                sy.wait_ge(sem["cs1"], 16 * 64)
                for t in range(8):
                    i, sb = t // 4, t % 4
                    if t >= 1:
                        sy.wait_ge(sem["prm"], 16 * t)
                        sy.wait_ge(sem["t1"], 16 * t)
                    for b in range(B):
                        sy.dma_start(out=ctx_rhs[:, b * 512:(b + 1) * 512],
                                     in_=ctx_dram[i, b, :, sb * 512:(sb + 1) * 512]
                                     ).then_inc(sem["ctxl"], 16)
                    for j in range(4):
                        s0 = sb * 512 + j * 128
                        sy.dma_start(out=xres_sb[:, j * 2048:(j + 1) * 2048],
                                     in_=x_res[i, s0:s0 + 128, :]).then_inc(sem["xrl"], 16)
                sy.nop().then_inc(sem["ph"], 1)

            @block.gpsimd
            def _(g):
                g.wait_ge(sem["ph"], 15)
                wp3 = wprojT.rearrange("(fc p) o -> p fc o", p=128)
                wpb3 = wprojT_bf[:].rearrange("p (fc o) -> p fc o", o=H)
                for q in range(4):
                    g.dma_start(out=wpb3[:, q * 4:(q + 1) * 4, :],
                                in_=wp3[:, q * 4:(q + 1) * 4, :]).then_inc(sem["wql"], 16)
                for t in range(8):
                    i, sb = t // 4, t % 4
                    for j in range(4):
                        gj = t * 4 + j
                        r0 = i * 2048 + sb * 512 + j * 128
                        g.wait_ge(sem["x1fr"], gj + 1)
                        g.dma_start(out=x1R_dram[r0:r0 + 128, :], in_=x1f[j % 2][:]
                                    ).then_inc(sem["xf%d" % (gj % 2)], 16)
                    g.wait_ge(sem["xte"], 16 * (t + 1))
                    rblk = i * 2048 + sb * 512
                    for hc in range(HC):
                        g.dma_start(out=x1T_dram[hc, :, rblk:rblk + 512],
                                    in_=x1t_blk[:, hc * 512:(hc + 1) * 512]
                                    ).then_inc(sem["xts"], 16)
                g.nop().then_inc(sem["ph"], 1)

            @block.tensor
            def _(t_):
                t_.wait_ge(sem["ph"], 15)
                t_.wait_ge(sem["wql"], 128)

                def emit_tr(t, oc):
                    k = t * 16 + oc
                    t_.wait_ge(sem["pre"], k + 1)
                    for j in range(4):
                        gid = t * 16 + (oc // 4) * 4 + j
                        if oc % 4 == 0 and gid >= 4:
                            t_.wait_ge(sem["t1"], gid - 3)
                        ins = t_.matmul(ps_rm[j][:, (oc % 4) * 128:(oc % 4 + 1) * 128],
                                        pr_sb[oc % 2][:, j * 128:(j + 1) * 128], identB[:],
                                        start=(oc % 4 == 0), stop=(oc % 4 == 3))
                    ins.then_inc(sem["prc"], 1)

                for t in range(8):
                    t_.wait_ge(sem["ctxl"], 256 * (t + 1))
                    for oc in range(16):
                        k = t * 16 + oc
                        if k >= 2:
                            t_.wait_ge(sem["pre"], k - 1)
                        for fc in range(HC):
                            ins = t_.matmul(ps_pr[oc % 2][:],
                                            wprojT_bf[:, fc * 2048 + oc * 128: fc * 2048 + (oc + 1) * 128],
                                            ctx_rhs[:, fc * 512:(fc + 1) * 512],
                                            start=(fc == 0), stop=(fc == 15))
                        ins.then_inc(sem["prm"], 1)
                        if oc >= 1:
                            emit_tr(t, oc - 1)
                    emit_tr(t, 15)
                    for j in range(4):
                        gj = t * 4 + j
                        t_.wait_ge(sem["x1b"], gj + 1)
                        for hcg in range(4):
                            xg = 4 * gj + hcg
                            if xg >= 2:
                                t_.wait_ge(sem["xte"], xg - 1)
                            for u in range(4):
                                hc = hcg * 4 + u
                                ins = t_.matmul(ps_x1[hcg % 2][:, u * 128:(u + 1) * 128],
                                                x1b[j % 2][:, hc * 128:(hc + 1) * 128], identB[:],
                                                start=(u == 0), stop=(u == 3))
                            ins.then_inc(sem["xt1"], 1)
                t_.nop().then_inc(sem["ph"], 1)

            @block.scalar
            def _(s):
                s.wait_ge(sem["ph"], 15)
                for t in range(8):
                    for oc in range(16):
                        k = t * 16 + oc
                        s.wait_ge(sem["prm"], k + 1)
                        if k >= 2:
                            s.wait_ge(sem["prc"], k - 1)
                        s.activation(pr_sb[oc % 2][:], ps_pr[oc % 2][:], AF.Identity,
                                     bias=bias_sb[:, oc:oc + 1]).then_inc(sem["pre"], 1)
                    for j in range(4):
                        gj = t * 4 + j
                        s.wait_ge(sem["t1"], t * 16 + 12 + j + 1)
                        if gj >= 2:
                            s.wait_ge(sem["xt1"], 4 * (gj - 1))
                        s.activation(x1b[j % 2][:], t1_sb[:, j * 2048:(j + 1) * 2048],
                                     AF.Square, accum_out=sumsq[j % 2][:]
                                     ).then_inc(sem["l2"], 1)
                        s.wait_ge(sem["l3"], gj + 1)
                        s.activation(lnv[j % 2][:], varb[j % 2][:], AF.Ln
                                     ).then_inc(sem["l4"], 1)
                        s.wait_ge(sem["l4"], gj + 1)
                        s.activation(rstd[j % 2][:], lnv[j % 2][:], AF.Exp, scale=-0.5
                                     ).then_inc(sem["l5"], 1)
                        for hcg in range(4):
                            xg = 4 * gj + hcg
                            s.wait_ge(sem["xt1"], xg + 1)
                            if t >= 1 and hcg == 0 and j == 0:
                                s.wait_ge(sem["xts"], 256 * t)
                            s.activation(x1t3[:, hcg * 4:(hcg + 1) * 4, j * 128:(j + 1) * 128],
                                         ps_x1[hcg % 2][:].rearrange("p (u c) -> p u c", c=128),
                                         AF.Copy).then_inc(sem["xte"], 1)
                s.nop().then_inc(sem["ph"], 1)

            @block.vector
            def _(v):
                v.wait_ge(sem["ph"], 15)
                for t in range(8):
                    for ocg in range(4):
                        for j in range(4):
                            gid = t * 16 + ocg * 4 + j
                            v.wait_ge(sem["prc"], t * 16 + 4 * ocg + 4)
                            v.wait_ge(sem["xrl"], 64 * (t + 1))
                            v.scalar_tensor_tensor(
                                t1_sb[:, j * 2048 + ocg * 512: j * 2048 + (ocg + 1) * 512],
                                ps_rm[j][:], 1.0,
                                xres_sb[:, j * 2048 + ocg * 512: j * 2048 + (ocg + 1) * 512],
                                ALU.mult, ALU.add).then_inc(sem["t1"], 1)
                    for j in range(4):
                        gj = t * 4 + j
                        tsl = t1_sb[:, j * 2048:(j + 1) * 2048]
                        v.wait_ge(sem["t1"], t * 16 + 12 + j + 1)
                        n0 = 5 * gj
                        v.reduce_sum(rowsum[j % 2][:], tsl, axis=AX.X).then_inc(sem["l1"], 1)
                        v.wait_ge(sem["l2"], gj + 1)
                        v.wait_ge(sem["l1"], n0 + 1)
                        v.tensor_scalar(meanb[j % 2][:], rowsum[j % 2][:], INVH, None,
                                        ALU.mult).then_inc(sem["l1"], 1)
                        v.tensor_scalar(ex2[j % 2][:], sumsq[j % 2][:], INVH, None,
                                        ALU.mult).then_inc(sem["l1"], 1)
                        v.wait_ge(sem["l1"], n0 + 3)
                        v.tensor_tensor(msq[j % 2][:], meanb[j % 2][:], meanb[j % 2][:],
                                        ALU.mult).then_inc(sem["l1"], 1)
                        v.wait_ge(sem["l1"], n0 + 4)
                        v.tensor_tensor(varb[j % 2][:], ex2[j % 2][:], msq[j % 2][:],
                                        ALU.subtract).then_inc(sem["l1"], 1)
                        v.wait_ge(sem["l1"], n0 + 5)
                        v.tensor_scalar(varb[j % 2][:], varb[j % 2][:], EPS, None,
                                        ALU.add).then_inc(sem["l3"], 1)
                        v.wait_ge(sem["l5"], gj + 1)
                        v.tensor_scalar(x1b[j % 2][:], tsl, meanb[j % 2][:], rstd[j % 2][:],
                                        ALU.subtract, ALU.mult).then_inc(sem["x1b"], 1)
                        if gj >= 2:
                            v.wait_ge(sem["xf%d" % (gj % 2)], 16 * (gj // 2))
                        v.tensor_scalar(x1f[j % 2][:], tsl, meanb[j % 2][:], rstd[j % 2][:],
                                        ALU.subtract, ALU.mult).then_inc(sem["x1fr"], 1)
                v.nop().then_inc(sem["ph"], 1)

        with contextlib.ExitStack() as st4:
            E4 = st4.enter_context
            x1T_sb = E4(nc.sbuf_tensor("x1T_sb", [128, HC * 4096], BF16))
            wg_t = [E4(nc.sbuf_tensor(f"wg{i}", [128, H], BF16)) for i in range(2)]
            wu_t = [E4(nc.sbuf_tensor(f"wu{i}", [128, H], BF16)) for i in range(2)]
            sg = [E4(nc.sbuf_tensor(f"sg{i}", [128, 512], F32)) for i in range(2)]
            h_sb = [E4(nc.sbuf_tensor(f"hs{i}", [128, 4096], BF16)) for i in range(2)]
            ps_g = [E4(nc.psum_tensor(f"ps_g{i}", [128, 512], F32)) for i in range(3)]
            ps_u = [E4(nc.psum_tensor(f"ps_u{i}", [128, 512], F32)) for i in range(3)]

            @block.sync
            def _(sy):
                sy.wait_ge(sem["ph"], 20)
                sy.wait_ge(sem["xts"], 256 * 8)
                for hc in range(HC):
                    sy.dma_start(out=x1T_sb[:, hc * 4096:(hc + 1) * 4096],
                                 in_=x1T_dram[hc]).then_inc(sem["h5"], 16)
                sy.wait_ge(sem["cc"], 1)
                for oc in range(OCI):
                    if oc >= 2:
                        sy.wait_ge(sem["gg"], 8 * (oc - 1))
                    sy.dma_start(out=wg_t[oc % 2][:], in_=ago[SLOT_G + oc]
                                 ).then_inc(sem["wl%d" % (oc % 2)], 16)
                    sy.dma_start(out=wu_t[oc % 2][:], in_=ago[SLOT_U + oc]
                                 ).then_inc(sem["wl%d" % (oc % 2)], 16)
                sy.nop().then_inc(sem["ph"], 1)

            @block.gpsimd
            def _(g):
                g.wait_ge(sem["ph"], 20)
                for oc in range(OCI):
                    g.wait_ge(sem["hh"], 8 * (oc + 1))
                    g.dma_start(out=h_dram[oc * 128:(oc + 1) * 128, :], in_=h_sb[oc % 2][:]
                                ).then_inc(sem["hs%d" % (oc % 2)], 16)
                g.nop().then_inc(sem["ph"], 1)

            @block.tensor
            def _(t):
                t.wait_ge(sem["ph"], 20)
                t.wait_ge(sem["h5"], 256)
                for oc in range(OCI):
                    t.wait_ge(sem["wl%d" % (oc % 2)], 32 * (oc // 2 + 1))
                    for rc in range(8):
                        m = oc * 8 + rc
                        if m >= 3:
                            t.wait_ge(sem["sil"], m - 2)
                            t.wait_ge(sem["hh"], m - 2)
                        for hc in range(HC):
                            t.matmul(ps_g[m % 3][:], wg_t[oc % 2][:, hc * 128:(hc + 1) * 128],
                                     x1T_sb[:, hc * 4096 + rc * 512: hc * 4096 + (rc + 1) * 512],
                                     start=(hc == 0), stop=(hc == 15))
                        for hc in range(HC):
                            ins = t.matmul(ps_u[m % 3][:], wu_t[oc % 2][:, hc * 128:(hc + 1) * 128],
                                           x1T_sb[:, hc * 4096 + rc * 512: hc * 4096 + (rc + 1) * 512],
                                           start=(hc == 0), stop=(hc == 15))
                        ins.then_inc(sem["gg"], 1)
                t.nop().then_inc(sem["ph"], 1)

            @block.scalar
            def _(s):
                s.wait_ge(sem["ph"], 20)
                for oc in range(OCI):
                    for rc in range(8):
                        m = oc * 8 + rc
                        s.wait_ge(sem["gg"], m + 1)
                        if m >= 2:
                            s.wait_ge(sem["hh"], m - 1)
                        s.activation(sg[rc % 2][:], ps_g[m % 3][:], AF.Silu
                                     ).then_inc(sem["sil"], 1)
                s.nop().then_inc(sem["ph"], 1)

            @block.vector
            def _(v):
                v.wait_ge(sem["ph"], 20)
                for oc in range(OCI):
                    for rc in range(8):
                        m = oc * 8 + rc
                        v.wait_ge(sem["sil"], m + 1)
                        if rc == 0 and oc >= 2:
                            v.wait_ge(sem["hs%d" % (oc % 2)], 16 * (oc // 2))
                        v.scalar_tensor_tensor(h_sb[oc % 2][:, rc * 512:(rc + 1) * 512],
                                               ps_u[m % 3][:], 1.0, sg[rc % 2][:],
                                               ALU.mult, ALU.mult).then_inc(sem["hh"], 1)
                v.nop().then_inc(sem["ph"], 1)

        with contextlib.ExitStack() as st5:
            E5 = st5.enter_context
            h5_sb = E5(nc.sbuf_tensor("h5_sb", [128, OCI * 512], BF16))
            wd_t = [E5(nc.sbuf_tensor(f"wd{i}", [128, INTER], BF16)) for i in range(2)]
            dn_sb = E5(nc.sbuf_tensor("dn_sb", [128, 16 * 512], BF16))
            x1r_sb = [E5(nc.sbuf_tensor(f"x1r{i}", [128, 2048], F32)) for i in range(4)]
            t2_sb = E5(nc.sbuf_tensor("t2_sb", [128, 4 * 2048], F32))
            rw5 = [E5(nc.sbuf_tensor(f"rw5{i}", [128, 1], F32)) for i in range(2)]
            sq5 = [E5(nc.sbuf_tensor(f"sq5{i}", [128, 1], F32)) for i in range(2)]
            mn5 = [E5(nc.sbuf_tensor(f"mn5{i}", [128, 1], F32)) for i in range(2)]
            e25 = [E5(nc.sbuf_tensor(f"e25{i}", [128, 1], F32)) for i in range(2)]
            mq5 = [E5(nc.sbuf_tensor(f"mq5{i}", [128, 1], F32)) for i in range(2)]
            vr5 = [E5(nc.sbuf_tensor(f"vr5{i}", [128, 1], F32)) for i in range(2)]
            lv5 = [E5(nc.sbuf_tensor(f"lv5{i}", [128, 1], F32)) for i in range(2)]
            rd5 = [E5(nc.sbuf_tensor(f"rd5{i}", [128, 1], F32)) for i in range(2)]
            ps_d = [E5(nc.psum_tensor(f"ps_d{i}", [128, 512], F32)) for i in range(3)]
            ps_r5 = [E5(nc.psum_tensor(f"ps_r5{i}", [128, 512], F32)) for i in range(4)]

            h_3 = h_dram.rearrange("(t p) r -> p t r", p=128)
            h5_3 = h5_sb[:].rearrange("p (t c) -> p t c", c=512)
            tchunks = [(0, 4), (4, 8), (8, 12), (12, 16), (16, 20), (20, 24),
                       (24, 28), (28, 32), (32, 36), (36, 40), (40, 43)]

            @block.sync
            def _(sy):
                sy.wait_ge(sem["ph"], 25)
                sy.wait_ge(sem["hs0"], 16 * 22)
                sy.wait_ge(sem["hs1"], 16 * 21)
                sy.wait_ge(sem["xf0"], 16 * 16)
                sy.wait_ge(sem["xf1"], 16 * 16)
                for rc in range(8):
                    r0 = rc * 512
                    i_n, s0 = rc // 4, (rc % 4) * 512
                    if rc >= 1:
                        sy.wait_ge(sem["dnm"], 16 * rc)
                    for (t0, t1) in tchunks:
                        sy.dma_start(out=h5_3[:, t0:t1, :], in_=h_3[:, t0:t1, r0:r0 + 512]
                                     ).then_inc(sem["h5"], 16)
                    for d2c in range(16):
                        dm = rc * 16 + d2c
                        if dm >= 2:
                            sy.wait_ge(sem["dnm"], dm - 1)
                        for kk in range(3):
                            sy.dma_start(
                                out=wd_t[d2c % 2][:, kk * 2048:kk * 2048 + DW[kk]],
                                in_=ago[SLOT_D + 3 * d2c + kk][:, :DW[kk]],
                            ).then_inc(sem["wd%d" % (d2c % 2)], 16)
                    for j in range(4):
                        if rc >= 1:
                            sy.wait_ge(sem["t15"], (rc - 1) * 16 + 13 + j)
                        sy.dma_start(out=x1r_sb[j][:],
                                     in_=x1R_dram[r0 + j * 128: r0 + (j + 1) * 128, :]
                                     ).then_inc(sem["xr0"], 16)
                sy.nop().then_inc(sem["ph"], 1)

            @block.gpsimd
            def _(g):
                g.wait_ge(sem["ph"], 25)
                for rc in range(8):
                    i_n, s0 = rc // 4, (rc % 4) * 512
                    for j in range(4):
                        gj5 = rc * 4 + j
                        g.wait_ge(sem["yrd"], gj5 + 1)
                        g.dma_start(out=y[i_n, s0 + j * 128: s0 + (j + 1) * 128, :],
                                    in_=t2_sb[:, j * 2048:(j + 1) * 2048]
                                    ).then_inc(sem["ou0"], 16)
                g.wait_ge(sem["ou0"], 64 * 8)
                g.nop().then_inc(sem["ph"], 1)

            @block.tensor
            def _(t):
                t.wait_ge(sem["ph"], 25)
                for rc in range(8):
                    t.wait_ge(sem["h5"], 256 + 176 * (rc + 1))
                    for d2c in range(16):
                        dm = rc * 16 + d2c
                        t.wait_ge(sem["wd%d" % (d2c % 2)], 48 * (dm // 2 + 1))
                        if dm >= 3:
                            t.wait_ge(sem["dne"], dm - 2)
                        for oc in range(OCI):
                            ins = t.matmul(ps_d[dm % 3][:], wd_t[d2c % 2][:, oc * 128:(oc + 1) * 128],
                                           h5_sb[:, oc * 512:(oc + 1) * 512],
                                           start=(oc == 0), stop=(oc == OCI - 1))
                        ins.then_inc(sem["dnm"], 1)
                    for d2c in range(16):
                        dm = rc * 16 + d2c
                        t.wait_ge(sem["dne"], dm + 1)
                        for j in range(4):
                            gid = rc * 16 + (d2c // 4) * 4 + j
                            if d2c % 4 == 0 and gid >= 4:
                                t.wait_ge(sem["t15"], gid - 3)
                            ins = t.matmul(ps_r5[j][:, (d2c % 4) * 128:(d2c % 4 + 1) * 128],
                                           dn_sb[:, d2c * 512 + j * 128: d2c * 512 + (j + 1) * 128],
                                           identB[:], start=(d2c % 4 == 0), stop=(d2c % 4 == 3))
                        ins.then_inc(sem["trm5"], 1)
                t.nop().then_inc(sem["ph"], 1)

            @block.scalar
            def _(s):
                s.wait_ge(sem["ph"], 25)
                for rc in range(8):
                    for d2c in range(16):
                        dm = rc * 16 + d2c
                        s.wait_ge(sem["dnm"], dm + 1)
                        if rc >= 1 and d2c == 0:
                            s.wait_ge(sem["trm5"], 16 * rc)
                        s.activation(dn_sb[:, d2c * 512:(d2c + 1) * 512], ps_d[dm % 3][:],
                                     AF.Copy).then_inc(sem["dne"], 1)
                    for j in range(4):
                        gj5 = rc * 4 + j
                        s.wait_ge(sem["t15"], rc * 16 + 12 + j + 1)
                        s.activation(dn_sb[:, j * 2048:(j + 1) * 2048],
                                     t2_sb[:, j * 2048:(j + 1) * 2048], AF.Square,
                                     accum_out=sq5[j % 2][:]).then_inc(sem["m2"], 1)
                        s.wait_ge(sem["m3"], gj5 + 1)
                        s.activation(lv5[j % 2][:], vr5[j % 2][:], AF.Ln
                                     ).then_inc(sem["m4"], 1)
                        s.wait_ge(sem["m4"], gj5 + 1)
                        s.activation(rd5[j % 2][:], lv5[j % 2][:], AF.Exp, scale=-0.5
                                     ).then_inc(sem["m5"], 1)
                s.nop().then_inc(sem["ph"], 1)

            @block.vector
            def _(v):
                v.wait_ge(sem["ph"], 25)
                for rc in range(8):
                    for d2g in range(4):
                        for j in range(4):
                            gid = rc * 16 + d2g * 4 + j
                            v.wait_ge(sem["trm5"], rc * 16 + 4 * d2g + 4)
                            v.wait_ge(sem["xr0"], 64 * (rc + 1))
                            if d2g == 0 and rc >= 1:
                                v.wait_ge(sem["ou0"], 64 * rc)
                            v.scalar_tensor_tensor(
                                t2_sb[:, j * 2048 + d2g * 512: j * 2048 + (d2g + 1) * 512],
                                ps_r5[j][:], 1.0,
                                x1r_sb[j][:, d2g * 512:(d2g + 1) * 512],
                                ALU.mult, ALU.add).then_inc(sem["t15"], 1)
                    for j in range(4):
                        gj5 = rc * 4 + j
                        tsl = t2_sb[:, j * 2048:(j + 1) * 2048]
                        v.wait_ge(sem["t15"], rc * 16 + 12 + j + 1)
                        n0 = 5 * gj5
                        v.reduce_sum(rw5[j % 2][:], tsl, axis=AX.X).then_inc(sem["m1"], 1)
                        v.wait_ge(sem["m2"], gj5 + 1)
                        v.wait_ge(sem["m1"], n0 + 1)
                        v.tensor_scalar(mn5[j % 2][:], rw5[j % 2][:], INVH, None,
                                        ALU.mult).then_inc(sem["m1"], 1)
                        v.tensor_scalar(e25[j % 2][:], sq5[j % 2][:], INVH, None,
                                        ALU.mult).then_inc(sem["m1"], 1)
                        v.wait_ge(sem["m1"], n0 + 3)
                        v.tensor_tensor(mq5[j % 2][:], mn5[j % 2][:], mn5[j % 2][:],
                                        ALU.mult).then_inc(sem["m1"], 1)
                        v.wait_ge(sem["m1"], n0 + 4)
                        v.tensor_tensor(vr5[j % 2][:], e25[j % 2][:], mq5[j % 2][:],
                                        ALU.subtract).then_inc(sem["m1"], 1)
                        v.wait_ge(sem["m1"], n0 + 5)
                        v.tensor_scalar(vr5[j % 2][:], vr5[j % 2][:], EPS, None,
                                        ALU.add).then_inc(sem["m3"], 1)
                        v.wait_ge(sem["m5"], gj5 + 1)
                        v.tensor_scalar(tsl, tsl, mn5[j % 2][:], rd5[j % 2][:],
                                        ALU.subtract, ALU.mult).then_inc(sem["yrd"], 1)
                v.nop().then_inc(sem["ph"], 1)
    return nc


def _gather_blocks(gi, ui, di):
    blocks = np.zeros((8 * NJL, 128, H), dtype=np.int32)
    for oc in range(OCI):
        blk = gi[oc * 128:(oc + 1) * 128, :].T.reshape(16, 128, 128)
        blocks[SLOT_G + oc] = blk.transpose(1, 0, 2).reshape(128, H)
        blk = ui[oc * 128:(oc + 1) * 128, :].T.reshape(16, 128, 128)
        blocks[SLOT_U + oc] = blk.transpose(1, 0, 2).reshape(128, H)
    for dc in range(16):
        wd = di[dc * 128:(dc + 1) * 128, :].T.reshape(OCI, 128, 128)
        wd = wd.transpose(1, 0, 2).reshape(128, INTER)
        for kk in range(3):
            blocks[SLOT_D + 3 * dc + kk, :, :DW[kk]] = \
                wd[:, kk * 2048:kk * 2048 + DW[kk]]
    return np.ascontiguousarray(blocks)


def kernel(**inputs):
    _ = inputs["L0_attn_key"], inputs["L0_attn_value"]
    x = np.ascontiguousarray(np.asarray(inputs["L0_attn_query"], dtype=np.float32))
    W = np.asarray(inputs["L0_attn_Wqkv"], dtype=np.float32)
    Wp = np.asarray(inputs["L0_attn_Wproj"], dtype=np.float32)
    bp = np.asarray(inputs["L0_attn_bproj"], dtype=np.float32)
    lut = np.asarray(inputs["L0_ffn_lut"], dtype=np.float32)
    gi = np.ascontiguousarray(np.asarray(inputs["L0_ffn_gate_idx"], dtype=np.int32))
    ui = np.ascontiguousarray(np.asarray(inputs["L0_ffn_up_idx"], dtype=np.int32))
    di = np.ascontiguousarray(np.asarray(inputs["L0_ffn_down_idx"], dtype=np.int32))

    xT = np.ascontiguousarray(x.transpose(2, 0, 1))
    wprojT = np.ascontiguousarray(Wp.T)
    bias2dT = np.ascontiguousarray(bp.reshape(16, 128).T)
    lut2d = np.ascontiguousarray(lut.reshape(1, 16))
    ident = np.eye(128, dtype=np.float32)
    onesc = np.ones((128, 1), dtype=np.float32)
    onesr = np.ones((1, 128), dtype=np.float32)
    gblocks = _gather_blocks(gi, ui, di)

    nc = build_nc()
    in_maps = []
    for c in range(8):
        n0, n1 = 2 * c, 2 * c + 1
        wq = np.concatenate([
            W[n0 * 384: n0 * 384 + 128],
            W[n0 * 384 + 128: n0 * 384 + 256],
            W[n1 * 384: n1 * 384 + 128],
            W[n1 * 384 + 128: n1 * 384 + 256],
            W[n0 * 384 + 256: n0 * 384 + 384],
            W[n1 * 384 + 256: n1 * 384 + 384],
        ], axis=0)
        in_maps.append({
            "xT_in": xT, "x_res": np.ascontiguousarray(x[n0:n1 + 1]),
            "wqkvT": np.ascontiguousarray(wq.T), "wprojT": wprojT,
            "bias2dT": bias2dT, "lut2d": lut2d,
            "idx_sh": gblocks[c * NJL:(c + 1) * NJL],
            "ident_in": ident, "onesc_in": onesc, "onesr_in": onesr,
        })
    res = run_bass_kernel_spmd(nc, in_maps, core_ids=list(range(8)))
    return np.concatenate([np.asarray(r["y"]) for r in res.results], axis=0)


# version 19 (final)
# speedup vs baseline: 1.0661x; 1.0205x over previous
import contextlib
import numpy as np
import concourse.bass as bass
import concourse.mybir as mybir
from concourse.bass_utils import run_bass_kernel_spmd

F32 = mybir.dt.float32
BF16 = mybir.dt.bfloat16
I32 = mybir.dt.int32
ALU = mybir.AluOpType
AF = mybir.ActivationFunctionType
AX = mybir.AxisListType

B = 16
S = 2048
H = 2048
HN = 128
HC = 16
INTER = 5504
OCI = 43
NBLK = 64
SCALE = float(1.0 / np.sqrt(HN))
EPS = 1e-5
INVH = 1.0 / 2048.0
NJL = 17
SLOT_G = 0
SLOT_U = OCI
SLOT_D = 2 * OCI
DW = (2048, 2048, INTER - 4096)


def build_nc(debug_dump=False):
    nc = bass.Bass("TRN2", target_bir_lowering=False)

    xT_in = nc.dram_tensor("xT_in", [H, B, S], F32, kind="ExternalInput").ap()
    x_res = nc.dram_tensor("x_res", [2, S, H], F32, kind="ExternalInput").ap()
    wqkvT = nc.dram_tensor("wqkvT", [H, 768], F32, kind="ExternalInput").ap()
    wprojT = nc.dram_tensor("wprojT", [H, H], F32, kind="ExternalInput").ap()
    bias2dT = nc.dram_tensor("bias2dT", [128, 16], F32, kind="ExternalInput").ap()
    lut2d = nc.dram_tensor("lut2d", [1, 16], F32, kind="ExternalInput").ap()
    idx_sh = nc.dram_tensor("idx_sh", [NJL, 128, H], I32, kind="ExternalInput").ap()
    ident_in = nc.dram_tensor("ident_in", [128, 128], F32, kind="ExternalInput").ap()
    onesc_in = nc.dram_tensor("onesc_in", [128, 1], F32, kind="ExternalInput").ap()
    onesr_in = nc.dram_tensor("onesr_in", [1, 128], F32, kind="ExternalInput").ap()
    y = nc.dram_tensor("y", [2, S, H], F32, kind="ExternalOutput").ap()

    KD = "ExternalOutput" if debug_dump else "Internal"
    qk_dram = nc.dram_tensor("qk_dram", [4, B, 128, S], BF16, kind=KD).ap()
    v_dram = nc.dram_tensor("v_dram", [2, B, S, 128], BF16, kind=KD).ap()
    ctx_dram = nc.dram_tensor("ctx_dram", [2, B, 128, S], BF16, kind=KD).ap()
    x1T_dram = nc.dram_tensor("x1T_dram", [HC, 128, 4096], BF16, kind=KD).ap()
    x1R_dram = nc.dram_tensor("x1R_dram", [4096, H], F32, kind=KD).ap()
    h_dram = nc.dram_tensor("h_dram", [INTER, 4096], BF16, kind=KD).ap()
    ag_in = nc.dram_tensor("ag_in", [NJL, 128, H], BF16, kind="Internal")
    ag_out = nc.dram_tensor("ag_out", [8 * NJL, 128, H], BF16, kind="Internal",
                            addr_space="Shared")
    ago = ag_out.ap()

    with contextlib.ExitStack() as st0:
        E = st0.enter_context

        identB = E(nc.sbuf_tensor("identB", [128, 128], BF16))
        identF = E(nc.sbuf_tensor("identF", [128, 128], F32))
        onesc = E(nc.sbuf_tensor("onesc", [128, 1], BF16))
        onesr = E(nc.sbuf_tensor("onesr", [1, 128], F32))
        lut_sb = E(nc.sbuf_tensor("lut_sb", [1, 16], F32))
        lutB = E(nc.sbuf_tensor("lutB", [128, 16], F32))
        bias_sb = E(nc.sbuf_tensor("bias_sb", [128, 16], F32))

        names = ["ph", "c0", "c1", "xld", "wql", "qm", "qe", "qs0", "qs1", "vm",
                 "ve", "vs0", "vs1", "gl", "gc", "gd", "gs", "cc",
                 "pa", "pb", "rsm", "gg", "uu",
                 "qk0", "qk1", "ss", "exp", "pv", "rc", "bcm", "bce", "ctx", "cs0", "cs1",
                 "ctxl", "prm", "prc", "pre", "t1", "xrl",
                 "l1", "l2", "l3", "l4", "l5", "x1b", "x1fr", "xf0", "xf1",
                 "xt1", "xte", "xts",
                 "wl0", "wl1", "gum", "sil", "hh", "hs0", "hs1",
                 "h5", "wd0", "wd1", "dnm", "dne", "trm5", "t15",
                 "m1", "m2", "m3", "m4", "m5", "yrd", "ou0", "xr0", "misc"]
        sem = {n: E(nc.semaphore("s_" + n)) for n in names}
        block = E(nc.Block())

        @block.gpsimd
        def _(g):
            g.dma_start(out=identB[:], in_=ident_in).then_inc(sem["c0"], 16)
            g.dma_start(out=onesc[:], in_=onesc_in).then_inc(sem["c0"], 16)
            g.wait_ge(sem["c0"], 32)
            g.nop().then_inc(sem["ph"], 1)

        @block.sync
        def _(sy):
            sy.dma_start(out=identF[:], in_=ident_in).then_inc(sem["c1"], 16)
            sy.dma_start(out=onesr[:], in_=onesr_in).then_inc(sem["c1"], 16)
            sy.dma_start(out=lut_sb[:], in_=lut2d).then_inc(sem["c1"], 16)
            sy.dma_start(out=bias_sb[:], in_=bias2dT).then_inc(sem["c1"], 16)
            sy.wait_ge(sem["c1"], 64)
            sy.nop().then_inc(sem["ph"], 1)

        with nc.psum_tensor("ps_lut", [128, 16], F32) as ps_lut:
            @block.tensor
            def _(t):
                t.wait_ge(sem["c1"], 64)
                t.matmul(ps_lut[:], onesr[:], lut_sb[:], start=True, stop=True
                         ).then_inc(sem["misc"], 1)
                t.nop().then_inc(sem["ph"], 1)

            @block.scalar
            def _(s):
                s.wait_ge(sem["misc"], 1)
                s.activation(lutB[:], ps_lut[:], AF.Copy).then_inc(sem["ph"], 1)

        @block.vector
        def _(v):
            v.nop().then_inc(sem["ph"], 1)

        st12 = contextlib.ExitStack()
        E12 = st12.enter_context
        QT = [E12(nc.sbuf_tensor(f"QT{i}", [128, S], BF16)) for i in range(2)]
        KT = [E12(nc.sbuf_tensor(f"KT{i}", [128, S], BF16)) for i in range(2)]
        V16 = [E12(nc.sbuf_tensor(f"V16{i}", [128, S], BF16)) for i in range(2)]
        PT = [E12(nc.sbuf_tensor(f"PT{i}", [128, 512], BF16)) for i in range(20)]
        recip = [E12(nc.sbuf_tensor(f"recip{i}", [1, 512], F32)) for i in range(2)]
        bc_sb = [E12(nc.sbuf_tensor(f"bc{i}", [128, 512], F32)) for i in range(2)]
        ctxo = [E12(nc.sbuf_tensor(f"ctxo{i}", [128, 512], BF16)) for i in range(2)]

        with contextlib.ExitStack() as st1:
            E1 = st1.enter_context
            wqkvT_bf = E1(nc.sbuf_tensor("wqkvT_bf", [128, HC * 768], BF16))
            XT = [E1(nc.sbuf_tensor(f"XT{i}", [128, HC * 512], BF16)) for i in range(3)]
            qk_buf = [E1(nc.sbuf_tensor(f"qkb{i}", [128, 512], BF16)) for i in range(2)]
            v_buf = [E1(nc.sbuf_tensor(f"vb{i}", [128, 256], BF16)) for i in range(2)]
            idx_i = [E1(nc.sbuf_tensor(f"idx_i{i}", [128, H], I32)) for i in range(2)]
            idxb = [E1(nc.sbuf_tensor(f"idxb{i}", [128, H], BF16)) for i in range(2)]
            wacc = [E1(nc.sbuf_tensor(f"wacc{i}", [128, H], BF16)) for i in range(2)]
            wtmp = E1(nc.sbuf_tensor("wtmp", [128, H], BF16))
            ps_qk = [E1(nc.psum_tensor(f"ps_qk{i}", [128, 512], F32)) for i in range(3)]
            ps_v = [E1(nc.psum_tensor(f"ps_v{i}", [128, 256], F32)) for i in range(3)]

            @block.sync
            def _(sy):
                sy.wait_ge(sem["ph"], 5)
                for j in range(NJL):
                    if j >= 2:
                        sy.wait_ge(sem["gc"], j - 1)
                    sy.dma_start(out=idx_i[j % 2][:], in_=idx_sh[j]
                                 ).then_inc(sem["gl"], 16)
                sy.nop().then_inc(sem["ph"], 1)

            @block.gpsimd
            def _(g):
                g.wait_ge(sem["ph"], 5)
                wq3 = wqkvT.rearrange("(hc p) o -> p hc o", p=128)
                wb3 = wqkvT_bf[:].rearrange("p (hc o) -> p hc o", o=768)
                for q in range(4):
                    g.dma_start(out=wb3[:, q * 4:(q + 1) * 4, :],
                                in_=wq3[:, q * 4:(q + 1) * 4, :]).then_inc(sem["wql"], 16)
                for blk in range(NBLK):
                    b, sb = blk // 4, blk % 4
                    if blk >= 3:
                        g.wait_ge(sem["qm"], 4 * (blk - 2))
                        g.wait_ge(sem["vm"], 4 * (blk - 2))
                    g.dma_start(
                        out=XT[blk % 3][:].rearrange("p (hc s) -> p hc s", s=512),
                        in_=xT_in[:, b, sb * 512:(sb + 1) * 512].rearrange(
                            "(hc p) s -> p hc s", p=128),
                    ).then_inc(sem["xld"], 16)
                    if blk % 2 == 0 and blk // 2 < NJL:
                        j = blk // 2
                        g.wait_ge(sem["gl"], 16 * (j + 1))
                        if j >= 2:
                            g.wait_ge(sem["gd"], j - 1)
                        g.tensor_copy(idxb[j % 2][:], idx_i[j % 2][:]
                                      ).then_inc(sem["gc"], 1)
                        if j >= 1:
                            g.wait_ge(sem["gd"], j)
                            g.dma_start(out=ag_in.ap()[j - 1], in_=wacc[(j - 1) % 2][:]
                                        ).then_inc(sem["gs"], 16)
                g.wait_ge(sem["gd"], NJL)
                g.dma_start(out=ag_in.ap()[NJL - 1], in_=wacc[(NJL - 1) % 2][:]
                            ).then_inc(sem["gs"], 16)
                g.wait_ge(sem["gs"], 16 * NJL)
                g.collective_compute(
                    "AllGather", ALU.bypass, replica_groups=[list(range(8))],
                    ins=[ag_in.ap().opt()], outs=[ago.opt()],
                ).then_inc(sem["cc"])
                g.nop().then_inc(sem["ph"], 1)

            @block.scalar
            def _(s):
                s.wait_ge(sem["ph"], 5)
                for blk in range(NBLK):
                    b, sb = blk // 4, blk % 4
                    srange = slice(sb * 512, (sb + 1) * 512)
                    for oc in range(4):
                        k = blk * 4 + oc
                        s.wait_ge(sem["qm"], k + 1)
                        if k >= 2:
                            s.wait_ge(sem["qs%d" % (k % 2)], 16 * (k // 2))
                        s.activation(qk_buf[oc % 2][:], ps_qk[k % 3][:], AF.Copy
                                     ).then_inc(sem["qe"], 1)
                        s.wait_ge(sem["qe"], k + 1)
                        s.dma_start(out=qk_dram[oc, b, :, srange], in_=qk_buf[oc % 2][:]
                                    ).then_inc(sem["qs%d" % (oc % 2)], 16)
                    for ss in range(4):
                        k = blk * 4 + ss
                        s.wait_ge(sem["vm"], k + 1)
                        if k >= 2:
                            s.wait_ge(sem["vs%d" % (k % 2)], 32 * (k // 2))
                        s.activation(v_buf[ss % 2][:], ps_v[k % 3][:], AF.Copy
                                     ).then_inc(sem["ve"], 1)
                        s.wait_ge(sem["ve"], k + 1)
                        sr2 = slice(sb * 512 + ss * 128, sb * 512 + (ss + 1) * 128)
                        s.dma_start(out=v_dram[0, b, sr2, :], in_=v_buf[ss % 2][:, 0:128]
                                    ).then_inc(sem["vs%d" % (ss % 2)], 16)
                        s.dma_start(out=v_dram[1, b, sr2, :], in_=v_buf[ss % 2][:, 128:256]
                                    ).then_inc(sem["vs%d" % (ss % 2)], 16)
                s.nop().then_inc(sem["ph"], 1)

            @block.tensor
            def _(t):
                t.wait_ge(sem["ph"], 5)
                t.wait_ge(sem["wql"], 64)
                for blk in range(NBLK):
                    t.wait_ge(sem["xld"], 16 * (blk + 1))
                    for oc in range(4):
                        k = blk * 4 + oc
                        if k >= 3:
                            t.wait_ge(sem["qe"], k - 2)
                        for hc in range(HC):
                            ins = t.matmul(ps_qk[k % 3][:],
                                           wqkvT_bf[:, hc * 768 + oc * 128: hc * 768 + (oc + 1) * 128],
                                           XT[blk % 3][:, hc * 512:(hc + 1) * 512],
                                           start=(hc == 0), stop=(hc == 15))
                        ins.then_inc(sem["qm"], 1)
                    for ss in range(4):
                        k = blk * 4 + ss
                        if k >= 3:
                            t.wait_ge(sem["ve"], k - 2)
                        for hc in range(HC):
                            ins = t.matmul(ps_v[k % 3][:],
                                           XT[blk % 3][:, hc * 512 + ss * 128: hc * 512 + (ss + 1) * 128],
                                           wqkvT_bf[:, hc * 768 + 512: hc * 768 + 768],
                                           start=(hc == 0), stop=(hc == 15))
                        ins.then_inc(sem["vm"], 1)
                t.nop().then_inc(sem["ph"], 1)

            @block.vector
            def _(v):
                v.wait_ge(sem["ph"], 5)
                for j in range(NJL):
                    w = H
                    v.wait_ge(sem["gc"], j + 1)
                    if j >= 2:
                        v.wait_ge(sem["gs"], 16 * (j - 1))
                    v.tensor_scalar(wacc[j % 2][:, :w], idxb[j % 2][:, :w], 0.0,
                                    lutB[:, 0:1], ALU.is_equal, ALU.mult)
                    for tt in range(1, 16):
                        v.tensor_scalar(wtmp[:, :w], idxb[j % 2][:, :w], float(tt),
                                        lutB[:, tt:tt + 1], ALU.is_equal, ALU.mult)
                        ins = v.tensor_tensor(wacc[j % 2][:, :w], wacc[j % 2][:, :w],
                                              wtmp[:, :w], ALU.add)
                    ins.then_inc(sem["gd"], 1)
                v.nop().then_inc(sem["ph"], 1)

        with contextlib.ExitStack() as st2:
            E2 = st2.enter_context
            ps_s = [E2(nc.psum_tensor(f"ps_s{i}", [128, 512], F32)) for i in range(4)]
            ps_c = [E2(nc.psum_tensor(f"ps_c{i}", [128, 512], F32)) for i in range(2)]
            ps_sum = [E2(nc.psum_tensor(f"ps_sm{i}", [1, 512], F32)) for i in range(1)]
            ps_bc = [E2(nc.psum_tensor(f"ps_bc{i}", [128, 512], F32)) for i in range(1)]

            @block.sync
            def _(sy):
                for g in range(32):
                    i, b = g // 16, g % 16
                    if g >= 2:
                        sy.wait_ge(sem["pv"], 64 * (g - 1))
                    w0 = 32 * (4 * b + 4)
                    sy.wait_ge(sem["qs0"], w0)
                    sy.wait_ge(sem["qs1"], w0)
                    sy.wait_ge(sem["vs0"], 2 * w0)
                    sy.wait_ge(sem["vs1"], 2 * w0)
                    sy.dma_start(out=QT[g % 2][:], in_=qk_dram[2 * i, b]
                                 ).then_inc(sem["qk%d" % (g % 2)], 16)
                    sy.dma_start(out=KT[g % 2][:], in_=qk_dram[2 * i + 1, b]
                                 ).then_inc(sem["qk%d" % (g % 2)], 16)
                    v3 = v_dram[i, b].rearrange("(t p) d -> p t d", p=128)
                    o3 = V16[g % 2][:].rearrange("p (t d) -> p t d", d=128)
                    for q in range(4):
                        sy.dma_start(out=o3[:, q * 4:(q + 1) * 4, :],
                                     in_=v3[:, q * 4:(q + 1) * 4, :]
                                     ).then_inc(sem["qk%d" % (g % 2)], 16)
                sy.nop().then_inc(sem["ph"], 1)

            @block.gpsimd
            def _(g):
                g.wait_ge(sem["ph"], 10)

                def spill(k2):
                    i, b, qc = k2 // 64, (k2 // 4) % 16, k2 % 4
                    g.wait_ge(sem["ctx"], k2 + 1)
                    g.dma_start(out=ctx_dram[i, b, :, qc * 512:(qc + 1) * 512],
                                in_=ctxo[qc % 2][:]).then_inc(sem["cs%d" % (k2 % 2)], 16)

                for k in range(128):
                    spill(k)
                g.wait_ge(sem["cs0"], 16 * 64)
                g.wait_ge(sem["cs1"], 16 * 64)
                g.nop().then_inc(sem["ph"], 1)

            @block.tensor
            def _(t):
                t.wait_ge(sem["ph"], 10)

                def rowsum(k2):
                    t.wait_ge(sem["exp"], 16 * (k2 + 1))
                    t.wait_ge(sem["rc"], k2)
                    e0b = k2 * 16
                    for i in range(16):
                        ins = t.matmul(ps_sum[0][:], onesc[:], PT[(e0b + i) % 20][:],
                                       start=(i == 0), stop=(i == 15))
                    ins.then_inc(sem["rsm"], 1)

                def bcast(k2):
                    t.wait_ge(sem["rc"], k2 + 1)
                    if k2 >= 1:
                        t.wait_ge(sem["bce"], k2)
                    t.matmul(ps_bc[0][:], onesr[:], recip[k2 % 2][:],
                             start=True, stop=True).then_inc(sem["bcm"], 1)

                def mmS(T):
                    g, qc, kt = T // 64, (T % 64) // 16, T % 16
                    if T % 64 == 0:
                        t.wait_ge(sem["qk%d" % (g % 2)], 96 * (g // 2 + 1))
                    if T >= 4:
                        t.wait_ge(sem["exp"], T - 3)
                    t.matmul(ps_s[T % 4][:],
                             KT[g % 2][:, kt * 128:(kt + 1) * 128],
                             QT[g % 2][:, qc * 512:(qc + 1) * 512], start=True, stop=True
                             ).then_inc(sem["ss"], 1)

                def mmPV(T):
                    g, kt, k = T // 64, T % 16, T // 16
                    t.wait_ge(sem["exp"], T + 1)
                    if kt == 0 and k >= 2:
                        t.wait_ge(sem["ctx"], k - 1)
                    t.matmul(ps_c[k % 2][:],
                             V16[g % 2][:, kt * 128:(kt + 1) * 128],
                             PT[T % 20][:], start=(kt == 0), stop=(kt == 15)
                             ).then_inc(sem["pv"], 1)

                for T in range(2048 + 3):
                    if T < 2048:
                        mmS(T)
                    Tp = T - 3
                    if Tp >= 0:
                        mmPV(Tp)
                        kt, k = Tp % 16, Tp // 16
                        if kt == 1 and k >= 1:
                            rowsum(k - 1)
                        if kt == 4 and k >= 1:
                            bcast(k - 1)
                rowsum(127)
                bcast(127)
                t.nop().then_inc(sem["ph"], 1)

            @block.scalar
            def _(s):
                s.wait_ge(sem["ph"], 10)

                def bce_evac(k2):
                    s.wait_ge(sem["bcm"], k2 + 1)
                    if k2 >= 2:
                        s.wait_ge(sem["ctx"], k2 - 1)
                    s.activation(bc_sb[k2 % 2][:], ps_bc[0][:], AF.Copy
                                 ).then_inc(sem["bce"], 1)

                for k in range(128):
                    e0 = k * 16
                    for kt in range(16):
                        T = e0 + kt
                        s.wait_ge(sem["ss"], T + 1)
                        if T >= 20:
                            s.wait_ge(sem["pv"], T - 19)
                        s.activation(PT[T % 20][:], ps_s[T % 4][:],
                                     AF.Exp, scale=SCALE).then_inc(sem["exp"], 1)
                        if kt == 8 and k >= 1:
                            bce_evac(k - 1)
                bce_evac(127)
                s.nop().then_inc(sem["ph"], 1)

            @block.vector
            def _(v):
                v.wait_ge(sem["ph"], 10)

                def vrecip(k2):
                    v.wait_ge(sem["rsm"], k2 + 1)
                    if k2 >= 2:
                        v.wait_ge(sem["bcm"], k2 - 1)
                    v.reciprocal(recip[k2 % 2][:], ps_sum[0][:]).then_inc(sem["rc"], 1)

                def vctx(k2):
                    v.wait_ge(sem["bce"], k2 + 1)
                    v.wait_ge(sem["pv"], 16 * (k2 + 1))
                    if k2 >= 2:
                        v.wait_ge(sem["cs%d" % (k2 % 2)], 16 * (k2 // 2))
                    v.scalar_tensor_tensor(ctxo[k2 % 2][:], ps_c[k2 % 2][:], 1.0,
                                           bc_sb[k2 % 2][:], ALU.mult, ALU.mult
                                           ).then_inc(sem["ctx"], 1)

                for k in range(1, 128):
                    vrecip(k - 1)
                    vctx(k - 1)
                vrecip(127)
                vctx(127)
                v.nop().then_inc(sem["ph"], 1)
        st12.close()

        with contextlib.ExitStack() as st3:
            E3 = st3.enter_context
            wprojT_bf = E3(nc.sbuf_tensor("wprojT_bf", [128, HC * H], BF16))
            ctx_rhs = [E3(nc.sbuf_tensor(f"ctx_rhs{i}", [128, HC * 512], BF16))
                       for i in range(2)]
            xres_sb = E3(nc.sbuf_tensor("xres_sb", [128, 4 * 2048], F32))
            t1_sb = E3(nc.sbuf_tensor("t1_sb", [128, 4 * 2048], F32))
            pr_sb = [E3(nc.sbuf_tensor(f"pr{i}", [128, 512], BF16)) for i in range(2)]
            x1b = [E3(nc.sbuf_tensor(f"x1b{i}", [128, 2048], BF16)) for i in range(2)]
            x1f = [E3(nc.sbuf_tensor(f"x1f{i}", [128, 2048], F32)) for i in range(2)]
            x1t_blk = E3(nc.sbuf_tensor("x1t_blk", [128, HC * 512], BF16))
            rowsum = [E3(nc.sbuf_tensor(f"rs{i}", [128, 1], F32)) for i in range(2)]
            sumsq = [E3(nc.sbuf_tensor(f"sq{i}", [128, 1], F32)) for i in range(2)]
            meanb = [E3(nc.sbuf_tensor(f"mn{i}", [128, 1], F32)) for i in range(2)]
            ex2 = [E3(nc.sbuf_tensor(f"e2{i}", [128, 1], F32)) for i in range(2)]
            msq = [E3(nc.sbuf_tensor(f"mq{i}", [128, 1], F32)) for i in range(2)]
            varb = [E3(nc.sbuf_tensor(f"vr{i}", [128, 1], F32)) for i in range(2)]
            lnv = [E3(nc.sbuf_tensor(f"lv{i}", [128, 1], F32)) for i in range(2)]
            rstd = [E3(nc.sbuf_tensor(f"rd{i}", [128, 1], F32)) for i in range(2)]
            ps_pr = [E3(nc.psum_tensor(f"ps_pr{i}", [128, 512], F32)) for i in range(2)]
            ps_rm = [E3(nc.psum_tensor(f"ps_rm{i}", [128, 512], F32)) for i in range(4)]
            ps_x1 = [E3(nc.psum_tensor(f"ps_x1{i}", [128, 512], F32)) for i in range(2)]

            x1t3 = x1t_blk[:].rearrange("p (hc jj) -> p hc jj", jj=512)

            @block.sync
            def _(sy):
                sy.wait_ge(sem["ph"], 15)
                sy.wait_ge(sem["cs0"], 16 * 64)
                sy.wait_ge(sem["cs1"], 16 * 64)
                for t in range(8):
                    i, sb = t // 4, t % 4
                    if t >= 2:
                        sy.wait_ge(sem["prm"], 16 * (t - 1))
                    for b in range(B):
                        sy.dma_start(out=ctx_rhs[t % 2][:, b * 512:(b + 1) * 512],
                                     in_=ctx_dram[i, b, :, sb * 512:(sb + 1) * 512]
                                     ).then_inc(sem["ctxl"], 16)
                    if t >= 1:
                        sy.wait_ge(sem["t1"], 16 * t)
                    for j in range(4):
                        s0 = sb * 512 + j * 128
                        sy.dma_start(out=xres_sb[:, j * 2048:(j + 1) * 2048],
                                     in_=x_res[i, s0:s0 + 128, :]).then_inc(sem["xrl"], 16)
                sy.nop().then_inc(sem["ph"], 1)

            @block.gpsimd
            def _(g):
                g.wait_ge(sem["ph"], 15)
                wp3 = wprojT.rearrange("(fc p) o -> p fc o", p=128)
                wpb3 = wprojT_bf[:].rearrange("p (fc o) -> p fc o", o=H)
                for q in range(4):
                    g.dma_start(out=wpb3[:, q * 4:(q + 1) * 4, :],
                                in_=wp3[:, q * 4:(q + 1) * 4, :]).then_inc(sem["wql"], 16)
                for t in range(8):
                    i, sb = t // 4, t % 4
                    for j in range(4):
                        gj = t * 4 + j
                        r0 = i * 2048 + sb * 512 + j * 128
                        g.wait_ge(sem["x1fr"], gj + 1)
                        g.dma_start(out=x1R_dram[r0:r0 + 128, :], in_=x1f[j % 2][:]
                                    ).then_inc(sem["xf%d" % (gj % 2)], 16)
                    g.wait_ge(sem["xte"], 16 * (t + 1))
                    rblk = i * 2048 + sb * 512
                    for hc in range(HC):
                        g.dma_start(out=x1T_dram[hc, :, rblk:rblk + 512],
                                    in_=x1t_blk[:, hc * 512:(hc + 1) * 512]
                                    ).then_inc(sem["xts"], 16)
                g.nop().then_inc(sem["ph"], 1)

            @block.tensor
            def _(t_):
                t_.wait_ge(sem["ph"], 15)
                t_.wait_ge(sem["wql"], 128)

                def emit_tr(t, oc):
                    k = t * 16 + oc
                    t_.wait_ge(sem["pre"], k + 1)
                    for j in range(4):
                        gid = t * 16 + (oc // 4) * 4 + j
                        if oc % 4 == 0 and gid >= 4:
                            t_.wait_ge(sem["t1"], gid - 3)
                        ins = t_.matmul(ps_rm[j][:, (oc % 4) * 128:(oc % 4 + 1) * 128],
                                        pr_sb[oc % 2][:, j * 128:(j + 1) * 128], identB[:],
                                        start=(oc % 4 == 0), stop=(oc % 4 == 3))
                    ins.then_inc(sem["prc"], 1)

                for t in range(8):
                    t_.wait_ge(sem["ctxl"], 256 * (t + 1))
                    for oc in range(16):
                        k = t * 16 + oc
                        if k >= 2:
                            t_.wait_ge(sem["pre"], k - 1)
                        for fc in range(HC):
                            ins = t_.matmul(ps_pr[oc % 2][:],
                                            wprojT_bf[:, fc * 2048 + oc * 128: fc * 2048 + (oc + 1) * 128],
                                            ctx_rhs[t % 2][:, fc * 512:(fc + 1) * 512],
                                            start=(fc == 0), stop=(fc == 15))
                        ins.then_inc(sem["prm"], 1)
                        if oc >= 1:
                            emit_tr(t, oc - 1)
                    emit_tr(t, 15)
                    for j in range(4):
                        gj = t * 4 + j
                        t_.wait_ge(sem["x1b"], gj + 1)
                        for hcg in range(4):
                            xg = 4 * gj + hcg
                            if xg >= 2:
                                t_.wait_ge(sem["xte"], xg - 1)
                            for u in range(4):
                                hc = hcg * 4 + u
                                ins = t_.matmul(ps_x1[hcg % 2][:, u * 128:(u + 1) * 128],
                                                x1b[j % 2][:, hc * 128:(hc + 1) * 128], identB[:],
                                                start=(u == 0), stop=(u == 3))
                            ins.then_inc(sem["xt1"], 1)
                t_.nop().then_inc(sem["ph"], 1)

            @block.scalar
            def _(s):
                s.wait_ge(sem["ph"], 15)
                for t in range(8):
                    for oc in range(16):
                        k = t * 16 + oc
                        s.wait_ge(sem["prm"], k + 1)
                        if k >= 2:
                            s.wait_ge(sem["prc"], k - 1)
                        s.activation(pr_sb[oc % 2][:], ps_pr[oc % 2][:], AF.Identity,
                                     bias=bias_sb[:, oc:oc + 1]).then_inc(sem["pre"], 1)
                    for j in range(4):
                        gj = t * 4 + j
                        s.wait_ge(sem["t1"], t * 16 + 12 + j + 1)
                        if gj >= 2:
                            s.wait_ge(sem["xt1"], 4 * (gj - 1))
                        s.activation(x1b[j % 2][:], t1_sb[:, j * 2048:(j + 1) * 2048],
                                     AF.Square, accum_out=sumsq[j % 2][:]
                                     ).then_inc(sem["l2"], 1)
                        s.wait_ge(sem["l3"], gj + 1)
                        s.activation(lnv[j % 2][:], varb[j % 2][:], AF.Ln
                                     ).then_inc(sem["l4"], 1)
                        s.wait_ge(sem["l4"], gj + 1)
                        s.activation(rstd[j % 2][:], lnv[j % 2][:], AF.Exp, scale=-0.5
                                     ).then_inc(sem["l5"], 1)
                        for hcg in range(4):
                            xg = 4 * gj + hcg
                            s.wait_ge(sem["xt1"], xg + 1)
                            if t >= 1 and hcg == 0 and j == 0:
                                s.wait_ge(sem["xts"], 256 * t)
                            s.activation(x1t3[:, hcg * 4:(hcg + 1) * 4, j * 128:(j + 1) * 128],
                                         ps_x1[hcg % 2][:].rearrange("p (u c) -> p u c", c=128),
                                         AF.Copy).then_inc(sem["xte"], 1)
                s.nop().then_inc(sem["ph"], 1)

            @block.vector
            def _(v):
                v.wait_ge(sem["ph"], 15)
                for t in range(8):
                    for ocg in range(4):
                        for j in range(4):
                            gid = t * 16 + ocg * 4 + j
                            v.wait_ge(sem["prc"], t * 16 + 4 * ocg + 4)
                            v.wait_ge(sem["xrl"], 64 * (t + 1))
                            v.scalar_tensor_tensor(
                                t1_sb[:, j * 2048 + ocg * 512: j * 2048 + (ocg + 1) * 512],
                                ps_rm[j][:], 1.0,
                                xres_sb[:, j * 2048 + ocg * 512: j * 2048 + (ocg + 1) * 512],
                                ALU.mult, ALU.add).then_inc(sem["t1"], 1)
                    for j in range(4):
                        gj = t * 4 + j
                        tsl = t1_sb[:, j * 2048:(j + 1) * 2048]
                        v.wait_ge(sem["t1"], t * 16 + 12 + j + 1)
                        n0 = 5 * gj
                        v.reduce_sum(rowsum[j % 2][:], tsl, axis=AX.X).then_inc(sem["l1"], 1)
                        v.wait_ge(sem["l2"], gj + 1)
                        v.wait_ge(sem["l1"], n0 + 1)
                        v.tensor_scalar(meanb[j % 2][:], rowsum[j % 2][:], INVH, None,
                                        ALU.mult).then_inc(sem["l1"], 1)
                        v.tensor_scalar(ex2[j % 2][:], sumsq[j % 2][:], INVH, None,
                                        ALU.mult).then_inc(sem["l1"], 1)
                        v.wait_ge(sem["l1"], n0 + 3)
                        v.tensor_tensor(msq[j % 2][:], meanb[j % 2][:], meanb[j % 2][:],
                                        ALU.mult).then_inc(sem["l1"], 1)
                        v.wait_ge(sem["l1"], n0 + 4)
                        v.tensor_tensor(varb[j % 2][:], ex2[j % 2][:], msq[j % 2][:],
                                        ALU.subtract).then_inc(sem["l1"], 1)
                        v.wait_ge(sem["l1"], n0 + 5)
                        v.tensor_scalar(varb[j % 2][:], varb[j % 2][:], EPS, None,
                                        ALU.add).then_inc(sem["l3"], 1)
                        v.wait_ge(sem["l5"], gj + 1)
                        v.tensor_scalar(x1b[j % 2][:], tsl, meanb[j % 2][:], rstd[j % 2][:],
                                        ALU.subtract, ALU.mult).then_inc(sem["x1b"], 1)
                        if gj >= 2:
                            v.wait_ge(sem["xf%d" % (gj % 2)], 16 * (gj // 2))
                        v.tensor_scalar(x1f[j % 2][:], tsl, meanb[j % 2][:], rstd[j % 2][:],
                                        ALU.subtract, ALU.mult).then_inc(sem["x1fr"], 1)
                v.nop().then_inc(sem["ph"], 1)

        with contextlib.ExitStack() as st4:
            E4 = st4.enter_context
            x1T_sb = E4(nc.sbuf_tensor("x1T_sb", [128, HC * 4096], BF16))
            wg_t = [E4(nc.sbuf_tensor(f"wg{i}", [128, H], BF16)) for i in range(2)]
            wu_t = [E4(nc.sbuf_tensor(f"wu{i}", [128, H], BF16)) for i in range(2)]
            sg = [E4(nc.sbuf_tensor(f"sg{i}", [128, 512], F32)) for i in range(2)]
            h_sb = [E4(nc.sbuf_tensor(f"hs{i}", [128, 4096], BF16)) for i in range(2)]
            ps_g = [E4(nc.psum_tensor(f"ps_g{i}", [128, 512], F32)) for i in range(3)]
            ps_u = [E4(nc.psum_tensor(f"ps_u{i}", [128, 512], F32)) for i in range(3)]

            @block.sync
            def _(sy):
                sy.wait_ge(sem["ph"], 20)
                sy.wait_ge(sem["xts"], 256 * 8)
                for hc in range(HC):
                    sy.dma_start(out=x1T_sb[:, hc * 4096:(hc + 1) * 4096],
                                 in_=x1T_dram[hc]).then_inc(sem["h5"], 16)
                sy.wait_ge(sem["cc"], 1)
                for oc in range(OCI):
                    if oc >= 2:
                        sy.wait_ge(sem["gg"], 8 * (oc - 1))
                    sy.dma_start(out=wg_t[oc % 2][:], in_=ago[SLOT_G + oc]
                                 ).then_inc(sem["wl%d" % (oc % 2)], 16)
                    sy.dma_start(out=wu_t[oc % 2][:], in_=ago[SLOT_U + oc]
                                 ).then_inc(sem["wl%d" % (oc % 2)], 16)
                sy.nop().then_inc(sem["ph"], 1)

            @block.gpsimd
            def _(g):
                g.wait_ge(sem["ph"], 20)
                for oc in range(OCI):
                    g.wait_ge(sem["hh"], 8 * (oc + 1))
                    g.dma_start(out=h_dram[oc * 128:(oc + 1) * 128, :], in_=h_sb[oc % 2][:]
                                ).then_inc(sem["hs%d" % (oc % 2)], 16)
                g.nop().then_inc(sem["ph"], 1)

            @block.tensor
            def _(t):
                t.wait_ge(sem["ph"], 20)
                t.wait_ge(sem["h5"], 256)
                for oc in range(OCI):
                    t.wait_ge(sem["wl%d" % (oc % 2)], 32 * (oc // 2 + 1))
                    for rc in range(8):
                        m = oc * 8 + rc
                        if m >= 3:
                            t.wait_ge(sem["sil"], m - 2)
                            t.wait_ge(sem["hh"], m - 2)
                        for hc in range(HC):
                            t.matmul(ps_g[m % 3][:], wg_t[oc % 2][:, hc * 128:(hc + 1) * 128],
                                     x1T_sb[:, hc * 4096 + rc * 512: hc * 4096 + (rc + 1) * 512],
                                     start=(hc == 0), stop=(hc == 15))
                        for hc in range(HC):
                            ins = t.matmul(ps_u[m % 3][:], wu_t[oc % 2][:, hc * 128:(hc + 1) * 128],
                                           x1T_sb[:, hc * 4096 + rc * 512: hc * 4096 + (rc + 1) * 512],
                                           start=(hc == 0), stop=(hc == 15))
                        ins.then_inc(sem["gg"], 1)
                t.nop().then_inc(sem["ph"], 1)

            @block.scalar
            def _(s):
                s.wait_ge(sem["ph"], 20)
                for oc in range(OCI):
                    for rc in range(8):
                        m = oc * 8 + rc
                        s.wait_ge(sem["gg"], m + 1)
                        if m >= 2:
                            s.wait_ge(sem["hh"], m - 1)
                        s.activation(sg[rc % 2][:], ps_g[m % 3][:], AF.Silu
                                     ).then_inc(sem["sil"], 1)
                s.nop().then_inc(sem["ph"], 1)

            @block.vector
            def _(v):
                v.wait_ge(sem["ph"], 20)
                for oc in range(OCI):
                    for rc in range(8):
                        m = oc * 8 + rc
                        v.wait_ge(sem["sil"], m + 1)
                        if rc == 0 and oc >= 2:
                            v.wait_ge(sem["hs%d" % (oc % 2)], 16 * (oc // 2))
                        v.scalar_tensor_tensor(h_sb[oc % 2][:, rc * 512:(rc + 1) * 512],
                                               ps_u[m % 3][:], 1.0, sg[rc % 2][:],
                                               ALU.mult, ALU.mult).then_inc(sem["hh"], 1)
                v.nop().then_inc(sem["ph"], 1)

        with contextlib.ExitStack() as st5:
            E5 = st5.enter_context
            h5_sb = E5(nc.sbuf_tensor("h5_sb", [128, OCI * 512], BF16))
            wd_t = [E5(nc.sbuf_tensor(f"wd{i}", [128, INTER], BF16)) for i in range(2)]
            dn_sb = E5(nc.sbuf_tensor("dn_sb", [128, 16 * 512], BF16))
            x1r_sb = [E5(nc.sbuf_tensor(f"x1r{i}", [128, 2048], F32)) for i in range(4)]
            t2_sb = E5(nc.sbuf_tensor("t2_sb", [128, 4 * 2048], F32))
            rw5 = [E5(nc.sbuf_tensor(f"rw5{i}", [128, 1], F32)) for i in range(2)]
            sq5 = [E5(nc.sbuf_tensor(f"sq5{i}", [128, 1], F32)) for i in range(2)]
            mn5 = [E5(nc.sbuf_tensor(f"mn5{i}", [128, 1], F32)) for i in range(2)]
            e25 = [E5(nc.sbuf_tensor(f"e25{i}", [128, 1], F32)) for i in range(2)]
            mq5 = [E5(nc.sbuf_tensor(f"mq5{i}", [128, 1], F32)) for i in range(2)]
            vr5 = [E5(nc.sbuf_tensor(f"vr5{i}", [128, 1], F32)) for i in range(2)]
            lv5 = [E5(nc.sbuf_tensor(f"lv5{i}", [128, 1], F32)) for i in range(2)]
            rd5 = [E5(nc.sbuf_tensor(f"rd5{i}", [128, 1], F32)) for i in range(2)]
            ps_d = [E5(nc.psum_tensor(f"ps_d{i}", [128, 512], F32)) for i in range(3)]
            ps_r5 = [E5(nc.psum_tensor(f"ps_r5{i}", [128, 512], F32)) for i in range(4)]

            h_3 = h_dram.rearrange("(t p) r -> p t r", p=128)
            h5_3 = h5_sb[:].rearrange("p (t c) -> p t c", c=512)
            tchunks = [(0, 4), (4, 8), (8, 12), (12, 16), (16, 20), (20, 24),
                       (24, 28), (28, 32), (32, 36), (36, 40), (40, 43)]

            @block.sync
            def _(sy):
                sy.wait_ge(sem["ph"], 25)
                sy.wait_ge(sem["hs0"], 16 * 22)
                sy.wait_ge(sem["hs1"], 16 * 21)
                sy.wait_ge(sem["xf0"], 16 * 16)
                sy.wait_ge(sem["xf1"], 16 * 16)
                for rc in range(8):
                    r0 = rc * 512
                    i_n, s0 = rc // 4, (rc % 4) * 512
                    if rc >= 1:
                        sy.wait_ge(sem["dnm"], 16 * rc)
                    for (t0, t1) in tchunks:
                        sy.dma_start(out=h5_3[:, t0:t1, :], in_=h_3[:, t0:t1, r0:r0 + 512]
                                     ).then_inc(sem["h5"], 16)
                    for d2c in range(16):
                        dm = rc * 16 + d2c
                        if dm >= 2:
                            sy.wait_ge(sem["dnm"], dm - 1)
                        for kk in range(3):
                            sy.dma_start(
                                out=wd_t[d2c % 2][:, kk * 2048:kk * 2048 + DW[kk]],
                                in_=ago[SLOT_D + 3 * d2c + kk][:, :DW[kk]],
                            ).then_inc(sem["wd%d" % (d2c % 2)], 16)
                    for j in range(4):
                        if rc >= 1:
                            sy.wait_ge(sem["t15"], (rc - 1) * 16 + 13 + j)
                        sy.dma_start(out=x1r_sb[j][:],
                                     in_=x1R_dram[r0 + j * 128: r0 + (j + 1) * 128, :]
                                     ).then_inc(sem["xr0"], 16)
                sy.nop().then_inc(sem["ph"], 1)

            @block.gpsimd
            def _(g):
                g.wait_ge(sem["ph"], 25)
                for rc in range(8):
                    i_n, s0 = rc // 4, (rc % 4) * 512
                    for j in range(4):
                        gj5 = rc * 4 + j
                        g.wait_ge(sem["yrd"], gj5 + 1)
                        g.dma_start(out=y[i_n, s0 + j * 128: s0 + (j + 1) * 128, :],
                                    in_=t2_sb[:, j * 2048:(j + 1) * 2048]
                                    ).then_inc(sem["ou0"], 16)
                g.wait_ge(sem["ou0"], 64 * 8)
                g.nop().then_inc(sem["ph"], 1)

            @block.tensor
            def _(t):
                t.wait_ge(sem["ph"], 25)
                for rc in range(8):
                    t.wait_ge(sem["h5"], 256 + 176 * (rc + 1))
                    for d2c in range(16):
                        dm = rc * 16 + d2c
                        t.wait_ge(sem["wd%d" % (d2c % 2)], 48 * (dm // 2 + 1))
                        if dm >= 3:
                            t.wait_ge(sem["dne"], dm - 2)
                        for oc in range(OCI):
                            ins = t.matmul(ps_d[dm % 3][:], wd_t[d2c % 2][:, oc * 128:(oc + 1) * 128],
                                           h5_sb[:, oc * 512:(oc + 1) * 512],
                                           start=(oc == 0), stop=(oc == OCI - 1))
                        ins.then_inc(sem["dnm"], 1)
                    for d2c in range(16):
                        dm = rc * 16 + d2c
                        t.wait_ge(sem["dne"], dm + 1)
                        for j in range(4):
                            gid = rc * 16 + (d2c // 4) * 4 + j
                            if d2c % 4 == 0 and gid >= 4:
                                t.wait_ge(sem["t15"], gid - 3)
                            ins = t.matmul(ps_r5[j][:, (d2c % 4) * 128:(d2c % 4 + 1) * 128],
                                           dn_sb[:, d2c * 512 + j * 128: d2c * 512 + (j + 1) * 128],
                                           identB[:], start=(d2c % 4 == 0), stop=(d2c % 4 == 3))
                        ins.then_inc(sem["trm5"], 1)
                t.nop().then_inc(sem["ph"], 1)

            @block.scalar
            def _(s):
                s.wait_ge(sem["ph"], 25)
                for rc in range(8):
                    for d2c in range(16):
                        dm = rc * 16 + d2c
                        s.wait_ge(sem["dnm"], dm + 1)
                        if rc >= 1 and d2c == 0:
                            s.wait_ge(sem["trm5"], 16 * rc)
                        s.activation(dn_sb[:, d2c * 512:(d2c + 1) * 512], ps_d[dm % 3][:],
                                     AF.Copy).then_inc(sem["dne"], 1)
                    for j in range(4):
                        gj5 = rc * 4 + j
                        s.wait_ge(sem["t15"], rc * 16 + 12 + j + 1)
                        s.activation(dn_sb[:, j * 2048:(j + 1) * 2048],
                                     t2_sb[:, j * 2048:(j + 1) * 2048], AF.Square,
                                     accum_out=sq5[j % 2][:]).then_inc(sem["m2"], 1)
                        s.wait_ge(sem["m3"], gj5 + 1)
                        s.activation(lv5[j % 2][:], vr5[j % 2][:], AF.Ln
                                     ).then_inc(sem["m4"], 1)
                        s.wait_ge(sem["m4"], gj5 + 1)
                        s.activation(rd5[j % 2][:], lv5[j % 2][:], AF.Exp, scale=-0.5
                                     ).then_inc(sem["m5"], 1)
                s.nop().then_inc(sem["ph"], 1)

            @block.vector
            def _(v):
                v.wait_ge(sem["ph"], 25)
                for rc in range(8):
                    for d2g in range(4):
                        for j in range(4):
                            gid = rc * 16 + d2g * 4 + j
                            v.wait_ge(sem["trm5"], rc * 16 + 4 * d2g + 4)
                            v.wait_ge(sem["xr0"], 64 * (rc + 1))
                            if d2g == 0 and rc >= 1:
                                v.wait_ge(sem["ou0"], 64 * rc)
                            v.scalar_tensor_tensor(
                                t2_sb[:, j * 2048 + d2g * 512: j * 2048 + (d2g + 1) * 512],
                                ps_r5[j][:], 1.0,
                                x1r_sb[j][:, d2g * 512:(d2g + 1) * 512],
                                ALU.mult, ALU.add).then_inc(sem["t15"], 1)
                    for j in range(4):
                        gj5 = rc * 4 + j
                        tsl = t2_sb[:, j * 2048:(j + 1) * 2048]
                        v.wait_ge(sem["t15"], rc * 16 + 12 + j + 1)
                        n0 = 5 * gj5
                        v.reduce_sum(rw5[j % 2][:], tsl, axis=AX.X).then_inc(sem["m1"], 1)
                        v.wait_ge(sem["m2"], gj5 + 1)
                        v.wait_ge(sem["m1"], n0 + 1)
                        v.tensor_scalar(mn5[j % 2][:], rw5[j % 2][:], INVH, None,
                                        ALU.mult).then_inc(sem["m1"], 1)
                        v.tensor_scalar(e25[j % 2][:], sq5[j % 2][:], INVH, None,
                                        ALU.mult).then_inc(sem["m1"], 1)
                        v.wait_ge(sem["m1"], n0 + 3)
                        v.tensor_tensor(mq5[j % 2][:], mn5[j % 2][:], mn5[j % 2][:],
                                        ALU.mult).then_inc(sem["m1"], 1)
                        v.wait_ge(sem["m1"], n0 + 4)
                        v.tensor_tensor(vr5[j % 2][:], e25[j % 2][:], mq5[j % 2][:],
                                        ALU.subtract).then_inc(sem["m1"], 1)
                        v.wait_ge(sem["m1"], n0 + 5)
                        v.tensor_scalar(vr5[j % 2][:], vr5[j % 2][:], EPS, None,
                                        ALU.add).then_inc(sem["m3"], 1)
                        v.wait_ge(sem["m5"], gj5 + 1)
                        v.tensor_scalar(tsl, tsl, mn5[j % 2][:], rd5[j % 2][:],
                                        ALU.subtract, ALU.mult).then_inc(sem["yrd"], 1)
                v.nop().then_inc(sem["ph"], 1)
    return nc


def _gather_blocks(gi, ui, di):
    blocks = np.zeros((8 * NJL, 128, H), dtype=np.int32)
    for oc in range(OCI):
        blk = gi[oc * 128:(oc + 1) * 128, :].T.reshape(16, 128, 128)
        blocks[SLOT_G + oc] = blk.transpose(1, 0, 2).reshape(128, H)
        blk = ui[oc * 128:(oc + 1) * 128, :].T.reshape(16, 128, 128)
        blocks[SLOT_U + oc] = blk.transpose(1, 0, 2).reshape(128, H)
    for dc in range(16):
        wd = di[dc * 128:(dc + 1) * 128, :].T.reshape(OCI, 128, 128)
        wd = wd.transpose(1, 0, 2).reshape(128, INTER)
        for kk in range(3):
            blocks[SLOT_D + 3 * dc + kk, :, :DW[kk]] = \
                wd[:, kk * 2048:kk * 2048 + DW[kk]]
    return np.ascontiguousarray(blocks)


def kernel(**inputs):
    _ = inputs["L0_attn_key"], inputs["L0_attn_value"]
    x = np.ascontiguousarray(np.asarray(inputs["L0_attn_query"], dtype=np.float32))
    W = np.asarray(inputs["L0_attn_Wqkv"], dtype=np.float32)
    Wp = np.asarray(inputs["L0_attn_Wproj"], dtype=np.float32)
    bp = np.asarray(inputs["L0_attn_bproj"], dtype=np.float32)
    lut = np.asarray(inputs["L0_ffn_lut"], dtype=np.float32)
    gi = np.ascontiguousarray(np.asarray(inputs["L0_ffn_gate_idx"], dtype=np.int32))
    ui = np.ascontiguousarray(np.asarray(inputs["L0_ffn_up_idx"], dtype=np.int32))
    di = np.ascontiguousarray(np.asarray(inputs["L0_ffn_down_idx"], dtype=np.int32))

    xT = np.ascontiguousarray(x.transpose(2, 0, 1))
    wprojT = np.ascontiguousarray(Wp.T)
    bias2dT = np.ascontiguousarray(bp.reshape(16, 128).T)
    lut2d = np.ascontiguousarray(lut.reshape(1, 16))
    ident = np.eye(128, dtype=np.float32)
    onesc = np.ones((128, 1), dtype=np.float32)
    onesr = np.ones((1, 128), dtype=np.float32)
    gblocks = _gather_blocks(gi, ui, di)

    nc = build_nc()
    in_maps = []
    for c in range(8):
        n0, n1 = 2 * c, 2 * c + 1
        wq = np.concatenate([
            W[n0 * 384: n0 * 384 + 128],
            W[n0 * 384 + 128: n0 * 384 + 256],
            W[n1 * 384: n1 * 384 + 128],
            W[n1 * 384 + 128: n1 * 384 + 256],
            W[n0 * 384 + 256: n0 * 384 + 384],
            W[n1 * 384 + 256: n1 * 384 + 384],
        ], axis=0)
        in_maps.append({
            "xT_in": xT, "x_res": np.ascontiguousarray(x[n0:n1 + 1]),
            "wqkvT": np.ascontiguousarray(wq.T), "wprojT": wprojT,
            "bias2dT": bias2dT, "lut2d": lut2d,
            "idx_sh": gblocks[c * NJL:(c + 1) * NJL],
            "ident_in": ident, "onesc_in": onesc, "onesr_in": onesr,
        })
    res = run_bass_kernel_spmd(nc, in_maps, core_ids=list(range(8)))
    return np.concatenate([np.asarray(r["y"]) for r in res.results], axis=0)


# version 20
# speedup vs baseline: 1.0702x; 1.0038x over previous
import contextlib
import numpy as np
import concourse.bass as bass
import concourse.mybir as mybir
from concourse.bass_utils import run_bass_kernel_spmd

F32 = mybir.dt.float32
BF16 = mybir.dt.bfloat16
I32 = mybir.dt.int32
ALU = mybir.AluOpType
AF = mybir.ActivationFunctionType
AX = mybir.AxisListType

B = 16
S = 2048
H = 2048
HN = 128
HC = 16
INTER = 5504
OCI = 43
NBLK = 64
SCALE = float(1.0 / np.sqrt(HN))
EPS = 1e-5
INVH = 1.0 / 2048.0
NJL = 17
SLOT_G = 0
SLOT_U = OCI
SLOT_D = 2 * OCI
DW = (2048, 2048, INTER - 4096)


def build_nc(debug_dump=False):
    nc = bass.Bass("TRN2", target_bir_lowering=False)

    xT_in = nc.dram_tensor("xT_in", [H, B, S], F32, kind="ExternalInput").ap()
    x_res = nc.dram_tensor("x_res", [2, S, H], F32, kind="ExternalInput").ap()
    wqkvT = nc.dram_tensor("wqkvT", [H, 768], F32, kind="ExternalInput").ap()
    wprojT = nc.dram_tensor("wprojT", [H, H], F32, kind="ExternalInput").ap()
    bias2dT = nc.dram_tensor("bias2dT", [128, 16], F32, kind="ExternalInput").ap()
    lut2d = nc.dram_tensor("lut2d", [1, 16], F32, kind="ExternalInput").ap()
    idx_sh = nc.dram_tensor("idx_sh", [NJL, 128, H], I32, kind="ExternalInput").ap()
    ident_in = nc.dram_tensor("ident_in", [128, 128], F32, kind="ExternalInput").ap()
    onesc_in = nc.dram_tensor("onesc_in", [128, 1], F32, kind="ExternalInput").ap()
    onesr_in = nc.dram_tensor("onesr_in", [1, 128], F32, kind="ExternalInput").ap()
    y = nc.dram_tensor("y", [2, S, H], F32, kind="ExternalOutput").ap()

    KD = "ExternalOutput" if debug_dump else "Internal"
    qk_dram = nc.dram_tensor("qk_dram", [4, B, 128, S], BF16, kind=KD).ap()
    v_dram = nc.dram_tensor("v_dram", [2, B, S, 128], BF16, kind=KD).ap()
    ctx_dram = nc.dram_tensor("ctx_dram", [2, B, 128, S], BF16, kind=KD).ap()
    x1T_dram = nc.dram_tensor("x1T_dram", [HC, 128, 4096], BF16, kind=KD).ap()
    x1R_dram = nc.dram_tensor("x1R_dram", [4096, H], F32, kind=KD).ap()
    h_dram = nc.dram_tensor("h_dram", [INTER, 4096], BF16, kind=KD).ap()
    wpbf_dram = nc.dram_tensor("wpbf_dram", [HC, 128, H], BF16, kind="Internal").ap()
    ag_in = nc.dram_tensor("ag_in", [NJL, 128, H], BF16, kind="Internal")
    ag_out = nc.dram_tensor("ag_out", [8 * NJL, 128, H], BF16, kind="Internal",
                            addr_space="Shared")
    ago = ag_out.ap()

    with contextlib.ExitStack() as st0:
        E = st0.enter_context

        identB = E(nc.sbuf_tensor("identB", [128, 128], BF16))
        identF = E(nc.sbuf_tensor("identF", [128, 128], F32))
        onesc = E(nc.sbuf_tensor("onesc", [128, 1], BF16))
        onesr = E(nc.sbuf_tensor("onesr", [1, 128], F32))
        lut_sb = E(nc.sbuf_tensor("lut_sb", [1, 16], F32))
        lutB = E(nc.sbuf_tensor("lutB", [128, 16], F32))
        bias_sb = E(nc.sbuf_tensor("bias_sb", [128, 16], F32))

        names = ["ph", "c0", "c1", "xld", "wql", "qm", "qe", "qs0", "qs1", "vm",
                 "ve", "vs0", "vs1", "gl", "gc", "gd", "gs", "cc",
                 "pa", "pb", "rsm", "gg", "uu",
                 "qk0", "qk1", "ss", "exp", "pv", "rc", "bcm", "bce", "ctx", "cs0", "cs1",
                 "ctxl", "prm", "prc", "pre", "t1", "xrl",
                 "l1", "l2", "l3", "l4", "l5", "x1b", "x1fr", "xf0", "xf1", "wpc",
                 "xt1", "xte", "xts",
                 "wl0", "wl1", "gum", "sil", "hh", "hs0", "hs1",
                 "h5", "wd0", "wd1", "dnm", "dne", "trm5", "t15",
                 "m1", "m2", "m3", "m4", "m5", "yrd", "ou0", "xr0", "misc"]
        sem = {n: E(nc.semaphore("s_" + n)) for n in names}
        block = E(nc.Block())

        @block.gpsimd
        def _(g):
            g.dma_start(out=identB[:], in_=ident_in).then_inc(sem["c0"], 16)
            g.dma_start(out=onesc[:], in_=onesc_in).then_inc(sem["c0"], 16)
            g.wait_ge(sem["c0"], 32)
            g.nop().then_inc(sem["ph"], 1)

        @block.sync
        def _(sy):
            sy.dma_start(out=identF[:], in_=ident_in).then_inc(sem["c1"], 16)
            sy.dma_start(out=onesr[:], in_=onesr_in).then_inc(sem["c1"], 16)
            sy.dma_start(out=lut_sb[:], in_=lut2d).then_inc(sem["c1"], 16)
            sy.dma_start(out=bias_sb[:], in_=bias2dT).then_inc(sem["c1"], 16)
            sy.wait_ge(sem["c1"], 64)
            sy.nop().then_inc(sem["ph"], 1)

        with nc.psum_tensor("ps_lut", [128, 16], F32) as ps_lut:
            @block.tensor
            def _(t):
                t.wait_ge(sem["c1"], 64)
                t.matmul(ps_lut[:], onesr[:], lut_sb[:], start=True, stop=True
                         ).then_inc(sem["misc"], 1)
                t.nop().then_inc(sem["ph"], 1)

            @block.scalar
            def _(s):
                s.wait_ge(sem["misc"], 1)
                s.activation(lutB[:], ps_lut[:], AF.Copy).then_inc(sem["ph"], 1)

        @block.vector
        def _(v):
            v.nop().then_inc(sem["ph"], 1)

        st12 = contextlib.ExitStack()
        E12 = st12.enter_context
        QT = [E12(nc.sbuf_tensor(f"QT{i}", [128, S], BF16)) for i in range(2)]
        KT = [E12(nc.sbuf_tensor(f"KT{i}", [128, S], BF16)) for i in range(2)]
        V16 = [E12(nc.sbuf_tensor(f"V16{i}", [128, S], BF16)) for i in range(2)]
        PT = [E12(nc.sbuf_tensor(f"PT{i}", [128, 512], BF16)) for i in range(20)]
        recip = [E12(nc.sbuf_tensor(f"recip{i}", [1, 512], F32)) for i in range(2)]
        bc_sb = [E12(nc.sbuf_tensor(f"bc{i}", [128, 512], F32)) for i in range(2)]
        ctxo = [E12(nc.sbuf_tensor(f"ctxo{i}", [128, 512], BF16)) for i in range(2)]

        with contextlib.ExitStack() as st1:
            E1 = st1.enter_context
            wqkvT_bf = E1(nc.sbuf_tensor("wqkvT_bf", [128, HC * 768], BF16))
            XT = [E1(nc.sbuf_tensor(f"XT{i}", [128, HC * 512], BF16)) for i in range(3)]
            qk_buf = [E1(nc.sbuf_tensor(f"qkb{i}", [128, 512], BF16)) for i in range(2)]
            v_buf = [E1(nc.sbuf_tensor(f"vb{i}", [128, 256], BF16)) for i in range(2)]
            idx_i = [E1(nc.sbuf_tensor(f"idx_i{i}", [128, H], I32)) for i in range(2)]
            idxb = [E1(nc.sbuf_tensor(f"idxb{i}", [128, H], BF16)) for i in range(2)]
            wacc = [E1(nc.sbuf_tensor(f"wacc{i}", [128, H], BF16)) for i in range(2)]
            wtmp = E1(nc.sbuf_tensor("wtmp", [128, H], BF16))
            ps_qk = [E1(nc.psum_tensor(f"ps_qk{i}", [128, 512], F32)) for i in range(3)]
            ps_v = [E1(nc.psum_tensor(f"ps_v{i}", [128, 256], F32)) for i in range(3)]

            @block.sync
            def _(sy):
                sy.wait_ge(sem["ph"], 5)
                for j in range(NJL):
                    if j >= 2:
                        sy.wait_ge(sem["gc"], j - 1)
                    sy.dma_start(out=idx_i[j % 2][:], in_=idx_sh[j]
                                 ).then_inc(sem["gl"], 16)
                sy.nop().then_inc(sem["ph"], 1)

            @block.gpsimd
            def _(g):
                g.wait_ge(sem["ph"], 5)
                wq3 = wqkvT.rearrange("(hc p) o -> p hc o", p=128)
                wb3 = wqkvT_bf[:].rearrange("p (hc o) -> p hc o", o=768)
                for q in range(4):
                    g.dma_start(out=wb3[:, q * 4:(q + 1) * 4, :],
                                in_=wq3[:, q * 4:(q + 1) * 4, :]).then_inc(sem["wql"], 16)
                wp3 = wprojT.rearrange("(fc p) o -> fc p o", p=128)
                for q in range(4):
                    g.dma_start(out=wpbf_dram[q * 4:(q + 1) * 4],
                                in_=wp3[q * 4:(q + 1) * 4]).then_inc(sem["wpc"], 16)
                for blk in range(NBLK):
                    b, sb = blk // 4, blk % 4
                    if blk >= 3:
                        g.wait_ge(sem["qm"], 4 * (blk - 2))
                        g.wait_ge(sem["vm"], 4 * (blk - 2))
                    g.dma_start(
                        out=XT[blk % 3][:].rearrange("p (hc s) -> p hc s", s=512),
                        in_=xT_in[:, b, sb * 512:(sb + 1) * 512].rearrange(
                            "(hc p) s -> p hc s", p=128),
                    ).then_inc(sem["xld"], 16)
                    if blk % 2 == 0 and blk // 2 < NJL:
                        j = blk // 2
                        g.wait_ge(sem["gl"], 16 * (j + 1))
                        if j >= 2:
                            g.wait_ge(sem["gd"], j - 1)
                        g.tensor_copy(idxb[j % 2][:], idx_i[j % 2][:]
                                      ).then_inc(sem["gc"], 1)
                        if j >= 1:
                            g.wait_ge(sem["gd"], j)
                            g.dma_start(out=ag_in.ap()[j - 1], in_=wacc[(j - 1) % 2][:]
                                        ).then_inc(sem["gs"], 16)
                g.wait_ge(sem["gd"], NJL)
                g.dma_start(out=ag_in.ap()[NJL - 1], in_=wacc[(NJL - 1) % 2][:]
                            ).then_inc(sem["gs"], 16)
                g.wait_ge(sem["gs"], 16 * NJL)
                g.collective_compute(
                    "AllGather", ALU.bypass, replica_groups=[list(range(8))],
                    ins=[ag_in.ap().opt()], outs=[ago.opt()],
                ).then_inc(sem["cc"])
                g.nop().then_inc(sem["ph"], 1)

            @block.scalar
            def _(s):
                s.wait_ge(sem["ph"], 5)
                for blk in range(NBLK):
                    b, sb = blk // 4, blk % 4
                    srange = slice(sb * 512, (sb + 1) * 512)
                    for oc in range(4):
                        k = blk * 4 + oc
                        s.wait_ge(sem["qm"], k + 1)
                        if k >= 2:
                            s.wait_ge(sem["qs%d" % (k % 2)], 16 * (k // 2))
                        s.activation(qk_buf[oc % 2][:], ps_qk[k % 3][:], AF.Copy
                                     ).then_inc(sem["qe"], 1)
                        s.wait_ge(sem["qe"], k + 1)
                        s.dma_start(out=qk_dram[oc, b, :, srange], in_=qk_buf[oc % 2][:]
                                    ).then_inc(sem["qs%d" % (oc % 2)], 16)
                    for ss in range(4):
                        k = blk * 4 + ss
                        s.wait_ge(sem["vm"], k + 1)
                        if k >= 2:
                            s.wait_ge(sem["vs%d" % (k % 2)], 32 * (k // 2))
                        s.activation(v_buf[ss % 2][:], ps_v[k % 3][:], AF.Copy
                                     ).then_inc(sem["ve"], 1)
                        s.wait_ge(sem["ve"], k + 1)
                        sr2 = slice(sb * 512 + ss * 128, sb * 512 + (ss + 1) * 128)
                        s.dma_start(out=v_dram[0, b, sr2, :], in_=v_buf[ss % 2][:, 0:128]
                                    ).then_inc(sem["vs%d" % (ss % 2)], 16)
                        s.dma_start(out=v_dram[1, b, sr2, :], in_=v_buf[ss % 2][:, 128:256]
                                    ).then_inc(sem["vs%d" % (ss % 2)], 16)
                s.nop().then_inc(sem["ph"], 1)

            @block.tensor
            def _(t):
                t.wait_ge(sem["ph"], 5)
                t.wait_ge(sem["wql"], 64)
                for blk in range(NBLK):
                    t.wait_ge(sem["xld"], 16 * (blk + 1))
                    for oc in range(4):
                        k = blk * 4 + oc
                        if k >= 3:
                            t.wait_ge(sem["qe"], k - 2)
                        for hc in range(HC):
                            ins = t.matmul(ps_qk[k % 3][:],
                                           wqkvT_bf[:, hc * 768 + oc * 128: hc * 768 + (oc + 1) * 128],
                                           XT[blk % 3][:, hc * 512:(hc + 1) * 512],
                                           start=(hc == 0), stop=(hc == 15))
                        ins.then_inc(sem["qm"], 1)
                    for ss in range(4):
                        k = blk * 4 + ss
                        if k >= 3:
                            t.wait_ge(sem["ve"], k - 2)
                        for hc in range(HC):
                            ins = t.matmul(ps_v[k % 3][:],
                                           XT[blk % 3][:, hc * 512 + ss * 128: hc * 512 + (ss + 1) * 128],
                                           wqkvT_bf[:, hc * 768 + 512: hc * 768 + 768],
                                           start=(hc == 0), stop=(hc == 15))
                        ins.then_inc(sem["vm"], 1)
                t.nop().then_inc(sem["ph"], 1)

            @block.vector
            def _(v):
                v.wait_ge(sem["ph"], 5)
                for j in range(NJL):
                    w = H
                    v.wait_ge(sem["gc"], j + 1)
                    if j >= 2:
                        v.wait_ge(sem["gs"], 16 * (j - 1))
                    v.tensor_scalar(wacc[j % 2][:, :w], idxb[j % 2][:, :w], 0.0,
                                    lutB[:, 0:1], ALU.is_equal, ALU.mult)
                    for tt in range(1, 16):
                        v.tensor_scalar(wtmp[:, :w], idxb[j % 2][:, :w], float(tt),
                                        lutB[:, tt:tt + 1], ALU.is_equal, ALU.mult)
                        ins = v.tensor_tensor(wacc[j % 2][:, :w], wacc[j % 2][:, :w],
                                              wtmp[:, :w], ALU.add)
                    ins.then_inc(sem["gd"], 1)
                v.nop().then_inc(sem["ph"], 1)

        with contextlib.ExitStack() as st2:
            E2 = st2.enter_context
            ps_s = [E2(nc.psum_tensor(f"ps_s{i}", [128, 512], F32)) for i in range(4)]
            ps_c = [E2(nc.psum_tensor(f"ps_c{i}", [128, 512], F32)) for i in range(2)]
            ps_sum = [E2(nc.psum_tensor(f"ps_sm{i}", [1, 512], F32)) for i in range(1)]
            ps_bc = [E2(nc.psum_tensor(f"ps_bc{i}", [128, 512], F32)) for i in range(1)]

            @block.sync
            def _(sy):
                for g in range(32):
                    i, b = g // 16, g % 16
                    if g >= 2:
                        sy.wait_ge(sem["pv"], 64 * (g - 1))
                    w0 = 32 * (4 * b + 4)
                    sy.wait_ge(sem["qs0"], w0)
                    sy.wait_ge(sem["qs1"], w0)
                    sy.wait_ge(sem["vs0"], 2 * w0)
                    sy.wait_ge(sem["vs1"], 2 * w0)
                    sy.dma_start(out=QT[g % 2][:], in_=qk_dram[2 * i, b]
                                 ).then_inc(sem["qk%d" % (g % 2)], 16)
                    sy.dma_start(out=KT[g % 2][:], in_=qk_dram[2 * i + 1, b]
                                 ).then_inc(sem["qk%d" % (g % 2)], 16)
                    v3 = v_dram[i, b].rearrange("(t p) d -> p t d", p=128)
                    o3 = V16[g % 2][:].rearrange("p (t d) -> p t d", d=128)
                    for q in range(4):
                        sy.dma_start(out=o3[:, q * 4:(q + 1) * 4, :],
                                     in_=v3[:, q * 4:(q + 1) * 4, :]
                                     ).then_inc(sem["qk%d" % (g % 2)], 16)
                sy.nop().then_inc(sem["ph"], 1)

            @block.gpsimd
            def _(g):
                g.wait_ge(sem["ph"], 10)

                def spill(k2):
                    i, b, qc = k2 // 64, (k2 // 4) % 16, k2 % 4
                    g.wait_ge(sem["ctx"], k2 + 1)
                    g.dma_start(out=ctx_dram[i, b, :, qc * 512:(qc + 1) * 512],
                                in_=ctxo[qc % 2][:]).then_inc(sem["cs%d" % (k2 % 2)], 16)

                for k in range(128):
                    spill(k)
                g.wait_ge(sem["cs0"], 16 * 64)
                g.wait_ge(sem["cs1"], 16 * 64)
                g.nop().then_inc(sem["ph"], 1)

            @block.tensor
            def _(t):
                t.wait_ge(sem["ph"], 10)

                def rowsum(k2):
                    t.wait_ge(sem["exp"], 16 * (k2 + 1))
                    t.wait_ge(sem["rc"], k2)
                    e0b = k2 * 16
                    for i in range(16):
                        ins = t.matmul(ps_sum[0][:], onesc[:], PT[(e0b + i) % 20][:],
                                       start=(i == 0), stop=(i == 15))
                    ins.then_inc(sem["rsm"], 1)

                def bcast(k2):
                    t.wait_ge(sem["rc"], k2 + 1)
                    if k2 >= 1:
                        t.wait_ge(sem["bce"], k2)
                    t.matmul(ps_bc[0][:], onesr[:], recip[k2 % 2][:],
                             start=True, stop=True).then_inc(sem["bcm"], 1)

                def mmS(T):
                    g, qc, kt = T // 64, (T % 64) // 16, T % 16
                    if T % 64 == 0:
                        t.wait_ge(sem["qk%d" % (g % 2)], 96 * (g // 2 + 1))
                    if T >= 4:
                        t.wait_ge(sem["exp"], T - 3)
                    t.matmul(ps_s[T % 4][:],
                             KT[g % 2][:, kt * 128:(kt + 1) * 128],
                             QT[g % 2][:, qc * 512:(qc + 1) * 512], start=True, stop=True
                             ).then_inc(sem["ss"], 1)

                def mmPV(T):
                    g, kt, k = T // 64, T % 16, T // 16
                    t.wait_ge(sem["exp"], T + 1)
                    if kt == 0 and k >= 2:
                        t.wait_ge(sem["ctx"], k - 1)
                    t.matmul(ps_c[k % 2][:],
                             V16[g % 2][:, kt * 128:(kt + 1) * 128],
                             PT[T % 20][:], start=(kt == 0), stop=(kt == 15)
                             ).then_inc(sem["pv"], 1)

                for T in range(2048 + 3):
                    if T < 2048:
                        mmS(T)
                    Tp = T - 3
                    if Tp >= 0:
                        mmPV(Tp)
                        kt, k = Tp % 16, Tp // 16
                        if kt == 1 and k >= 1:
                            rowsum(k - 1)
                        if kt == 4 and k >= 1:
                            bcast(k - 1)
                rowsum(127)
                bcast(127)
                t.nop().then_inc(sem["ph"], 1)

            @block.scalar
            def _(s):
                s.wait_ge(sem["ph"], 10)

                def bce_evac(k2):
                    s.wait_ge(sem["bcm"], k2 + 1)
                    if k2 >= 2:
                        s.wait_ge(sem["ctx"], k2 - 1)
                    s.activation(bc_sb[k2 % 2][:], ps_bc[0][:], AF.Copy
                                 ).then_inc(sem["bce"], 1)

                for k in range(128):
                    e0 = k * 16
                    for kt in range(16):
                        T = e0 + kt
                        s.wait_ge(sem["ss"], T + 1)
                        if T >= 20:
                            s.wait_ge(sem["pv"], T - 19)
                        s.activation(PT[T % 20][:], ps_s[T % 4][:],
                                     AF.Exp, scale=SCALE).then_inc(sem["exp"], 1)
                        if kt == 8 and k >= 1:
                            bce_evac(k - 1)
                bce_evac(127)
                s.nop().then_inc(sem["ph"], 1)

            @block.vector
            def _(v):
                v.wait_ge(sem["ph"], 10)

                def vrecip(k2):
                    v.wait_ge(sem["rsm"], k2 + 1)
                    if k2 >= 2:
                        v.wait_ge(sem["bcm"], k2 - 1)
                    v.reciprocal(recip[k2 % 2][:], ps_sum[0][:]).then_inc(sem["rc"], 1)

                def vctx(k2):
                    v.wait_ge(sem["bce"], k2 + 1)
                    v.wait_ge(sem["pv"], 16 * (k2 + 1))
                    if k2 >= 2:
                        v.wait_ge(sem["cs%d" % (k2 % 2)], 16 * (k2 // 2))
                    v.scalar_tensor_tensor(ctxo[k2 % 2][:], ps_c[k2 % 2][:], 1.0,
                                           bc_sb[k2 % 2][:], ALU.mult, ALU.mult
                                           ).then_inc(sem["ctx"], 1)

                for k in range(1, 128):
                    vrecip(k - 1)
                    vctx(k - 1)
                vrecip(127)
                vctx(127)
                v.nop().then_inc(sem["ph"], 1)
        st12.close()

        with contextlib.ExitStack() as st3:
            E3 = st3.enter_context
            wprojT_bf = E3(nc.sbuf_tensor("wprojT_bf", [128, HC * H], BF16))
            ctx_rhs = [E3(nc.sbuf_tensor(f"ctx_rhs{i}", [128, HC * 512], BF16))
                       for i in range(2)]
            xres_sb = E3(nc.sbuf_tensor("xres_sb", [128, 4 * 2048], F32))
            t1_sb = E3(nc.sbuf_tensor("t1_sb", [128, 4 * 2048], F32))
            pr_sb = [E3(nc.sbuf_tensor(f"pr{i}", [128, 512], BF16)) for i in range(2)]
            x1b = [E3(nc.sbuf_tensor(f"x1b{i}", [128, 2048], BF16)) for i in range(2)]
            x1f = [E3(nc.sbuf_tensor(f"x1f{i}", [128, 2048], F32)) for i in range(2)]
            x1t_blk = E3(nc.sbuf_tensor("x1t_blk", [128, HC * 512], BF16))
            rowsum = [E3(nc.sbuf_tensor(f"rs{i}", [128, 1], F32)) for i in range(2)]
            sumsq = [E3(nc.sbuf_tensor(f"sq{i}", [128, 1], F32)) for i in range(2)]
            meanb = [E3(nc.sbuf_tensor(f"mn{i}", [128, 1], F32)) for i in range(2)]
            ex2 = [E3(nc.sbuf_tensor(f"e2{i}", [128, 1], F32)) for i in range(2)]
            msq = [E3(nc.sbuf_tensor(f"mq{i}", [128, 1], F32)) for i in range(2)]
            varb = [E3(nc.sbuf_tensor(f"vr{i}", [128, 1], F32)) for i in range(2)]
            lnv = [E3(nc.sbuf_tensor(f"lv{i}", [128, 1], F32)) for i in range(2)]
            rstd = [E3(nc.sbuf_tensor(f"rd{i}", [128, 1], F32)) for i in range(2)]
            ps_pr = [E3(nc.psum_tensor(f"ps_pr{i}", [128, 512], F32)) for i in range(2)]
            ps_rm = [E3(nc.psum_tensor(f"ps_rm{i}", [128, 512], F32)) for i in range(4)]
            ps_x1 = [E3(nc.psum_tensor(f"ps_x1{i}", [128, 512], F32)) for i in range(2)]

            x1t3 = x1t_blk[:].rearrange("p (hc jj) -> p hc jj", jj=512)

            @block.sync
            def _(sy):
                sy.wait_ge(sem["ph"], 15)
                sy.wait_ge(sem["cs0"], 16 * 64)
                sy.wait_ge(sem["cs1"], 16 * 64)
                for t in range(8):
                    i, sb = t // 4, t % 4
                    if t >= 2:
                        sy.wait_ge(sem["prm"], 16 * (t - 1))
                    for b in range(B):
                        sy.dma_start(out=ctx_rhs[t % 2][:, b * 512:(b + 1) * 512],
                                     in_=ctx_dram[i, b, :, sb * 512:(sb + 1) * 512]
                                     ).then_inc(sem["ctxl"], 16)
                    if t >= 1:
                        sy.wait_ge(sem["t1"], 16 * t)
                    for j in range(4):
                        s0 = sb * 512 + j * 128
                        sy.dma_start(out=xres_sb[:, j * 2048:(j + 1) * 2048],
                                     in_=x_res[i, s0:s0 + 128, :]).then_inc(sem["xrl"], 16)
                sy.nop().then_inc(sem["ph"], 1)

            @block.gpsimd
            def _(g):
                g.wait_ge(sem["ph"], 15)
                g.wait_ge(sem["wpc"], 64)
                wpb3 = wprojT_bf[:].rearrange("p (fc o) -> p fc o", o=H)
                for q in range(4):
                    g.dma_start(out=wpb3[:, q * 4:(q + 1) * 4, :],
                                in_=wpbf_dram[q * 4:(q + 1) * 4].rearrange("fc p o -> p fc o")
                                ).then_inc(sem["wql"], 16)
                for t in range(8):
                    i, sb = t // 4, t % 4
                    for j in range(4):
                        gj = t * 4 + j
                        r0 = i * 2048 + sb * 512 + j * 128
                        g.wait_ge(sem["x1fr"], gj + 1)
                        g.dma_start(out=x1R_dram[r0:r0 + 128, :], in_=x1f[j % 2][:]
                                    ).then_inc(sem["xf%d" % (gj % 2)], 16)
                    g.wait_ge(sem["xte"], 16 * (t + 1))
                    rblk = i * 2048 + sb * 512
                    for hc in range(HC):
                        g.dma_start(out=x1T_dram[hc, :, rblk:rblk + 512],
                                    in_=x1t_blk[:, hc * 512:(hc + 1) * 512]
                                    ).then_inc(sem["xts"], 16)
                g.nop().then_inc(sem["ph"], 1)

            @block.tensor
            def _(t_):
                t_.wait_ge(sem["ph"], 15)
                t_.wait_ge(sem["wql"], 128)

                def emit_tr(t, oc):
                    k = t * 16 + oc
                    t_.wait_ge(sem["pre"], k + 1)
                    for j in range(4):
                        gid = t * 16 + (oc // 4) * 4 + j
                        if oc % 4 == 0 and gid >= 4:
                            t_.wait_ge(sem["t1"], gid - 3)
                        ins = t_.matmul(ps_rm[j][:, (oc % 4) * 128:(oc % 4 + 1) * 128],
                                        pr_sb[oc % 2][:, j * 128:(j + 1) * 128], identB[:],
                                        start=(oc % 4 == 0), stop=(oc % 4 == 3))
                    ins.then_inc(sem["prc"], 1)

                for t in range(8):
                    t_.wait_ge(sem["ctxl"], 256 * (t + 1))
                    for oc in range(16):
                        k = t * 16 + oc
                        if k >= 2:
                            t_.wait_ge(sem["pre"], k - 1)
                        for fc in range(HC):
                            ins = t_.matmul(ps_pr[oc % 2][:],
                                            wprojT_bf[:, fc * 2048 + oc * 128: fc * 2048 + (oc + 1) * 128],
                                            ctx_rhs[t % 2][:, fc * 512:(fc + 1) * 512],
                                            start=(fc == 0), stop=(fc == 15))
                        ins.then_inc(sem["prm"], 1)
                        if oc >= 1:
                            emit_tr(t, oc - 1)
                    emit_tr(t, 15)
                    for j in range(4):
                        gj = t * 4 + j
                        t_.wait_ge(sem["x1b"], gj + 1)
                        for hcg in range(4):
                            xg = 4 * gj + hcg
                            if xg >= 2:
                                t_.wait_ge(sem["xte"], xg - 1)
                            for u in range(4):
                                hc = hcg * 4 + u
                                ins = t_.matmul(ps_x1[hcg % 2][:, u * 128:(u + 1) * 128],
                                                x1b[j % 2][:, hc * 128:(hc + 1) * 128], identB[:],
                                                start=(u == 0), stop=(u == 3))
                            ins.then_inc(sem["xt1"], 1)
                t_.nop().then_inc(sem["ph"], 1)

            @block.scalar
            def _(s):
                s.wait_ge(sem["ph"], 15)
                for t in range(8):
                    for oc in range(16):
                        k = t * 16 + oc
                        s.wait_ge(sem["prm"], k + 1)
                        if k >= 2:
                            s.wait_ge(sem["prc"], k - 1)
                        s.activation(pr_sb[oc % 2][:], ps_pr[oc % 2][:], AF.Identity,
                                     bias=bias_sb[:, oc:oc + 1]).then_inc(sem["pre"], 1)
                    for j in range(4):
                        gj = t * 4 + j
                        s.wait_ge(sem["t1"], t * 16 + 12 + j + 1)
                        if gj >= 2:
                            s.wait_ge(sem["xt1"], 4 * (gj - 1))
                        s.activation(x1b[j % 2][:], t1_sb[:, j * 2048:(j + 1) * 2048],
                                     AF.Square, accum_out=sumsq[j % 2][:]
                                     ).then_inc(sem["l2"], 1)
                        s.wait_ge(sem["l3"], gj + 1)
                        s.activation(lnv[j % 2][:], varb[j % 2][:], AF.Ln
                                     ).then_inc(sem["l4"], 1)
                        s.wait_ge(sem["l4"], gj + 1)
                        s.activation(rstd[j % 2][:], lnv[j % 2][:], AF.Exp, scale=-0.5
                                     ).then_inc(sem["l5"], 1)
                        for hcg in range(4):
                            xg = 4 * gj + hcg
                            s.wait_ge(sem["xt1"], xg + 1)
                            if t >= 1 and hcg == 0 and j == 0:
                                s.wait_ge(sem["xts"], 256 * t)
                            s.activation(x1t3[:, hcg * 4:(hcg + 1) * 4, j * 128:(j + 1) * 128],
                                         ps_x1[hcg % 2][:].rearrange("p (u c) -> p u c", c=128),
                                         AF.Copy).then_inc(sem["xte"], 1)
                s.nop().then_inc(sem["ph"], 1)

            @block.vector
            def _(v):
                v.wait_ge(sem["ph"], 15)
                for t in range(8):
                    for ocg in range(4):
                        for j in range(4):
                            gid = t * 16 + ocg * 4 + j
                            v.wait_ge(sem["prc"], t * 16 + 4 * ocg + 4)
                            v.wait_ge(sem["xrl"], 64 * (t + 1))
                            v.scalar_tensor_tensor(
                                t1_sb[:, j * 2048 + ocg * 512: j * 2048 + (ocg + 1) * 512],
                                ps_rm[j][:], 1.0,
                                xres_sb[:, j * 2048 + ocg * 512: j * 2048 + (ocg + 1) * 512],
                                ALU.mult, ALU.add).then_inc(sem["t1"], 1)
                    for j in range(4):
                        gj = t * 4 + j
                        tsl = t1_sb[:, j * 2048:(j + 1) * 2048]
                        v.wait_ge(sem["t1"], t * 16 + 12 + j + 1)
                        n0 = 5 * gj
                        v.reduce_sum(rowsum[j % 2][:], tsl, axis=AX.X).then_inc(sem["l1"], 1)
                        v.wait_ge(sem["l2"], gj + 1)
                        v.wait_ge(sem["l1"], n0 + 1)
                        v.tensor_scalar(meanb[j % 2][:], rowsum[j % 2][:], INVH, None,
                                        ALU.mult).then_inc(sem["l1"], 1)
                        v.tensor_scalar(ex2[j % 2][:], sumsq[j % 2][:], INVH, None,
                                        ALU.mult).then_inc(sem["l1"], 1)
                        v.wait_ge(sem["l1"], n0 + 3)
                        v.tensor_tensor(msq[j % 2][:], meanb[j % 2][:], meanb[j % 2][:],
                                        ALU.mult).then_inc(sem["l1"], 1)
                        v.wait_ge(sem["l1"], n0 + 4)
                        v.tensor_tensor(varb[j % 2][:], ex2[j % 2][:], msq[j % 2][:],
                                        ALU.subtract).then_inc(sem["l1"], 1)
                        v.wait_ge(sem["l1"], n0 + 5)
                        v.tensor_scalar(varb[j % 2][:], varb[j % 2][:], EPS, None,
                                        ALU.add).then_inc(sem["l3"], 1)
                        v.wait_ge(sem["l5"], gj + 1)
                        v.tensor_scalar(x1b[j % 2][:], tsl, meanb[j % 2][:], rstd[j % 2][:],
                                        ALU.subtract, ALU.mult).then_inc(sem["x1b"], 1)
                        if gj >= 2:
                            v.wait_ge(sem["xf%d" % (gj % 2)], 16 * (gj // 2))
                        v.tensor_scalar(x1f[j % 2][:], tsl, meanb[j % 2][:], rstd[j % 2][:],
                                        ALU.subtract, ALU.mult).then_inc(sem["x1fr"], 1)
                v.nop().then_inc(sem["ph"], 1)

        with contextlib.ExitStack() as st4:
            E4 = st4.enter_context
            x1T_sb = E4(nc.sbuf_tensor("x1T_sb", [128, HC * 4096], BF16))
            wg_t = [E4(nc.sbuf_tensor(f"wg{i}", [128, H], BF16)) for i in range(2)]
            wu_t = [E4(nc.sbuf_tensor(f"wu{i}", [128, H], BF16)) for i in range(2)]
            sg = [E4(nc.sbuf_tensor(f"sg{i}", [128, 512], F32)) for i in range(2)]
            h_sb = [E4(nc.sbuf_tensor(f"hs{i}", [128, 4096], BF16)) for i in range(2)]
            ps_g = [E4(nc.psum_tensor(f"ps_g{i}", [128, 512], F32)) for i in range(3)]
            ps_u = [E4(nc.psum_tensor(f"ps_u{i}", [128, 512], F32)) for i in range(3)]

            @block.sync
            def _(sy):
                sy.wait_ge(sem["ph"], 20)
                sy.wait_ge(sem["xts"], 256 * 8)
                for hc in range(HC):
                    sy.dma_start(out=x1T_sb[:, hc * 4096:(hc + 1) * 4096],
                                 in_=x1T_dram[hc]).then_inc(sem["h5"], 16)
                sy.wait_ge(sem["cc"], 1)
                for oc in range(OCI):
                    if oc >= 2:
                        sy.wait_ge(sem["gg"], 8 * (oc - 1))
                    sy.dma_start(out=wg_t[oc % 2][:], in_=ago[SLOT_G + oc]
                                 ).then_inc(sem["wl%d" % (oc % 2)], 16)
                    sy.dma_start(out=wu_t[oc % 2][:], in_=ago[SLOT_U + oc]
                                 ).then_inc(sem["wl%d" % (oc % 2)], 16)
                sy.nop().then_inc(sem["ph"], 1)

            @block.gpsimd
            def _(g):
                g.wait_ge(sem["ph"], 20)
                for oc in range(OCI):
                    g.wait_ge(sem["hh"], 8 * (oc + 1))
                    g.dma_start(out=h_dram[oc * 128:(oc + 1) * 128, :], in_=h_sb[oc % 2][:]
                                ).then_inc(sem["hs%d" % (oc % 2)], 16)
                g.nop().then_inc(sem["ph"], 1)

            @block.tensor
            def _(t):
                t.wait_ge(sem["ph"], 20)
                t.wait_ge(sem["h5"], 256)
                for oc in range(OCI):
                    t.wait_ge(sem["wl%d" % (oc % 2)], 32 * (oc // 2 + 1))
                    for rc in range(8):
                        m = oc * 8 + rc
                        if m >= 3:
                            t.wait_ge(sem["sil"], m - 2)
                            t.wait_ge(sem["hh"], m - 2)
                        for hc in range(HC):
                            t.matmul(ps_g[m % 3][:], wg_t[oc % 2][:, hc * 128:(hc + 1) * 128],
                                     x1T_sb[:, hc * 4096 + rc * 512: hc * 4096 + (rc + 1) * 512],
                                     start=(hc == 0), stop=(hc == 15))
                        for hc in range(HC):
                            ins = t.matmul(ps_u[m % 3][:], wu_t[oc % 2][:, hc * 128:(hc + 1) * 128],
                                           x1T_sb[:, hc * 4096 + rc * 512: hc * 4096 + (rc + 1) * 512],
                                           start=(hc == 0), stop=(hc == 15))
                        ins.then_inc(sem["gg"], 1)
                t.nop().then_inc(sem["ph"], 1)

            @block.scalar
            def _(s):
                s.wait_ge(sem["ph"], 20)
                for oc in range(OCI):
                    for rc in range(8):
                        m = oc * 8 + rc
                        s.wait_ge(sem["gg"], m + 1)
                        if m >= 2:
                            s.wait_ge(sem["hh"], m - 1)
                        s.activation(sg[rc % 2][:], ps_g[m % 3][:], AF.Silu
                                     ).then_inc(sem["sil"], 1)
                s.nop().then_inc(sem["ph"], 1)

            @block.vector
            def _(v):
                v.wait_ge(sem["ph"], 20)
                for oc in range(OCI):
                    for rc in range(8):
                        m = oc * 8 + rc
                        v.wait_ge(sem["sil"], m + 1)
                        if rc == 0 and oc >= 2:
                            v.wait_ge(sem["hs%d" % (oc % 2)], 16 * (oc // 2))
                        v.scalar_tensor_tensor(h_sb[oc % 2][:, rc * 512:(rc + 1) * 512],
                                               ps_u[m % 3][:], 1.0, sg[rc % 2][:],
                                               ALU.mult, ALU.mult).then_inc(sem["hh"], 1)
                v.nop().then_inc(sem["ph"], 1)

        with contextlib.ExitStack() as st5:
            E5 = st5.enter_context
            h5_sb = E5(nc.sbuf_tensor("h5_sb", [128, OCI * 512], BF16))
            wd_t = [E5(nc.sbuf_tensor(f"wd{i}", [128, INTER], BF16)) for i in range(3)]
            dn_sb = E5(nc.sbuf_tensor("dn_sb", [128, 16 * 512], BF16))
            x1r_sb = [E5(nc.sbuf_tensor(f"x1r{i}", [128, 2048], F32)) for i in range(4)]
            t2_sb = E5(nc.sbuf_tensor("t2_sb", [128, 4 * 2048], F32))
            rw5 = [E5(nc.sbuf_tensor(f"rw5{i}", [128, 1], F32)) for i in range(2)]
            sq5 = [E5(nc.sbuf_tensor(f"sq5{i}", [128, 1], F32)) for i in range(2)]
            mn5 = [E5(nc.sbuf_tensor(f"mn5{i}", [128, 1], F32)) for i in range(2)]
            e25 = [E5(nc.sbuf_tensor(f"e25{i}", [128, 1], F32)) for i in range(2)]
            mq5 = [E5(nc.sbuf_tensor(f"mq5{i}", [128, 1], F32)) for i in range(2)]
            vr5 = [E5(nc.sbuf_tensor(f"vr5{i}", [128, 1], F32)) for i in range(2)]
            lv5 = [E5(nc.sbuf_tensor(f"lv5{i}", [128, 1], F32)) for i in range(2)]
            rd5 = [E5(nc.sbuf_tensor(f"rd5{i}", [128, 1], F32)) for i in range(2)]
            ps_d = [E5(nc.psum_tensor(f"ps_d{i}", [128, 512], F32)) for i in range(3)]
            ps_r5 = [E5(nc.psum_tensor(f"ps_r5{i}", [128, 512], F32)) for i in range(4)]

            h_3 = h_dram.rearrange("(t p) r -> p t r", p=128)
            h5_3 = h5_sb[:].rearrange("p (t c) -> p t c", c=512)
            tchunks = [(0, 4), (4, 8), (8, 12), (12, 16), (16, 20), (20, 24),
                       (24, 28), (28, 32), (32, 36), (36, 40), (40, 43)]

            @block.sync
            def _(sy):
                sy.wait_ge(sem["ph"], 25)
                sy.wait_ge(sem["hs0"], 16 * 22)
                sy.wait_ge(sem["hs1"], 16 * 21)
                sy.wait_ge(sem["xf0"], 16 * 16)
                sy.wait_ge(sem["xf1"], 16 * 16)
                for rc in range(8):
                    r0 = rc * 512
                    i_n, s0 = rc // 4, (rc % 4) * 512
                    if rc >= 1:
                        sy.wait_ge(sem["dnm"], 16 * rc)
                    for (t0, t1) in tchunks:
                        sy.dma_start(out=h5_3[:, t0:t1, :], in_=h_3[:, t0:t1, r0:r0 + 512]
                                     ).then_inc(sem["h5"], 16)
                    for d2c in range(16):
                        dm = rc * 16 + d2c
                        if dm >= 3:
                            sy.wait_ge(sem["dnm"], dm - 2)
                        for kk in range(3):
                            sy.dma_start(
                                out=wd_t[dm % 3][:, kk * 2048:kk * 2048 + DW[kk]],
                                in_=ago[SLOT_D + 3 * d2c + kk][:, :DW[kk]],
                            ).then_inc(sem["wd%d" % (d2c % 2)], 16)
                    for j in range(4):
                        if rc >= 1:
                            sy.wait_ge(sem["t15"], (rc - 1) * 16 + 13 + j)
                        sy.dma_start(out=x1r_sb[j][:],
                                     in_=x1R_dram[r0 + j * 128: r0 + (j + 1) * 128, :]
                                     ).then_inc(sem["xr0"], 16)
                sy.nop().then_inc(sem["ph"], 1)

            @block.gpsimd
            def _(g):
                g.wait_ge(sem["ph"], 25)
                for rc in range(8):
                    i_n, s0 = rc // 4, (rc % 4) * 512
                    for j in range(4):
                        gj5 = rc * 4 + j
                        g.wait_ge(sem["yrd"], gj5 + 1)
                        g.dma_start(out=y[i_n, s0 + j * 128: s0 + (j + 1) * 128, :],
                                    in_=t2_sb[:, j * 2048:(j + 1) * 2048]
                                    ).then_inc(sem["ou0"], 16)
                g.wait_ge(sem["ou0"], 64 * 8)
                g.nop().then_inc(sem["ph"], 1)

            @block.tensor
            def _(t):
                t.wait_ge(sem["ph"], 25)
                for rc in range(8):
                    t.wait_ge(sem["h5"], 256 + 176 * (rc + 1))
                    for d2c in range(16):
                        dm = rc * 16 + d2c
                        t.wait_ge(sem["wd%d" % (d2c % 2)], 48 * (dm // 2 + 1))
                        if dm >= 3:
                            t.wait_ge(sem["dne"], dm - 2)
                        for oc in range(OCI):
                            ins = t.matmul(ps_d[dm % 3][:], wd_t[dm % 3][:, oc * 128:(oc + 1) * 128],
                                           h5_sb[:, oc * 512:(oc + 1) * 512],
                                           start=(oc == 0), stop=(oc == OCI - 1))
                        ins.then_inc(sem["dnm"], 1)
                    for d2c in range(16):
                        dm = rc * 16 + d2c
                        t.wait_ge(sem["dne"], dm + 1)
                        for j in range(4):
                            gid = rc * 16 + (d2c // 4) * 4 + j
                            if d2c % 4 == 0 and gid >= 4:
                                t.wait_ge(sem["t15"], gid - 3)
                            ins = t.matmul(ps_r5[j][:, (d2c % 4) * 128:(d2c % 4 + 1) * 128],
                                           dn_sb[:, d2c * 512 + j * 128: d2c * 512 + (j + 1) * 128],
                                           identB[:], start=(d2c % 4 == 0), stop=(d2c % 4 == 3))
                        ins.then_inc(sem["trm5"], 1)
                t.nop().then_inc(sem["ph"], 1)

            @block.scalar
            def _(s):
                s.wait_ge(sem["ph"], 25)
                for rc in range(8):
                    for d2c in range(16):
                        dm = rc * 16 + d2c
                        s.wait_ge(sem["dnm"], dm + 1)
                        if rc >= 1 and d2c == 0:
                            s.wait_ge(sem["trm5"], 16 * rc)
                        s.activation(dn_sb[:, d2c * 512:(d2c + 1) * 512], ps_d[dm % 3][:],
                                     AF.Copy).then_inc(sem["dne"], 1)
                    for j in range(4):
                        gj5 = rc * 4 + j
                        s.wait_ge(sem["t15"], rc * 16 + 12 + j + 1)
                        s.activation(dn_sb[:, j * 2048:(j + 1) * 2048],
                                     t2_sb[:, j * 2048:(j + 1) * 2048], AF.Square,
                                     accum_out=sq5[j % 2][:]).then_inc(sem["m2"], 1)
                        s.wait_ge(sem["m3"], gj5 + 1)
                        s.activation(lv5[j % 2][:], vr5[j % 2][:], AF.Ln
                                     ).then_inc(sem["m4"], 1)
                        s.wait_ge(sem["m4"], gj5 + 1)
                        s.activation(rd5[j % 2][:], lv5[j % 2][:], AF.Exp, scale=-0.5
                                     ).then_inc(sem["m5"], 1)
                s.nop().then_inc(sem["ph"], 1)

            @block.vector
            def _(v):
                v.wait_ge(sem["ph"], 25)
                for rc in range(8):
                    for d2g in range(4):
                        for j in range(4):
                            gid = rc * 16 + d2g * 4 + j
                            v.wait_ge(sem["trm5"], rc * 16 + 4 * d2g + 4)
                            v.wait_ge(sem["xr0"], 64 * (rc + 1))
                            if d2g == 0 and rc >= 1:
                                v.wait_ge(sem["ou0"], 64 * rc)
                            v.scalar_tensor_tensor(
                                t2_sb[:, j * 2048 + d2g * 512: j * 2048 + (d2g + 1) * 512],
                                ps_r5[j][:], 1.0,
                                x1r_sb[j][:, d2g * 512:(d2g + 1) * 512],
                                ALU.mult, ALU.add).then_inc(sem["t15"], 1)
                    for j in range(4):
                        gj5 = rc * 4 + j
                        tsl = t2_sb[:, j * 2048:(j + 1) * 2048]
                        v.wait_ge(sem["t15"], rc * 16 + 12 + j + 1)
                        n0 = 5 * gj5
                        v.reduce_sum(rw5[j % 2][:], tsl, axis=AX.X).then_inc(sem["m1"], 1)
                        v.wait_ge(sem["m2"], gj5 + 1)
                        v.wait_ge(sem["m1"], n0 + 1)
                        v.tensor_scalar(mn5[j % 2][:], rw5[j % 2][:], INVH, None,
                                        ALU.mult).then_inc(sem["m1"], 1)
                        v.tensor_scalar(e25[j % 2][:], sq5[j % 2][:], INVH, None,
                                        ALU.mult).then_inc(sem["m1"], 1)
                        v.wait_ge(sem["m1"], n0 + 3)
                        v.tensor_tensor(mq5[j % 2][:], mn5[j % 2][:], mn5[j % 2][:],
                                        ALU.mult).then_inc(sem["m1"], 1)
                        v.wait_ge(sem["m1"], n0 + 4)
                        v.tensor_tensor(vr5[j % 2][:], e25[j % 2][:], mq5[j % 2][:],
                                        ALU.subtract).then_inc(sem["m1"], 1)
                        v.wait_ge(sem["m1"], n0 + 5)
                        v.tensor_scalar(vr5[j % 2][:], vr5[j % 2][:], EPS, None,
                                        ALU.add).then_inc(sem["m3"], 1)
                        v.wait_ge(sem["m5"], gj5 + 1)
                        v.tensor_scalar(tsl, tsl, mn5[j % 2][:], rd5[j % 2][:],
                                        ALU.subtract, ALU.mult).then_inc(sem["yrd"], 1)
                v.nop().then_inc(sem["ph"], 1)
    return nc


def _gather_blocks(gi, ui, di):
    blocks = np.zeros((8 * NJL, 128, H), dtype=np.int32)
    for oc in range(OCI):
        blk = gi[oc * 128:(oc + 1) * 128, :].T.reshape(16, 128, 128)
        blocks[SLOT_G + oc] = blk.transpose(1, 0, 2).reshape(128, H)
        blk = ui[oc * 128:(oc + 1) * 128, :].T.reshape(16, 128, 128)
        blocks[SLOT_U + oc] = blk.transpose(1, 0, 2).reshape(128, H)
    for dc in range(16):
        wd = di[dc * 128:(dc + 1) * 128, :].T.reshape(OCI, 128, 128)
        wd = wd.transpose(1, 0, 2).reshape(128, INTER)
        for kk in range(3):
            blocks[SLOT_D + 3 * dc + kk, :, :DW[kk]] = \
                wd[:, kk * 2048:kk * 2048 + DW[kk]]
    return np.ascontiguousarray(blocks)


def kernel(**inputs):
    _ = inputs["L0_attn_key"], inputs["L0_attn_value"]
    x = np.ascontiguousarray(np.asarray(inputs["L0_attn_query"], dtype=np.float32))
    W = np.asarray(inputs["L0_attn_Wqkv"], dtype=np.float32)
    Wp = np.asarray(inputs["L0_attn_Wproj"], dtype=np.float32)
    bp = np.asarray(inputs["L0_attn_bproj"], dtype=np.float32)
    lut = np.asarray(inputs["L0_ffn_lut"], dtype=np.float32)
    gi = np.ascontiguousarray(np.asarray(inputs["L0_ffn_gate_idx"], dtype=np.int32))
    ui = np.ascontiguousarray(np.asarray(inputs["L0_ffn_up_idx"], dtype=np.int32))
    di = np.ascontiguousarray(np.asarray(inputs["L0_ffn_down_idx"], dtype=np.int32))

    xT = np.ascontiguousarray(x.transpose(2, 0, 1))
    wprojT = np.ascontiguousarray(Wp.T)
    bias2dT = np.ascontiguousarray(bp.reshape(16, 128).T)
    lut2d = np.ascontiguousarray(lut.reshape(1, 16))
    ident = np.eye(128, dtype=np.float32)
    onesc = np.ones((128, 1), dtype=np.float32)
    onesr = np.ones((1, 128), dtype=np.float32)
    gblocks = _gather_blocks(gi, ui, di)

    nc = build_nc()
    in_maps = []
    for c in range(8):
        n0, n1 = 2 * c, 2 * c + 1
        wq = np.concatenate([
            W[n0 * 384: n0 * 384 + 128],
            W[n0 * 384 + 128: n0 * 384 + 256],
            W[n1 * 384: n1 * 384 + 128],
            W[n1 * 384 + 128: n1 * 384 + 256],
            W[n0 * 384 + 256: n0 * 384 + 384],
            W[n1 * 384 + 256: n1 * 384 + 384],
        ], axis=0)
        in_maps.append({
            "xT_in": xT, "x_res": np.ascontiguousarray(x[n0:n1 + 1]),
            "wqkvT": np.ascontiguousarray(wq.T), "wprojT": wprojT,
            "bias2dT": bias2dT, "lut2d": lut2d,
            "idx_sh": gblocks[c * NJL:(c + 1) * NJL],
            "ident_in": ident, "onesc_in": onesc, "onesr_in": onesr,
        })
    res = run_bass_kernel_spmd(nc, in_maps, core_ids=list(range(8)))
    return np.concatenate([np.asarray(r["y"]) for r in res.results], axis=0)
